# version 16
# baseline (speedup 1.0000x reference)
import contextlib
import numpy as np
import ml_dtypes
import concourse.bass as bass
import concourse.mybir as mybir
from concourse.bass_utils import run_bass_kernel_spmd

F32 = mybir.dt.float32
I32 = mybir.dt.int32
BF16 = mybir.dt.bfloat16
ALU = mybir.AluOpType
ACT = mybir.ActivationFunctionType
AXL = mybir.AxisListType

T, C = 2048, 1024
NT, NCH, NH, HD = 16, 8, 16, 64
QCW, NQC, NKT = 512, 4, 16
IFF, NIT = 2816, 22
EPS = 1e-5

STAGE = 4
NORM_GPS = False
RECB2 = True
ATT_SUB = 5


class _Sem:
    __slots__ = ("h", "n")

    def __init__(self, handle):
        self.h = handle
        self.n = 0


class _Plan:

    def __init__(self, real_block):
        self.rb = real_block
        self.q = {k: [] for k in ("sync", "tensor", "vector", "scalar", "gpsimd")}

    def sync(self, f):
        self.q["sync"].append(f)

    def tensor(self, f):
        self.q["tensor"].append(f)

    def vector(self, f):
        self.q["vector"].append(f)

    def scalar(self, f):
        self.q["scalar"].append(f)

    def gpsimd(self, f):
        self.q["gpsimd"].append(f)

    def flush(self):
        for name in ("sync", "tensor", "vector", "scalar", "gpsimd"):
            fns = self.q[name]
            if not fns:
                continue
            self.q[name] = []

            def _run(e, fns=fns):
                for f in fns:
                    f(e)
            getattr(self.rb, name)(_run)


def _build(stage):
    nc = bass.Bass("TRN2", target_bir_lowering=False, debug=False)

    def din(name, shape, dt=F32):
        return nc.dram_tensor(name, list(shape), dt, kind="ExternalInput").ap()

    d = {"x": din("x", [T, C])}
    for L in ("l0", "l1"):
        for w in ("wq", "wk", "wv", "wo"):
            d[f"{L}_{w}"] = din(f"{L}_{w}", [C, C])
        for b in ("bq", "bk", "bv", "bo"):
            d[f"{L}_{b}"] = din(f"{L}_{b}", [C])
    for w in ("wr", "wk", "wv"):
        d[f"f0_{w}"] = din(f"f0_{w}", [C, C])
    d["f0_mur"] = din("f0_mur", [C])
    d["f0_muk"] = din("f0_muk", [C])
    d["f1_wg_sl"] = din("f1_wg_sl", [IFF // 8, C])
    d["f1_wu_sl"] = din("f1_wu_sl", [IFF // 8, C])
    d["f1_wd_sl"] = din("f1_wd_sl", [C, IFF // 8])
    d["c_ident"] = din("c_ident", [128, 128], BF16)
    d["c_tri"] = din("c_tri", [128, 128], BF16)
    d["c_alibi"] = din("c_alibi", [128, NH * NKT], F32)
    out_ap = nc.dram_tensor("out", [T, C], F32, kind="ExternalOutput").ap()
    hq_dram = nc.dram_tensor("hq_scr", [NT, NIT, 128, 128], BF16,
                             kind="Internal").ap()
    rec_dram = nc.dram_tensor("rec_scr", [16, 512], BF16, kind="Internal").ap()
    qstage = nc.dram_tensor("qstage", [14, 128, 1024], BF16, kind="Internal").ap()
    SLC = IFF // 8
    ccg_in = nc.dram_tensor("ccg_in", [C, SLC], BF16, kind="Internal").ap()
    ccu_in = nc.dram_tensor("ccu_in", [C, SLC], BF16, kind="Internal").ap()
    ccd_in = nc.dram_tensor("ccd_in", [SLC, C], BF16, kind="Internal").ap()
    ccg = nc.dram_tensor("ccg", [8 * C, SLC], BF16, kind="Internal").ap()
    ccu = nc.dram_tensor("ccu", [8 * C, SLC], BF16, kind="Internal").ap()
    ccd = nc.dram_tensor("ccd", [IFF, C], BF16, kind="Internal").ap()

    with contextlib.ExitStack() as est:
        def sbuf(name, shape, dt):
            return est.enter_context(nc.sbuf_tensor(name, list(shape), dt))

        def psum(name, shape, dt):
            return est.enter_context(nc.psum_tensor(name, list(shape), dt))

        xbuf = sbuf("xbuf", [128, NT, C], F32)
        qTa = sbuf("qTa", [128, NCH, T], BF16)
        kTa = sbuf("kTa", [128, NCH * T], BF16)
        Va = sbuf("Va", [128, NT * (C + NH)], BF16)
        wcur = sbuf("wcur", [128, NCH, C], BF16)
        u1 = sbuf("u1", [128, NCH * QCW], BF16)
        u2 = sbuf("u2", [128, C], BF16)
        ctxq = sbuf("ctxq", [128, NCH * QCW], BF16)
        scrB = sbuf("scrB", [128, 1600], F32)
        recb = sbuf("recb", [128, 512], BF16)
        epsc = sbuf("epsc", [128, 1], F32)
        den16 = scrB[:, 0:512]
        stats = sbuf("stats", [128, 16], F32)
        bq_c = sbuf("bq_c", [128, 8], F32)
        bk_c = sbuf("bk_c", [128, 8], F32)
        mur_c = sbuf("mur_c", [128, 8], F32)
        muk_c = sbuf("muk_c", [128, 8], F32)
        brow = sbuf("brow", [1, C], BF16)
        ident = sbuf("ident", [128, 128], BF16)
        tri = sbuf("tri", [128, 128], BF16)

        ps_ab = psum("ps_ab", [128, 1024], F32)
        ps_cd = psum("ps_cd", [128, 1024], F32)
        ps_x = [psum("ps_x0", [128, 512], F32), psum("ps_x1", [128, 512], F32)]
        ps_e = psum("ps_e", [128, 512], F32)
        ps_tr = psum("ps_tr", [128, 1024], BF16)
        ps_s = [ps_ab[:, 0:512], ps_ab[:, 512:1024]]
        ps_slot = [ps_ab[:, 0:512], ps_cd[:, 0:512], ps_ab[:, 512:1024]]
        NSS = 3
        ps_w = ps_ab

        LD = _Sem(est.enter_context(nc.semaphore("s_ld")))
        GL = _Sem(est.enter_context(nc.semaphore("s_gl")))
        DN = _Sem(est.enter_context(nc.semaphore("s_dn")))
        HQ = _Sem(est.enter_context(nc.semaphore("s_hq")))
        OUT = _Sem(est.enter_context(nc.semaphore("s_out")))
        TE = _Sem(est.enter_context(nc.semaphore("s_te")))
        VE = _Sem(est.enter_context(nc.semaphore("s_ve")))
        SE = _Sem(est.enter_context(nc.semaphore("s_se")))
        GE = _Sem(est.enter_context(nc.semaphore("s_ge")))
        CC = _Sem(est.enter_context(nc.semaphore("s_cc")))
        BAR = _Sem(est.enter_context(nc.semaphore("s_bar")))

        rblock = est.enter_context(nc.Block())
        block = _Plan(rblock)

        def bar():
            block.flush()
            BAR.n += 5
            tgt = BAR.n

            def _f(e, tgt=tgt):
                e.drain()
                e.nop().then_inc(BAR.h, 1)
                e.wait_ge(BAR.h, tgt)
            for em in (rblock.sync, rblock.tensor, rblock.vector, rblock.scalar,
                       rblock.gpsimd):
                em(_f)

        def p0():
            def _y(e):
                xv = xbuf[:]
                dv = d["x"].rearrange("(t p) c -> p t c", p=128)
                for g in range(4):
                    e.dma_start(xv[:, g * 4:(g + 1) * 4],
                                dv[:, g * 4:(g + 1) * 4]).then_inc(LD.h, 16)
                for ap, key in ((ident[:], "c_ident"), (tri[:], "c_tri")):
                    e.dma_start(ap, d[key]).then_inc(LD.h, 16)
            LD.n += 96
            block.sync(_y)

            GL.n += 64
            GE.n += 1
            gl_t0 = GL.n

            def _g(e, gl_t0=gl_t0):
                for ap, key in ((bq_c[:], "l0_bq"), (bk_c[:], "l0_bk"),
                                (mur_c[:], "f0_mur"), (muk_c[:], "f0_muk")):
                    e.dma_start(ap, d[key].rearrange("(c p) -> p c", p=128),
                            allow_slow_non_contiguous=True).then_inc(GL.h, 16)
                e.wait_ge(GL.h, gl_t0)
                e.memset(epsc[:], float(EPS))
                e.drain()
                e.nop().then_inc(GE.h, 1)
            block.gpsimd(_g)
            ld_t0 = LD.n
            ge_t0 = GE.n

            def _v(e, ld_t0=ld_t0, ge_t0=ge_t0):
                e.wait_ge(LD.h, ld_t0)
                e.wait_ge(GE.h, ge_t0)
            block.vector(_v)
            bar()

        def load_brow(key):
            GL.n += 16

            def _g(e):
                e.dma_start(brow[:, :], d[key].rearrange("(a c) -> a c", a=1))\
                    .then_inc(GL.h, 16)
            block.gpsimd(_g)
            tgt = GL.n

            def _t(e, tgt=tgt):
                e.wait_ge(GL.h, tgt)
            block.tensor(_t)

        def ln_tile(t, dst, src=None):
            xin = xbuf[:, t, :] if src is None else src
            se0 = SE.n
            VE.n += 1

            def _v(e, xin=xin, se0=se0):
                e.wait_ge(SE.h, se0)
                e.bn_stats(stats[:, 0:6], xin[:, 0:512])
                e.bn_stats(stats[:, 6:12], xin[:, 512:1024])
                e.drain()
                e.bn_aggr(stats[:, 12:14], stats[:, 0:12])
                e.drain()
                e.nop().then_inc(VE.h, 1)
            block.vector(_v)
            ve1 = VE.n
            SE.n += 1

            def _s(e, ve1=ve1):
                e.wait_ge(VE.h, ve1)
                e.activation(stats[:, 14:15], stats[:, 13:14], ACT.Sqrt,
                             bias=epsc[:, 0:1])
                e.drain()
                e.nop().then_inc(SE.h, 1)
            block.scalar(_s)
            se1 = SE.n
            VE.n += 1

            def _v2(e, se1=se1):
                e.wait_ge(SE.h, se1)
                e.reciprocal(stats[:, 14:15], stats[:, 14:15])
                e.drain()
                e.scalar_tensor_tensor(stats[:, 15:16], stats[:, 12:13], -1.0,
                                       stats[:, 14:15], ALU.mult, ALU.mult)
                e.drain()
                e.nop().then_inc(VE.h, 1)
            block.vector(_v2)
            ve2 = VE.n
            SE.n += 1

            def _s2(e, xin=xin, dst=dst, ve2=ve2):
                e.wait_ge(VE.h, ve2)
                e.activation(dst, xin, ACT.Identity,
                             bias=stats[:, 15:16], scale=stats[:, 14:15])
                e.drain()
                e.nop().then_inc(SE.h, 1)
            block.scalar(_s2)

        def transpose_tile(src, dst_fn, ncc=8, ge_wait=None, te_guard=None):
            for g in range((ncc + 3) // 4):
                kk = min(4, ncc - g * 4)
                se_t = SE.n
                ve_t = VE.n
                t0 = TE.n

                TE.n += 1

                def _t(e, g=g, kk=kk, se_t=se_t, ve_t=ve_t, src=src,
                       ge_wait=ge_wait):
                    e.wait_ge(SE.h, se_t)
                    e.wait_ge(VE.h, ve_t)
                    if ge_wait is not None:
                        e.wait_ge(GE.h, ge_wait)
                    li = None
                    for k in range(kk):
                        cc = g * 4 + k
                        li = e.transpose(ps_tr[:, k * 128:(k + 1) * 128],
                                         src[:, cc * 128:(cc + 1) * 128], ident[:])
                    li.then_inc(TE.h, 1)
                block.tensor(_t)
                VE.n += 1

                def _v(e, g=g, kk=kk, t0=t0, dst_fn=dst_fn, te_guard=te_guard):
                    e.wait_ge(TE.h, t0 + 1)
                    if te_guard is not None:
                        e.wait_ge(TE.h, te_guard)
                    for k in range(kk):
                        e.tensor_copy(dst_fn(g * 4 + k),
                                      ps_tr[:, k * 128:(k + 1) * 128])
                    e.drain()
                    e.nop().then_inc(VE.h, 1)
                block.vector(_v)

        def wprep_T(src, wdst_fn):
            for r in range(8):
                ve_t = VE.n

                GL.n += 16

                def _g(e, r=r, ve_t=ve_t, src=src):
                    e.wait_ge(VE.h, ve_t)
                    e.dma_start(u2[:, :], src[r * 128:(r + 1) * 128, :])\
                        .then_inc(GL.h, 16)
                block.gpsimd(_g)
                gl_t = GL.n
                transpose_strip_w(gl_t, wdst_fn, r)

        def transpose_strip_w(gl_t, wdst_fn, r):
            for g in range(2):
                ve_t = VE.n
                t0 = TE.n

                TE.n += 1

                def _t(e, g=g, gl_t=gl_t, ve_t=ve_t):
                    e.wait_ge(GL.h, gl_t)
                    e.wait_ge(VE.h, ve_t)
                    li = None
                    for k in range(4):
                        cc = g * 4 + k
                        li = e.transpose(ps_tr[:, k * 128:(k + 1) * 128],
                                         u2[:, cc * 128:(cc + 1) * 128], ident[:])
                    li.then_inc(TE.h, 1)
                block.tensor(_t)
                VE.n += 1

                def _v(e, g=g, t0=t0, r=r, wdst_fn=wdst_fn):
                    e.wait_ge(TE.h, t0 + 1)
                    for k in range(4):
                        cc = g * 4 + k
                        e.tensor_copy(wdst_fn(cc)[:, r * 128:(r + 1) * 128],
                                      ps_tr[:, k * 128:(k + 1) * 128])
                    e.drain()
                    e.nop().then_inc(VE.h, 1)
                block.vector(_v)

        def wload_nat(src, wdst, nr=NCH):
            GL.n += 16

            def _g(e, src=src, wdst=wdst, nr=nr):
                e.dma_start(wdst[:, 0:nr, :],
                            src.rearrange("(r p) c -> p r c", p=128))\
                    .then_inc(GL.h, 16)
            block.gpsimd(_g)

        def _quant_gps(src, dst, w, nb, u_a, u_t, u_c, u_s, m_, scl, rb_, sb2):
            t1 = u_c[:, 128:128 + w // 2]
            t2 = u_c[:, 128 + w // 2:128 + w // 2 + w // 4]
            t3 = u_c[:, 128 + w // 2 - 64:128 + w // 2 - 64 + w // 8]

            def _g1(e):
                e.tensor_scalar(u_a.bitcast(I32), src.bitcast(I32), 0x7FFFFFFF,
                                None, ALU.bitwise_and)
                e.drain()
                a2 = u_a.rearrange("p (b k) -> p b k", k=2)
                e.tensor_tensor(t1, a2[:, :, 0], a2[:, :, 1], ALU.max)
                e.drain()
                b2 = t1.rearrange("p (b k) -> p b k", k=2)
                e.tensor_tensor(t2, b2[:, :, 0], b2[:, :, 1], ALU.max)
                e.drain()
                c2 = t2.rearrange("p (b k) -> p b k", k=2)
                e.tensor_tensor(t3, c2[:, :, 0], c2[:, :, 1], ALU.max)
                e.drain()
                d2 = t3.rearrange("p (b k) -> p b k", k=2)
                e.tensor_tensor(m_, d2[:, :, 0], d2[:, :, 1], ALU.max)
                e.drain()
                e.tensor_scalar(scl, m_, 1.0 / 6.0, 1e-12, ALU.mult, ALU.max)
                e.drain()
                e.tensor_scalar(sb2, scl, 0.5, None, ALU.mult)
                e.drain()
                e.nop().then_inc(GE.h, 1)
            GE.n += 1
            block.gpsimd(_g1)
            ge1 = GE.n
            VE.n += 1

            def _vr(e, ge1=ge1):
                e.wait_ge(GE.h, ge1)
                e.reciprocal(rb_, scl)
                e.drain()
                e.nop().then_inc(VE.h, 1)
            block.vector(_vr)
            ve1 = VE.n
            GE.n += 1

            def _g2(e, ve1=ve1):
                e.wait_ge(VE.h, ve1)
                rbb = rb_.unsqueeze(2).broadcast_to([pp, nb, 16])
                e.tensor_tensor(u_t.rearrange("p (b k) -> p b k", k=16),
                                u_a.rearrange("p (b k) -> p b k", k=16), rbb,
                                ALU.mult)
                e.drain()
                e.tensor_scalar(u_c, u_t, 5.0, 2.0, ALU.is_gt, ALU.mult)
                e.tensor_scalar(u_a, u_t, 0.25, None, ALU.is_gt)
                e.drain()
                e.scalar_tensor_tensor(u_c, u_t, 3.5, u_c, ALU.is_gt, ALU.add)
                e.scalar_tensor_tensor(u_a, u_t, 0.75, u_a, ALU.is_gt, ALU.add)
                e.drain()
                e.scalar_tensor_tensor(u_c, u_t, 2.5, u_c, ALU.is_gt, ALU.add)
                e.scalar_tensor_tensor(u_a, u_t, 1.25, u_a, ALU.is_gt, ALU.add)
                e.drain()
                e.scalar_tensor_tensor(u_a, u_t, 1.75, u_a, ALU.is_gt, ALU.add)
                e.drain()
                e.scalar_tensor_tensor(u_a, u_c, 2.0, u_a, ALU.mult, ALU.add)
                e.drain()
                sbb = sb2.unsqueeze(2).broadcast_to([pp, nb, 16])
                e.tensor_tensor(u_c.rearrange("p (b k) -> p b k", k=16),
                                u_a.rearrange("p (b k) -> p b k", k=16), sbb,
                                ALU.mult)
                e.drain()
                e.tensor_scalar(u_s, src, 0.0, 2.0, ALU.is_ge, ALU.mult)
                e.drain()
                e.scalar_tensor_tensor(dst, u_s, -1.0, u_c, ALU.add, ALU.mult)
                e.drain()
                e.nop().then_inc(GE.h, 1)
            block.gpsimd(_g2)

        def quant(src, dst, w, base, eng="dve", scr=None, pp=128,
                  sg=None, ua=None, src_evt=None, sg_guard=0):
            nb = w // 16
            S_ = (scrB if scr is None else scr)[0:pp, :]
            u_a = S_[:, base:base + w] if ua is None else ua[0:pp, 0:w]
            u_t = S_[:, base + w:base + 2 * w]
            u_c = S_[:, base + 2 * w:base + 3 * w]
            u_s = u_t
            m_ = S_[:, base + 2 * w:base + 2 * w + nb]
            scl = S_[:, base + 2 * w + nb:base + 2 * w + 2 * nb]
            rb_ = S_[:, base + 2 * w + 2 * nb:base + 2 * w + 3 * nb]
            sb2 = S_[:, base + 3 * w:base + 3 * w + nb]
            act_t = 0
            if sg is not None:
                SE.n += 1

                def _sa(e, src=src, sg=sg, src_evt=src_evt,
                        sg_guard=sg_guard, u_a=u_a):
                    if src_evt is not None:
                        e.wait_ge(src_evt[0].h, src_evt[1])
                    if sg_guard:
                        e.wait_ge(VE.h, sg_guard)
                    e.activation(u_a, src, ACT.Abs)
                    e.activation(sg, src, ACT.Sign)
                    e.drain()
                    e.nop().then_inc(SE.h, 1)
                block.scalar(_sa)
                act_t = SE.n

            def _v(e, act_t=act_t):
                if act_t:
                    e.wait_ge(SE.h, act_t)
                sv = src.rearrange("p (b k) -> p b k", k=16)
                e.tensor_reduce(m_, sv, AXL.X, ALU.max, apply_absolute_value=True)
                e.drain()
                e.tensor_scalar(scl, m_, 1.0 / 6.0, 1e-12, ALU.mult, ALU.max)
                e.drain()
                e.reciprocal(rb_, scl)
                e.tensor_scalar(sb2, scl, 0.5, None, ALU.mult)
                e.drain()
                if sg is None:
                    e.tensor_reduce(u_a, src.rearrange("p (n o) -> p n o", o=1),
                                    AXL.X, ALU.max, apply_absolute_value=True)
                    e.drain()
                rbb = rb_.unsqueeze(2).broadcast_to([pp, nb, 16])
                e.tensor_tensor(u_t.rearrange("p (b k) -> p b k", k=16),
                                u_a.rearrange("p (b k) -> p b k", k=16), rbb,
                                ALU.mult)
                e.drain()
                e.tensor_scalar(u_c, u_t, 5.0, 2.0, ALU.is_gt, ALU.mult)
                e.tensor_scalar(u_a, u_t, 0.25, None, ALU.is_gt)
                e.drain()
                e.scalar_tensor_tensor(u_c, u_t, 3.5, u_c, ALU.is_gt, ALU.add)
                e.scalar_tensor_tensor(u_a, u_t, 0.75, u_a, ALU.is_gt, ALU.add)
                e.drain()
                e.scalar_tensor_tensor(u_c, u_t, 2.5, u_c, ALU.is_gt, ALU.add)
                e.scalar_tensor_tensor(u_a, u_t, 1.25, u_a, ALU.is_gt, ALU.add)
                e.drain()
                e.scalar_tensor_tensor(u_a, u_t, 1.75, u_a, ALU.is_gt, ALU.add)
                e.drain()
                e.scalar_tensor_tensor(u_a, u_c, 2.0, u_a, ALU.mult, ALU.add)
                e.drain()
                sbb = sb2.unsqueeze(2).broadcast_to([pp, nb, 16])
                e.tensor_tensor(u_c.rearrange("p (b k) -> p b k", k=16),
                                u_a.rearrange("p (b k) -> p b k", k=16), sbb,
                                ALU.mult)
                e.drain()
                if sg is None:
                    e.tensor_scalar(u_s, src, 0.0, 2.0, ALU.is_ge, ALU.mult)
                    e.drain()
                    e.scalar_tensor_tensor(dst, u_s, -1.0, u_c, ALU.add,
                                           ALU.mult)
                else:
                    e.tensor_tensor(dst, u_c, sg, ALU.mult)
                e.drain()
                e.nop().then_inc(VE.h, 1)
            VE.n += 1
            block.vector(_v)

        def emit_out():
            ve_t = VE.n
            se_t = SE.n
            OUT.n += 64
            out_t = OUT.n

            def _y(e, ve_t=ve_t, se_t=se_t, out_t=out_t):
                e.wait_ge(VE.h, ve_t)
                e.wait_ge(SE.h, se_t)
                xv = xbuf[:]
                dv = out_ap.rearrange("(t p) c -> p t c", p=128)
                for g in range(4):
                    e.dma_start(dv[:, g * 4:(g + 1) * 4],
                                xv[:, g * 4:(g + 1) * 4]).then_inc(OUT.h, 16)
                e.wait_ge(OUT.h, out_t)
            block.sync(_y)
            block.flush()

        wcc_tiles = []
        for (srcw, cin) in ((d["f1_wg_sl"], ccg_in), (d["f1_wu_sl"], ccu_in)):
            for (r0, rr) in ((0, 128), (128, 128), (256, 96)):
                wcc_tiles.append(("gu", srcw[r0:r0 + rr, :], cin, r0, rr, C))
        for r in range(8):
            wcc_tiles.append(("d", d["f1_wd_sl"][r * 128:(r + 1) * 128, :],
                              ccd_in, r * 128, 128, SLC))
        wcc_st = {"li": 0, "qi": 0, "ti": 0, "store": {}, "q_ve": {},
                  "tr_ve": 0, "ld_all": 0}
        wtr_dmas = {}
        wtr_evs = {}
        cxf = ctxq[:].bitcast(F32)
        cxq_dst = ctxq[:, 2048:3072]

        def wcc_feed_load():
            if stage < 4 or wcc_st["li"] >= len(wcc_tiles):
                return
            i = wcc_st["li"]
            wcc_st["li"] += 1
            (kind, srcp, cin, r0, rr, w) = wcc_tiles[i]
            ve_t = wcc_st["q_ve"].get(i - 1, 0)
            LD.n += 16

            def _y(e, srcp=srcp, rr=rr, w=w, ve_t=ve_t):
                e.wait_ge(VE.h, ve_t)
                e.dma_start(cxf[0:rr, 0:w], srcp).then_inc(LD.h, 16)
            block.sync(_y)
            wcc_st["load_t", i] = LD.n

        def wcc_feed_quant():
            if stage < 4 or wcc_st["qi"] >= wcc_st["li"]:
                return
            i = wcc_st["qi"]
            wcc_st["qi"] += 1
            (kind, srcp, cin, r0, rr, w) = wcc_tiles[i]
            ld_t = wcc_st["load_t", i]
            st_prev = wcc_st["store"].get(i - 1, 0)
            VE.n += 1

            def _v0(e, ld_t=ld_t, st_prev=st_prev):
                e.wait_ge(LD.h, max(ld_t, st_prev))
                e.nop().then_inc(VE.h, 1)
            block.vector(_v0)
            if kind == "gu":
                for so_ in (0, 512):
                    quant(cxf[0:rr, so_:so_ + 512],
                          cxq_dst[0:rr, so_:so_ + 512], 512, 0, pp=rr)
            else:
                quant(cxf[0:rr, 0:w], cxq_dst[0:rr, 0:w], w, 0, pp=rr)
            wcc_st["q_ve"][i] = VE.n
            ve_q = VE.n
            LD.n += 16

            def _y2(e, i=i, rr=rr, w=w, ve_q=ve_q):
                e.wait_ge(VE.h, ve_q)
                e.dma_start(qstage[i, 0:rr, 0:w], cxq_dst[0:rr, 0:w])\
                    .then_inc(LD.h, 16)
            block.sync(_y2)
            wcc_st["store"][i] = LD.n

        def wcc_tr_store(blocks):
            ve_entry = VE.n
            for g in range((len(blocks) + 3) // 4):
                grp = blocks[g * 4:(g + 1) * 4]
                ve_q = max([ve_entry] +
                           [wtr_evs.get(k, 0) for k in range(len(grp))])
                t0 = TE.n
                TE.n += 1

                def _t(e, grp=grp, ve_q=ve_q):
                    e.wait_ge(VE.h, ve_q)
                    li = None
                    for k, (sap, dap, rp, cw) in enumerate(grp):
                        li = e.transpose(ps_tr[0:cw, k * 128:k * 128 + rp],
                                         sap, ident[0:rp, 0:rp])
                    li.then_inc(TE.h, 1)
                block.tensor(_t)
                VE.n += 1
                for k in range(len(grp)):
                    wtr_evs[k] = VE.n
                dprev = max([0] + [wtr_dmas.get(k, 0)
                                   for k in range(len(grp))])

                def _v(e, grp=grp, t0=t0, dprev=dprev):
                    e.wait_ge(TE.h, t0 + 1)
                    e.wait_ge(LD.h, dprev)
                    for k, (sap, dap, rp, cw) in enumerate(grp):
                        e.tensor_copy(recb[0:cw, k * 128:k * 128 + rp],
                                      ps_tr[0:cw, k * 128:k * 128 + rp])
                    e.drain()
                    e.nop().then_inc(VE.h, 1)
                block.vector(_v)
                ve_e = VE.n
                for k in range(len(grp)):
                    LD.n += 16
                    wtr_dmas[k] = LD.n

                def _y(e, grp=grp, ve_e=ve_e):
                    e.wait_ge(VE.h, ve_e)
                    for k, (sap, dap, rp, cw) in enumerate(grp):
                        e.dma_start(dap, recb[0:cw, k * 128:k * 128 + rp])\
                            .then_inc(LD.h, 16)
                block.sync(_y)

        def wcc_feed_tr():
            if stage < 4 or wcc_st["ti"] >= wcc_st["qi"]:
                return
            i = wcc_st["ti"]
            wcc_st["ti"] += 1
            (kind, srcp, cin, r0, rr, w) = wcc_tiles[i]
            tstg = ctxq[:, 3072:4096]
            ve_t = wcc_st["tr_ve"]
            st_t = wcc_st["store"][i]
            LD.n += 16
            ld_t = LD.n

            def _y(e, i=i, rr=rr, w=w, ve_t=ve_t, st_t=st_t, tstg=tstg):
                e.wait_ge(VE.h, ve_t)
                e.wait_ge(LD.h, st_t)
                e.dma_start(tstg[0:rr, 0:w], qstage[i, 0:rr, 0:w])\
                    .then_inc(LD.h, 16)
            block.sync(_y)
            VE.n += 1

            def _v0(e, ld_t=ld_t):
                e.wait_ge(LD.h, ld_t)
                e.nop().then_inc(VE.h, 1)
            block.vector(_v0)
            blocks = []
            if kind == "gu":
                for cc in range(8):
                    blocks.append((tstg[0:rr, cc * 128:(cc + 1) * 128],
                                   cin[cc * 128:(cc + 1) * 128, r0:r0 + rr],
                                   rr, 128))
            else:
                for b, bw in ((0, 128), (1, 128), (2, 96)):
                    blocks.append((tstg[:, b * 128:b * 128 + bw],
                                   cin[b * 128:b * 128 + bw, r0:r0 + 128],
                                   128, bw))
            wcc_tr_store(blocks)
            wcc_st["tr_ve"] = VE.n

        def wcc_finish():
            if stage < 4:
                return
            while wcc_st["li"] < len(wcc_tiles):
                wcc_feed_load()
                wcc_feed_quant()
            while wcc_st["qi"] < len(wcc_tiles):
                wcc_feed_quant()
            while wcc_st["ti"] < len(wcc_tiles):
                wcc_feed_tr()
            ld_all = LD.n
            wcc_st["ld_all"] = ld_all
            CC.n += 3

            def _g(e, ld_all=ld_all):
                e.wait_ge(LD.h, ld_all)
                rg = [[0, 1, 2, 3, 4, 5, 6, 7]]
                e.collective_compute("AllGather", ALU.bypass, replica_groups=rg,
                                     ins=[ccg_in], outs=[ccg]).then_inc(CC.h, 1)
                e.collective_compute("AllGather", ALU.bypass, replica_groups=rg,
                                     ins=[ccu_in], outs=[ccu]).then_inc(CC.h, 1)
                e.collective_compute("AllGather", ALU.bypass, replica_groups=rg,
                                     ins=[ccd_in], outs=[ccd]).then_inc(CC.h, 1)
            block.gpsimd(_g)

        def wcc_guard():
            if stage < 4 or not wcc_st["ld_all"]:
                return
            ld_all = wcc_st["ld_all"]

            def _w(e, ld_all=ld_all):
                e.wait_ge(LD.h, ld_all)
            block.vector(_w)
            block.scalar(_w)

        def attention(layer):
            l = f"l{layer}"
            causal = (layer == 0)
            trw = (layer == 0)
            lnTb = u1[:].rearrange("p (c q) -> p c q", q=QCW)

            if layer == 1:
                GL.n += 32
                gl_b = GL.n

                def _g(e, gl_b=gl_b):
                    e.dma_start(bq_c[:], d["l1_bq"].rearrange("(c p) -> p c", p=128), allow_slow_non_contiguous=True).then_inc(GL.h, 16)
                    e.dma_start(bk_c[:], d["l1_bk"].rearrange("(c p) -> p c", p=128), allow_slow_non_contiguous=True).then_inc(GL.h, 16)
                    e.wait_ge(GL.h, gl_b)
                block.gpsimd(_g)
                bar()

            va4 = Va[:].rearrange("p (t h j) -> p t h j", h=NH, j=C // NH + 1)
            prods = ("k", "v", "q")[:ATT_SUB]
            for which in prods:
                if which == "v":
                    load_brow(f"{l}_bv")
                    VE.n += 1

                    def _vones(e):
                        e.memset(va4[:, :, :, 64:65], 1.0)
                        e.drain()
                        e.nop().then_inc(VE.h, 1)
                    block.vector(_vones)
                if trw:
                    wprep_T(d[f"{l}_w{which}"], lambda cc: wcur[:, cc, :])
                else:
                    wload_nat(d[f"{l}_w{which}"], wcur)
                bar()
                for qc in range(NQC):
                    if layer == 0:
                        wcc_feed_load()
                    for tt in range(4):
                        t = qc * 4 + tt
                        ln_tile(t, u2[:, :])
                        transpose_tile(u2[:, :],
                                       lambda cc, tt=tt: lnTb[:, cc, tt * 128:(tt + 1) * 128])
                    bar()
                    if layer == 0:
                        wcc_feed_quant()
                        wcc_feed_tr()
                    if which in ("k", "q"):
                        dst = kTa[:].rearrange("p (c q) -> p c q", q=T) if which == "k" \
                            else qTa[:, :, :]
                        bcol = bk_c if which == "k" else bq_c
                        evac_tgts = []
                        for m in range(8):
                            t0 = TE.n
                            se_need = evac_tgts[m - 2] if m >= 2 else SE.n

                            def _t(e, m=m, se_need=se_need):
                                e.wait_ge(SE.h, se_need)
                                li = None
                                for cc in range(8):
                                    li = e.matmul(ps_s[m % 2][:, :],
                                                  wcur[:, cc, m * 128:(m + 1) * 128],
                                                  lnTb[:, cc, :],
                                                  start=(cc == 0), stop=(cc == 7))
                                li.then_inc(TE.h, 1)
                            block.tensor(_t)
                            TE.n += 1

                            def _s(e, m=m, qc=qc, t0=t0, dst=dst, bcol=bcol):
                                e.wait_ge(TE.h, t0 + 1)
                                e.activation(dst[:, m, qc * QCW:(qc + 1) * QCW],
                                             ps_s[m % 2][:, :], ACT.Identity,
                                             bias=bcol[:, m:m + 1])
                                e.drain()
                                e.nop().then_inc(SE.h, 1)
                            block.scalar(_s)
                            SE.n += 1
                            evac_tgts.append(SE.n)
                    else:
                        evac_tgts = []
                        for tt in range(4):
                            t = qc * 4 + tt
                            t0 = TE.n
                            ve_need = evac_tgts[tt - 1] if tt >= 1 else VE.n

                            def _t(e, tt=tt, ve_need=ve_need):
                                e.wait_ge(VE.h, ve_need)
                                li = None
                                for half in range(2):
                                    for cc in range(8):
                                        e.matmul(ps_w[:, half * 512:(half + 1) * 512],
                                                 lnTb[:, cc, tt * 128:(tt + 1) * 128],
                                                 wcur[:, cc, half * 512:(half + 1) * 512],
                                                 start=(cc == 0), stop=False)
                                    li = e.matmul(ps_w[:, half * 512:(half + 1) * 512],
                                                  tri[0:1, :],
                                                  brow[:, half * 512:(half + 1) * 512],
                                                  start=False, stop=True)
                                li.then_inc(TE.h, 1)
                            block.tensor(_t)
                            TE.n += 1

                            def _v(e, t=t, t0=t0):
                                e.wait_ge(TE.h, t0 + 1)
                                e.tensor_copy(va4[:, t, :, 0:64],
                                              ps_w[:].rearrange(
                                                  "p (h j) -> p h j", j=64))
                                e.drain()
                                e.nop().then_inc(VE.h, 1)
                            block.vector(_v)
                            VE.n += 1
                            evac_tgts.append(VE.n)
                bar()

            if ATT_SUB < 4:
                return
            if layer == 0:
                wcc_finish()
            load_brow(f"{l}_bo")
            if trw:
                wprep_T(d[f"{l}_wo"], lambda cc: wcur[:, cc, :])
            else:
                wload_nat(d[f"{l}_wo"], wcur)
            if layer == 0:
                wcc_guard()
            bar()

            kT3 = kTa[:].rearrange("p (c q) -> p c q", q=T)
            cq = ctxq[:].rearrange("p (c q) -> p c q", q=QCW)
            rb_e = scrB[64:65, 768:1024].bitcast(BF16)
            rb_o = scrB[96:97, 768:1024].bitcast(BF16)
            alibi = u1[:, 3072:3584].bitcast(F32)
            pt_slot = [u2[:, 0:512], u2[:, 512:1024],
                       u1[:, 0:512], u1[:, 512:1024]]
            if layer == 1:
                LD.n += 16
                al_t = LD.n

                def _yal(e):
                    e.dma_start(alibi, d["c_alibi"]).then_inc(LD.h, 16)
                block.sync(_yal)

                def _sal(e, al_t=al_t):
                    e.wait_ge(LD.h, al_t)
                block.scalar(_sal)
            exp_done = {}
            mask_done = {}
            av_done = {}
            ctx_evac = {}
            den_dma = {}
            seq_params = {}
            norm_done = {}
            rec_dma = {}
            pav = [[ps_x[0], ps_x[1]], [ps_cd[:, 512:1024], ps_e]]
            slot_ve_guard = {}
            LAG = 3
            recb2 = [recb[:, :], scrB[:, 512:768].bitcast(BF16)] if RECB2 \
                else [recb[:, :], recb[:, :]]

            def emit_av(seqj):
                (h2, kt2, slot2, par2, first2, last2, pcg2, qc2) = seq_params[seqj]
                se_t = exp_done[seqj]
                ve_t = mask_done.get(seqj, None)
                ve_ctx = ctx_evac.get(pcg2 - 2, 0)
                pxp = pav[pcg2 % 2][par2]

                def _t2(e, h=h2, kt=kt2, slot=slot2, pxp=pxp,
                        se_t=se_t, ve_t=ve_t, ve_ctx=ve_ctx,
                        first=first2, last=last2):
                    e.wait_ge(SE.h, se_t)
                    if ve_t is not None:
                        e.wait_ge(VE.h, ve_t)
                    if first:
                        e.wait_ge(VE.h, ve_ctx)
                    li = e.matmul(pxp[0:65, :],
                                  va4[:, kt, h, :],
                                  pt_slot[slot],
                                  start=first, stop=last)
                    li.then_inc(TE.h, 1)
                block.tensor(_t2)
                TE.n += 1
                av_done[seqj] = TE.n
                if last2 and par2 == 1:
                    emit_pair_evac(pcg2, qc2)

            def emit_pair_evac(pcg2, qc2):
                av_t = TE.n
                pc2 = pcg2 % (NH // 2)
                h_e, h_o = 2 * pc2, 2 * pc2 + 1
                dn_prev = rec_dma.get(pcg2 - 1, 0)

                pxe, pxo = pav[pcg2 % 2]

                def _v(e, pc2=pc2, av_t=av_t, dn_prev=dn_prev,
                       pxe=pxe, pxo=pxo):
                    e.wait_ge(TE.h, av_t)
                    e.wait_ge(DN.h, dn_prev)
                    e.tensor_copy(cq[0:64, pc2, :], pxe[0:64, :])
                    e.tensor_copy(cq[64:128, pc2, :], pxo[0:64, :])
                    e.tensor_copy(scrB[64:65, 0:512], pxe[64:65, :])
                    e.tensor_copy(scrB[96:97, 0:512], pxo[64:65, :])
                    e.drain()
                    e.nop().then_inc(VE.h, 1)
                    e.reciprocal(scrB[64:65, 0:512], scrB[64:65, 0:512])
                    e.reciprocal(scrB[96:97, 0:512], scrB[96:97, 0:512])
                    e.drain()
                    e.tensor_copy(rb_e, scrB[64:65, 0:512])
                    e.tensor_copy(rb_o, scrB[96:97, 0:512])
                    e.drain()
                    e.nop().then_inc(VE.h, 1)
                block.vector(_v)
                VE.n += 2
                ctx_evac[pcg2] = VE.n - 1
                ve_t = VE.n
                DN.n += 32

                def _y1(e, ve_t=ve_t, h_e=h_e, h_o=h_o):
                    e.wait_ge(VE.h, ve_t)
                    e.dma_start(rec_dram[h_e:h_e + 1, :], rb_e).then_inc(DN.h, 16)
                    e.dma_start(rec_dram[h_o:h_o + 1, :], rb_o).then_inc(DN.h, 16)
                block.sync(_y1)
                rec_dma[pcg2] = DN.n
                st_t = DN.n
                nm_prev = norm_done.get(pcg2 - 2, 0)
                sl = pc2 % 2
                DN.n += 32

                def _y2(e, sl=sl, h_e=h_e, h_o=h_o, st_t=st_t,
                        nm_prev=nm_prev):
                    e.wait_ge(DN.h, st_t)
                    e.wait_ge(VE.h, nm_prev)
                    s0 = rec_dram[h_e, :].unsqueeze(0).broadcast_to([64, 512])
                    s1 = rec_dram[h_o, :].unsqueeze(0).broadcast_to([64, 512])
                    e.dma_start(recb2[sl][0:64, :], s0).then_inc(DN.h, 16)
                    e.dma_start(recb2[sl][64:128, :], s1).then_inc(DN.h, 16)
                block.sync(_y2)
                bc_t = DN.n
                VE.n += 1

                def _vn(e, pc2=pc2, sl=sl, bc_t=bc_t):
                    e.wait_ge(DN.h, bc_t)
                    e.tensor_tensor(cq[:, pc2, :], cq[:, pc2, :],
                                    recb2[sl][:, :], ALU.mult)
                    e.drain()
                    e.nop().then_inc(VE.h, 1)
                block.vector(_vn)
                norm_done[pcg2] = VE.n
                if pc2 == NH // 2 - 1:
                    emit_qc_tail(qc2, pcg2)

            def emit_qc_tail(qc2, pcg2):
                se_guard = SE.n
                pcg_base = pcg2 - (NH // 2 - 1)
                px_free = ctx_evac.get(pcg2 - 1, 0)
                for half in range(2 if ATT_SUB >= 5 else 0):
                    ve_free = VE.n if half > 0 else 0
                    t0 = TE.n
                    for jc in range(8):
                        nm = norm_done[pcg_base + jc]
                        TE.n += 1

                        def _t3(e, half=half, jc=jc, nm=nm,
                                se_guard=se_guard, ve_free=ve_free,
                                px_free=px_free):
                            e.wait_ge(VE.h, nm)
                            if jc == 0:
                                e.wait_ge(SE.h, se_guard)
                                if half == 0:
                                    e.wait_ge(VE.h, px_free)
                                else:
                                    e.wait_ge(VE.h, ve_free)
                            li = None
                            for ti in range(2):
                                t_loc = half * 2 + ti
                                for nh2 in range(2):
                                    pp = (ps_s if ti == 0 else ps_x)[nh2]
                                    li = e.matmul(
                                        pp[:, :],
                                        cq[:, jc, t_loc * 128:(t_loc + 1) * 128],
                                        wcur[:, jc, nh2 * 512:(nh2 + 1) * 512],
                                        start=(jc == 0), stop=False)
                            li.then_inc(TE.h, 1)
                        block.tensor(_t3)
                    TE.n += 1

                    def _t4(e, half=half):
                        li = None
                        for ti in range(2):
                            for nh2 in range(2):
                                pp = (ps_s if ti == 0 else ps_x)[nh2]
                                li = e.matmul(pp[:, :], tri[0:1, :],
                                              brow[:, nh2 * 512:(nh2 + 1) * 512],
                                              start=False, stop=True)
                        li.then_inc(TE.h, 1)
                    block.tensor(_t4)
                    op_t = TE.n
                    VE.n += 1

                    def _v3(e, half=half, qc=qc2, op_t=op_t):
                        e.wait_ge(TE.h, op_t)
                        for ti in range(2):
                            t = qc * 4 + half * 2 + ti
                            for nh2 in range(2):
                                pp = (ps_s if ti == 0 else ps_x)[nh2]
                                e.tensor_tensor(xbuf[:, t, nh2 * 512:(nh2 + 1) * 512],
                                                xbuf[:, t, nh2 * 512:(nh2 + 1) * 512],
                                                pp[:, :], ALU.add)
                        e.drain()
                        e.nop().then_inc(VE.h, 1)
                    block.vector(_v3)
                ctx_evac[pcg2] = VE.n
                ctx_evac[pcg2 - 1] = VE.n
                slot_ve_guard[0] = VE.n
                slot_ve_guard[2] = VE.n

            seqg = 0
            pcg = 0
            for qc in range(NQC):
                kts = list(range(4 * qc + 4)) if causal else list(range(NKT))
                nk = len(kts)
                for pc in range(NH // 2):
                    for ki, kt in enumerate(kts):
                        for par in range(2):
                            h = 2 * pc + par
                            p0_ = par * 64
                            chk = pc
                            slot = seqg % 4
                            ssl = seqg % NSS
                            seq_params[seqg] = (h, kt, slot, par,
                                                ki == 0, ki == nk - 1, pcg, qc)
                            w_exp = exp_done.get(seqg - NSS, 0)
                            w_ve = slot_ve_guard.pop(ssl, 0)

                            def _t(e, kt=kt, ssl=ssl, p0_=p0_, chk=chk,
                                   qc=qc, w_exp=w_exp, w_ve=w_ve):
                                e.wait_ge(SE.h, w_exp)
                                if w_ve:
                                    e.wait_ge(VE.h, w_ve)
                                li = e.matmul(ps_slot[ssl][:, :],
                                              kT3[p0_:p0_ + 64, chk, kt * 128:(kt + 1) * 128],
                                              qTa[p0_:p0_ + 64, chk, qc * QCW:(qc + 1) * QCW],
                                              start=True, stop=True)
                                li.then_inc(TE.h, 1)
                            block.tensor(_t)
                            TE.n += 1
                            sc_t = TE.n
                            w_av2 = av_done.get(seqg - 4, 0)

                            def _s(e, h=h, kt=kt, slot=slot, ssl=ssl,
                                   sc_t=sc_t, layer=layer, w_av2=w_av2):
                                e.wait_ge(TE.h, max(sc_t, w_av2))
                                if layer == 1:
                                    b = alibi[:, h * NKT + kt:h * NKT + kt + 1]
                                    e.activation(pt_slot[slot], ps_slot[ssl][:, :],
                                                 ACT.Exp, bias=b, scale=0.125)
                                else:
                                    e.activation(pt_slot[slot], ps_slot[ssl][:, :],
                                                 ACT.Exp, scale=0.125)
                                e.drain()
                                e.nop().then_inc(SE.h, 1)
                            block.scalar(_s)
                            SE.n += 1
                            exp_done[seqg] = SE.n

                            need_mask = causal and (4 * qc <= kt <= 4 * qc + 3)
                            if need_mask:
                                v = kt - 4 * qc
                                se_t = SE.n

                                def _v(e, slot=slot, v=v, se_t=se_t):
                                    e.wait_ge(SE.h, se_t)
                                    pt = pt_slot[slot]
                                    if v > 0:
                                        e.memset(pt[:, 0:128 * v], 0.0)
                                    e.tensor_tensor(pt[:, 128 * v:128 * (v + 1)],
                                                    pt[:, 128 * v:128 * (v + 1)],
                                                    tri[:], ALU.mult)
                                    e.drain()
                                    e.nop().then_inc(VE.h, 1)
                                block.vector(_v)
                                VE.n += 1
                                mask_done[seqg] = VE.n

                            if seqg - LAG >= 0:
                                emit_av(seqg - LAG)
                            seqg += 1
                    pcg += 1
            for seqj in range(max(0, seqg - LAG), seqg):
                emit_av(seqj)
            bar()


        def ffn0():
            kf = kTa[:]
            xr = kf[:, 0:4096].rearrange("p (c q) -> p c q", q=512)
            xk = kf[:, 4096:8192].rearrange("p (c q) -> p c q", q=512)
            kact = kf[:, 8192:12288].rearrange("p (c q) -> p c q", q=512)
            Vf = Va[:]
            wkf = Vf[:, 0:8192].rearrange("p (c k) -> p c k", k=C)
            wvf = Vf[:, 8192:16384].rearrange("p (c k) -> p c k", k=C)
            ln2T = qTa

            wprep_T(d["f0_wr"], lambda cc: wcur[:, cc, :])
            wprep_T(d["f0_wk"], lambda cc: wkf[:, cc, :])
            wprep_T(d["f0_wv"], lambda cc: wvf[:, cc, :])
            bar()
            for t in range(NT):
                ln_tile(t, u2[:, :])
                transpose_tile(u2[:, :],
                               lambda cc, t=t: ln2T[:, cc, t * 128:(t + 1) * 128])
            bar()

            dbuf = scrB[:, 0:512]
            rfull = scrB[:, 512:1536]
            for pc in range(NQC):
                s0 = pc * 512
                for cc in range(8):
                    def _v(e, cc=cc, s0=s0, pc=pc):
                        if pc == 0:
                            cur = ln2T[:, cc, 1:512]
                            prv = ln2T[:, cc, 0:511]
                            e.tensor_sub(dbuf[:, 1:512], cur, prv)
                            e.drain()
                            e.scalar_tensor_tensor(xr[:, cc, 1:512], dbuf[:, 1:512],
                                                   mur_c[:, cc:cc + 1], prv,
                                                   ALU.mult, ALU.add)
                            e.scalar_tensor_tensor(xk[:, cc, 1:512], dbuf[:, 1:512],
                                                   muk_c[:, cc:cc + 1], prv,
                                                   ALU.mult, ALU.add)
                            e.drain()
                            e.tensor_scalar(xr[:, cc, 0:1], ln2T[:, cc, 0:1],
                                            mur_c[:, cc:cc + 1], None, ALU.mult)
                            e.tensor_scalar(xk[:, cc, 0:1], ln2T[:, cc, 0:1],
                                            muk_c[:, cc:cc + 1], None, ALU.mult)
                            e.drain()
                        else:
                            cur = ln2T[:, cc, s0:s0 + 512]
                            prv = ln2T[:, cc, s0 - 1:s0 + 511]
                            e.tensor_sub(dbuf[:, :], cur, prv)
                            e.drain()
                            e.scalar_tensor_tensor(xr[:, cc, :], dbuf[:, :],
                                                   mur_c[:, cc:cc + 1], prv,
                                                   ALU.mult, ALU.add)
                            e.scalar_tensor_tensor(xk[:, cc, :], dbuf[:, :],
                                                   muk_c[:, cc:cc + 1], prv,
                                                   ALU.mult, ALU.add)
                            e.drain()
                        e.nop().then_inc(VE.h, 1)
                    block.vector(_v)
                    VE.n += 1
                xrk_done = VE.n

                se_evt = []
                for m in range(8):
                    t0 = TE.n
                    se_prev = se_evt[m - 2] if m >= 2 else SE.n

                    def _t(e, m=m, se_prev=se_prev, xrk=xrk_done):
                        e.wait_ge(VE.h, xrk)
                        e.wait_ge(SE.h, se_prev)
                        li = None
                        for cc in range(8):
                            li = e.matmul(ps_s[m % 2][:, :],
                                          wkf[:, cc, m * 128:(m + 1) * 128],
                                          xk[:, cc, :], start=(cc == 0), stop=(cc == 7))
                        li.then_inc(TE.h, 1)
                    block.tensor(_t)
                    TE.n += 1

                    def _s(e, m=m, t0=t0):
                        e.wait_ge(TE.h, t0 + 1)
                        e.activation(u2[:, (m % 2) * 512:(m % 2) * 512 + 512],
                                     ps_s[m % 2][:, :], ACT.Relu)
                        e.drain()
                        e.nop().then_inc(SE.h, 1)
                    block.scalar(_s)
                    SE.n += 1
                    se_evt.append(SE.n)
                    se_now = SE.n

                    def _v(e, m=m, se_now=se_now):
                        e.wait_ge(SE.h, se_now)
                        rk = u2[:, (m % 2) * 512:(m % 2) * 512 + 512]
                        e.tensor_tensor(kact[:, m, :], rk, rk, ALU.mult)
                        e.drain()
                        e.nop().then_inc(VE.h, 1)
                    block.vector(_v)
                    VE.n += 1
                kact_done = VE.n

                for tt in range(4):
                    t = pc * 4 + tt
                    t0 = TE.n
                    ve_now = VE.n

                    def _t(e, tt=tt, kd=kact_done, ve_now=ve_now):
                        e.wait_ge(VE.h, max(kd, ve_now))
                        li = None
                        for half in range(2):
                            for cc in range(8):
                                li = e.matmul(ps_w[:, half * 512:(half + 1) * 512],
                                              xr[:, cc, tt * 128:(tt + 1) * 128],
                                              wcur[:, cc, half * 512:(half + 1) * 512],
                                              start=(cc == 0), stop=(cc == 7))
                        li.then_inc(TE.h, 1)
                    block.tensor(_t)
                    TE.n += 1
                    r_t = TE.n

                    def _s(e, r_t=r_t):
                        e.wait_ge(TE.h, r_t)
                        e.activation(rfull[:, :], ps_w[:, :], ACT.Sigmoid)
                        e.drain()
                        e.nop().then_inc(SE.h, 1)
                    block.scalar(_s)
                    SE.n += 1
                    sg_t = SE.n

                    def _t2(e, tt=tt, sg_t=sg_t):
                        e.wait_ge(SE.h, sg_t)
                        li = None
                        for half in range(2):
                            for m in range(8):
                                li = e.matmul(ps_w[:, half * 512:(half + 1) * 512],
                                              kact[:, m, tt * 128:(tt + 1) * 128],
                                              wvf[:, m, half * 512:(half + 1) * 512],
                                              start=(m == 0), stop=(m == 7))
                        li.then_inc(TE.h, 1)
                    block.tensor(_t2)
                    TE.n += 1
                    ov_t = TE.n

                    def _v(e, t=t, ov_t=ov_t):
                        e.wait_ge(TE.h, ov_t)
                        e.tensor_tensor(rfull[:, :], rfull[:, :], ps_w[:, :],
                                        ALU.mult)
                        e.drain()
                        e.tensor_tensor(xbuf[:, t, :], xbuf[:, t, :], rfull[:, :],
                                        ALU.add)
                        e.drain()
                        e.nop().then_inc(VE.h, 1)
                    block.vector(_v)
                    VE.n += 1
            bar()

        def ffn1():
            xqT = qTa
            kf = kTa[:]
            wg_sl = [kf[:, s * 4096:(s + 1) * 4096]
                     .rearrange("p (c k) -> p c k", k=512) for s in range(2)]
            wu_sl = [kf[:, (2 + s) * 4096:(3 + s) * 4096]
                     .rearrange("p (c k) -> p c k", k=512) for s in range(2)]
            hqw = [Va[:, 0:512], Va[:, 512:1024]]
            wbf = wcur[:].rearrange("p r k -> p (r k)").bitcast(F32)
            hbuf = [wbf[:, 1024:1536], wbf[:, 1536:2048]]
            sgs = [wbf[:, 2048:2560], wbf[:, 2560:3072]]
            uas = [wbf[:, 3072:3584], wbf[:, 3584:4096]]
            u2s = [u2[:, 0:512], u2[:, 512:1024]]
            psg = [ps_ab[:, 0:512], ps_ab[:, 512:1024]]
            psu = [ps_cd[:, 0:512], ps_cd[:, 512:1024]]
            cc_t = CC.n

            wd_chunk = []
            for kk in range(NIT):
                if kk < 15:
                    wd_chunk.append(Va[:, (1 + kk) * 1024:(2 + kk) * 1024])
                elif kk < 19:
                    wd_chunk.append(ctxq[:, (kk - 15) * 1024:(kk - 14) * 1024])
                elif kk < 21:
                    wd_chunk.append(u1[:, (kk - 19) * 1024:(kk - 18) * 1024])
                else:
                    wd_chunk.append(qTa[:, 0, 0:1024])

            def wd_load(kk, guard_te):
                DN.n += 16

                def _y(e, kk=kk, guard_te=guard_te, cc_t=cc_t):
                    e.wait_ge(CC.h, cc_t)
                    if guard_te is not None:
                        e.wait_ge(TE.h, guard_te)
                    e.dma_start(wd_chunk[kk], ccd[kk * 128:(kk + 1) * 128, :])\
                        .then_inc(DN.h, 16)
                block.sync(_y)
            for kk in range(21):
                wd_load(kk, None)

            ICW = [512] * 5 + [256]
            NIC2 = 6
            wld_t = {}

            def wld(ic2, guard_te):
                sl = ic2 % 2
                w = ICW[ic2]
                i0 = ic2 * 512
                pieces = []
                i = i0
                while i < i0 + w:
                    b = i // SLC
                    take = min(i0 + w, (b + 1) * SLC) - i
                    pieces.append((b, i - b * SLC, i - i0, take))
                    i += take
                LD.n += 16 * 2 * len(pieces)

                def _y(e, sl=sl, pieces=pieces, guard_te=guard_te, cc_t=cc_t):
                    e.wait_ge(CC.h, cc_t)
                    if guard_te is not None:
                        e.wait_ge(TE.h, guard_te)
                    for (csrc, dsl) in ((ccg, wg_sl[sl]), (ccu, wu_sl[sl])):
                        for (b, so, do, take) in pieces:
                            e.dma_start(dsl[:, :, do:do + take],
                                        csrc[b * C:(b + 1) * C, so:so + take]
                                        .rearrange("(c p) k -> p c k", p=128))\
                                .then_inc(LD.h, 16)
                block.sync(_y)
                wld_t[ic2] = LD.n
            wld(0, None)
            wld(1, None)

            qa_prev = {}
            for t in range(NT):
                ln_tile(t, wbf[:, 0:1024])
                se_t = SE.n

                def _v0(e, se_t=se_t):
                    e.wait_ge(SE.h, se_t)
                block.vector(_v0)
                for sub in range(2):
                    quant(wbf[:, sub * 512:(sub + 1) * 512],
                          u2[:, sub * 512:(sub + 1) * 512], 512, 0,
                          sg=sgs[sub], ua=uas[sub],
                          sg_guard=qa_prev.get(sub, 0))
                    qa_prev[sub] = VE.n
                transpose_tile(u2[:, :],
                               lambda cc, t=t: xqT[:, cc, t * 128:(t + 1) * 128])
            bar()

            mm_d = {}
            silu_d = {}
            tt_d = {}
            q_d = {}
            trmm_d = {}
            trev_d = {}
            hq_dma = {}

            def emit_trhq(g):
                ic2, t = divmod(g, NT)
                w = ICW[ic2]
                nb = w // 128
                s = g % 2
                hq_prev = hq_dma.get(g - 2, 0)
                q_t = q_d[g]
                prev_ev = trev_d.get(g - 1, 0)
                t0 = TE.n
                TE.n += 1

                def _t(e, s=s, nb=nb, q_t=q_t, prev_ev=prev_ev):
                    e.wait_ge(VE.h, max(q_t, prev_ev))
                    li = None
                    for k in range(nb):
                        li = e.transpose(ps_tr[:, k * 128:(k + 1) * 128],
                                         u2s[s][:, k * 128:(k + 1) * 128],
                                         ident[:])
                    li.then_inc(TE.h, 1)
                block.tensor(_t)
                trmm_d[g] = TE.n
                VE.n += 1

                def _v(e, s=s, nb=nb, t0=t0, hq_prev=hq_prev):
                    e.wait_ge(TE.h, t0 + 1)
                    e.wait_ge(HQ.h, hq_prev)
                    for k in range(nb):
                        e.tensor_copy(hqw[s][:, k * 128:(k + 1) * 128],
                                      ps_tr[:, k * 128:(k + 1) * 128])
                    e.drain()
                    e.nop().then_inc(VE.h, 1)
                block.vector(_v)
                trev_d[g] = VE.n
                ve_t2 = VE.n
                HQ.n += 16

                def _y(e, ic2=ic2, t=t, s=s, nb=nb, ve_t2=ve_t2):
                    e.wait_ge(VE.h, ve_t2)
                    e.dma_start(hq_dram[t, ic2 * 4:ic2 * 4 + nb]
                                .rearrange("b p q -> p b q"),
                                hqw[s][:, 0:nb * 128]
                                .rearrange("p (b q) -> p b q", q=128))\
                        .then_inc(HQ.h, 16)
                block.sync(_y)
                hq_dma[g] = HQ.n

            G = NIC2 * NT
            for ic2 in range(NIC2):
                w = ICW[ic2]
                sl = ic2 % 2
                for t in range(NT):
                    g = ic2 * NT + t
                    p = t % 2
                    se_w = silu_d.get(g - 2, 0)
                    ve_w = tt_d.get(g - 2, 0)
                    ld_w = wld_t[ic2]
                    TE.n += 1

                    def _t(e, t=t, w=w, p=p, sl=sl, se_w=se_w, ve_w=ve_w,
                           ld_w=ld_w):
                        e.wait_ge(SE.h, se_w)
                        e.wait_ge(VE.h, ve_w)
                        e.wait_ge(LD.h, ld_w)
                        for cc in range(8):
                            e.matmul(psg[p][:, 0:w],
                                     xqT[:, cc, t * 128:(t + 1) * 128],
                                     wg_sl[sl][:, cc, 0:w],
                                     start=(cc == 0), stop=(cc == 7))
                        li = None
                        for cc in range(8):
                            li = e.matmul(psu[p][:, 0:w],
                                          xqT[:, cc, t * 128:(t + 1) * 128],
                                          wu_sl[sl][:, cc, 0:w],
                                          start=(cc == 0), stop=(cc == 7))
                        li.then_inc(TE.h, 1)
                    block.tensor(_t)
                    mm_d[g] = TE.n
                    q_w = q_d.get(g - 2, 0)
                    mt = mm_d[g]
                    SE.n += 1

                    def _s(e, w=w, p=p, mt=mt, q_w=q_w):
                        e.wait_ge(TE.h, mt)
                        e.wait_ge(VE.h, q_w)
                        e.activation(hbuf[p][:, 0:w], psg[p][:, 0:w], ACT.Silu)
                        e.drain()
                        e.nop().then_inc(SE.h, 1)
                    block.scalar(_s)
                    silu_d[g] = SE.n
                    st = silu_d[g]
                    trg = trmm_d.get(g - 2, 0)
                    VE.n += 1

                    def _v(e, w=w, p=p, st=st, trg=trg):
                        e.wait_ge(SE.h, st)
                        e.wait_ge(TE.h, trg)
                        e.tensor_tensor(hbuf[p][:, 0:w], hbuf[p][:, 0:w],
                                        psu[p][:, 0:w], ALU.mult)
                        e.drain()
                        e.nop().then_inc(VE.h, 1)
                    block.vector(_v)
                    tt_d[g] = VE.n
                    quant(hbuf[p][:, 0:w], u2s[p][:, 0:w], w, 0,
                          sg=sgs[p][:, 0:w], ua=uas[p],
                          src_evt=(VE, tt_d[g]),
                          sg_guard=q_d.get(g - 2, 0))
                    q_d[g] = VE.n
                    if g >= 2:
                        emit_trhq(g - 2)
                if ic2 + 2 < NIC2:
                    wld(ic2 + 2, TE.n)
            emit_trhq(G - 2)
            emit_trhq(G - 1)
            wd_load(21, TE.n)
            bar()

            hq_sl = [kf[:, 0:NIT * 128], kf[:, NIT * 128:2 * NIT * 128]]
            wd_all = DN.n
            use_done = {}
            for t in range(NT):
                sl = t % 2
                prev_use = use_done.get(t - 2, 0)
                HQ.n += 32

                def _y(e, t=t, sl=sl, prev_use=prev_use):
                    e.wait_ge(TE.h, prev_use)
                    hv = hq_sl[sl].rearrange("p (b q) -> p b q", q=128)
                    dv = hq_dram[t].rearrange("b p q -> p b q")
                    e.dma_start(hv[:, 0:11], dv[:, 0:11]).then_inc(HQ.h, 16)
                    e.dma_start(hv[:, 11:22], dv[:, 11:22]).then_inc(HQ.h, 16)
                block.sync(_y)
                hq_t = HQ.n
                ve_t = VE.n

                def _t(e, sl=sl, hq_t=hq_t, ve_t=ve_t, wd_all=wd_all):
                    e.wait_ge(HQ.h, hq_t)
                    e.wait_ge(VE.h, ve_t)
                    e.wait_ge(DN.h, wd_all)
                    li = None
                    for half in range(2):
                        for kk in range(NIT):
                            li = e.matmul(ps_w[:, half * 512:(half + 1) * 512],
                                          hq_sl[sl][:, kk * 128:(kk + 1) * 128],
                                          wd_chunk[kk][:, half * 512:(half + 1) * 512],
                                          start=(kk == 0), stop=(kk == NIT - 1))
                    li.then_inc(TE.h, 1)
                block.tensor(_t)
                TE.n += 1
                use_done[t] = TE.n
                mm_t = TE.n

                def _v(e, t=t, mm_t=mm_t):
                    e.wait_ge(TE.h, mm_t)
                    e.tensor_tensor(xbuf[:, t, :], xbuf[:, t, :], ps_w[:, :],
                                    ALU.add)
                    e.drain()
                    e.nop().then_inc(VE.h, 1)
                block.vector(_v)
                VE.n += 1
            bar()

        p0()
        if stage >= 1:
            attention(0)
        if stage >= 2:
            ffn0()
        if stage >= 3:
            attention(1)
        if stage >= 4:
            ffn1()
        emit_out()

    return nc


def _consts():
    ident = np.eye(128, dtype=np.float32).astype(ml_dtypes.bfloat16)
    kp = np.arange(128)[:, None]
    qp = np.arange(128)[None, :]
    tri = (qp >= kp).astype(np.float32).astype(ml_dtypes.bfloat16)
    alibi = np.zeros((128, NH * NKT), dtype=np.float32)
    xs = (2.0 ** 8) ** (1.0 / NH)
    slopes = np.array([1.0 / xs ** (i + 1) for i in range(NH)], dtype=np.float64)
    p = np.arange(128)
    for h in range(NH):
        for kt in range(NKT):
            alibi[:, h * NKT + kt] = (slopes[h] * (kt * 128 + p - (T - 1))).astype(np.float32)
    return ident, tri, alibi


def kernel(**inputs):
    B = 8
    ident, tri, alibi = _consts()
    nc = _build(STAGE)
    base = {
        "l0_wq": inputs["L0_attn_Wq"], "l0_wk": inputs["L0_attn_Wk"],
        "l0_wv": inputs["L0_attn_Wv"], "l0_wo": inputs["L0_attn_Wo"],
        "l0_bq": inputs["L0_attn_bq"], "l0_bk": inputs["L0_attn_bk"],
        "l0_bv": inputs["L0_attn_bv"], "l0_bo": inputs["L0_attn_bo"],
        "f0_wr": inputs["L0_ffn_W_r"], "f0_wk": inputs["L0_ffn_W_k"],
        "f0_wv": inputs["L0_ffn_W_v"],
        "f0_mur": inputs["L0_ffn_mu_r"], "f0_muk": inputs["L0_ffn_mu_k"],
        "l1_wq": inputs["L1_attn_Wq"], "l1_wk": inputs["L1_attn_Wk"],
        "l1_wv": inputs["L1_attn_Wv"], "l1_wo": inputs["L1_attn_Wo"],
        "l1_bq": inputs["L1_attn_bq"], "l1_bk": inputs["L1_attn_bk"],
        "l1_bv": inputs["L1_attn_bv"], "l1_bo": inputs["L1_attn_bo"],
        "c_ident": ident, "c_tri": tri, "c_alibi": alibi,
    }
    base = {k: np.ascontiguousarray(np.asarray(v)) for k, v in base.items()}
    wg = np.asarray(inputs["L1_ffn_w_gate"])
    wu = np.asarray(inputs["L1_ffn_w_up"])
    wd = np.asarray(inputs["L1_ffn_w_down"])
    in_maps = []
    for b in range(B):
        m = dict(base)
        m["x"] = np.ascontiguousarray(np.asarray(inputs["L0_attn_x"][b]))
        m["f1_wg_sl"] = np.ascontiguousarray(wg[b * (IFF // 8):(b + 1) * (IFF // 8)])
        m["f1_wu_sl"] = np.ascontiguousarray(wu[b * (IFF // 8):(b + 1) * (IFF // 8)])
        m["f1_wd_sl"] = np.ascontiguousarray(wd[:, b * (IFF // 8):(b + 1) * (IFF // 8)])
        in_maps.append(m)
    res = run_bass_kernel_spmd(nc, in_maps, core_ids=list(range(B)))
    out = np.stack([np.asarray(res.results[b]["out"]) for b in range(B)], axis=0)
    return out.astype(np.float32)


# version 17 (final)
# speedup vs baseline: 1.0062x; 1.0062x over previous
import contextlib
import numpy as np
import ml_dtypes
import concourse.bass as bass
import concourse.mybir as mybir
from concourse.bass_utils import run_bass_kernel_spmd

F32 = mybir.dt.float32
I32 = mybir.dt.int32
BF16 = mybir.dt.bfloat16
ALU = mybir.AluOpType
ACT = mybir.ActivationFunctionType
AXL = mybir.AxisListType

T, C = 2048, 1024
NT, NCH, NH, HD = 16, 8, 16, 64
QCW, NQC, NKT = 512, 4, 16
IFF, NIT = 2816, 22
EPS = 1e-5

STAGE = 4
NORM_GPS = False
RECB2 = True
ATT_SUB = 5


class _Sem:
    __slots__ = ("h", "n")

    def __init__(self, handle):
        self.h = handle
        self.n = 0


class _Plan:

    def __init__(self, real_block):
        self.rb = real_block
        self.q = {k: [] for k in ("sync", "tensor", "vector", "scalar", "gpsimd")}

    def sync(self, f):
        self.q["sync"].append(f)

    def tensor(self, f):
        self.q["tensor"].append(f)

    def vector(self, f):
        self.q["vector"].append(f)

    def scalar(self, f):
        self.q["scalar"].append(f)

    def gpsimd(self, f):
        self.q["gpsimd"].append(f)

    def flush(self):
        for name in ("sync", "tensor", "vector", "scalar", "gpsimd"):
            fns = self.q[name]
            if not fns:
                continue
            self.q[name] = []

            def _run(e, fns=fns):
                for f in fns:
                    f(e)
            getattr(self.rb, name)(_run)


def _build(stage):
    nc = bass.Bass("TRN2", target_bir_lowering=False, debug=False)

    def din(name, shape, dt=F32):
        return nc.dram_tensor(name, list(shape), dt, kind="ExternalInput").ap()

    d = {"x": din("x", [T, C])}
    for L in ("l0", "l1"):
        for w in ("wq", "wk", "wv", "wo"):
            d[f"{L}_{w}"] = din(f"{L}_{w}", [C, C])
        for b in ("bq", "bk", "bv", "bo"):
            d[f"{L}_{b}"] = din(f"{L}_{b}", [C])
    for w in ("wr", "wk", "wv"):
        d[f"f0_{w}"] = din(f"f0_{w}", [C, C])
    d["f0_mur"] = din("f0_mur", [C])
    d["f0_muk"] = din("f0_muk", [C])
    d["f1_wg_sl"] = din("f1_wg_sl", [IFF // 8, C])
    d["f1_wu_sl"] = din("f1_wu_sl", [IFF // 8, C])
    d["f1_wd_sl"] = din("f1_wd_sl", [C, IFF // 8])
    d["c_ident"] = din("c_ident", [128, 128], BF16)
    d["c_tri"] = din("c_tri", [128, 128], BF16)
    d["c_alibi"] = din("c_alibi", [128, NH * NKT], F32)
    out_ap = nc.dram_tensor("out", [T, C], F32, kind="ExternalOutput").ap()
    hq_dram = nc.dram_tensor("hq_scr", [NT, NIT, 128, 128], BF16,
                             kind="Internal").ap()
    rec_dram = nc.dram_tensor("rec_scr", [16, 512], BF16, kind="Internal").ap()
    qstage = nc.dram_tensor("qstage", [14, 128, 1024], BF16, kind="Internal").ap()
    SLC = IFF // 8
    ccg_in = nc.dram_tensor("ccg_in", [C, SLC], BF16, kind="Internal").ap()
    ccu_in = nc.dram_tensor("ccu_in", [C, SLC], BF16, kind="Internal").ap()
    ccd_in = nc.dram_tensor("ccd_in", [SLC, C], BF16, kind="Internal").ap()
    ccg = nc.dram_tensor("ccg", [8 * C, SLC], BF16, kind="Internal").ap()
    ccu = nc.dram_tensor("ccu", [8 * C, SLC], BF16, kind="Internal").ap()
    ccd = nc.dram_tensor("ccd", [IFF, C], BF16, kind="Internal").ap()

    with contextlib.ExitStack() as est:
        def sbuf(name, shape, dt):
            return est.enter_context(nc.sbuf_tensor(name, list(shape), dt))

        def psum(name, shape, dt):
            return est.enter_context(nc.psum_tensor(name, list(shape), dt))

        xbuf = sbuf("xbuf", [128, NT, C], F32)
        qTa = sbuf("qTa", [128, NCH, T], BF16)
        kTa = sbuf("kTa", [128, NCH * T], BF16)
        Va = sbuf("Va", [128, NT * (C + NH)], BF16)
        wcur = sbuf("wcur", [128, NCH, C], BF16)
        u1 = sbuf("u1", [128, NCH * QCW], BF16)
        u2 = sbuf("u2", [128, C], BF16)
        ctxq = sbuf("ctxq", [128, NCH * QCW], BF16)
        scrB = sbuf("scrB", [128, 1600], F32)
        recb = sbuf("recb", [128, 512], BF16)
        epsc = sbuf("epsc", [128, 1], F32)
        den16 = scrB[:, 0:512]
        stats = sbuf("stats", [128, 16], F32)
        bq_c = sbuf("bq_c", [128, 8], F32)
        bk_c = sbuf("bk_c", [128, 8], F32)
        mur_c = sbuf("mur_c", [128, 8], F32)
        muk_c = sbuf("muk_c", [128, 8], F32)
        brow = sbuf("brow", [1, C], BF16)
        ident = sbuf("ident", [128, 128], BF16)
        tri = sbuf("tri", [128, 128], BF16)

        ps_ab = psum("ps_ab", [128, 1024], F32)
        ps_cd = psum("ps_cd", [128, 1024], F32)
        ps_x = [psum("ps_x0", [128, 512], F32), psum("ps_x1", [128, 512], F32)]
        ps_e = psum("ps_e", [128, 512], F32)
        ps_tr = psum("ps_tr", [128, 1024], BF16)
        ps_s = [ps_ab[:, 0:512], ps_ab[:, 512:1024]]
        ps_slot = [ps_ab[:, 0:512], ps_cd[:, 0:512], ps_ab[:, 512:1024]]
        NSS = 3
        ps_w = ps_ab

        LD = _Sem(est.enter_context(nc.semaphore("s_ld")))
        GL = _Sem(est.enter_context(nc.semaphore("s_gl")))
        DN = _Sem(est.enter_context(nc.semaphore("s_dn")))
        HQ = _Sem(est.enter_context(nc.semaphore("s_hq")))
        OUT = _Sem(est.enter_context(nc.semaphore("s_out")))
        TE = _Sem(est.enter_context(nc.semaphore("s_te")))
        VE = _Sem(est.enter_context(nc.semaphore("s_ve")))
        SE = _Sem(est.enter_context(nc.semaphore("s_se")))
        GE = _Sem(est.enter_context(nc.semaphore("s_ge")))
        CC = _Sem(est.enter_context(nc.semaphore("s_cc")))
        BAR = _Sem(est.enter_context(nc.semaphore("s_bar")))

        rblock = est.enter_context(nc.Block())
        block = _Plan(rblock)

        def bar():
            block.flush()
            BAR.n += 5
            tgt = BAR.n

            def _f(e, tgt=tgt):
                e.drain()
                e.nop().then_inc(BAR.h, 1)
                e.wait_ge(BAR.h, tgt)
            for em in (rblock.sync, rblock.tensor, rblock.vector, rblock.scalar,
                       rblock.gpsimd):
                em(_f)

        def p0():
            def _y(e):
                xv = xbuf[:]
                dv = d["x"].rearrange("(t p) c -> p t c", p=128)
                for g in range(4):
                    e.dma_start(xv[:, g * 4:(g + 1) * 4],
                                dv[:, g * 4:(g + 1) * 4]).then_inc(LD.h, 16)
                for ap, key in ((ident[:], "c_ident"), (tri[:], "c_tri")):
                    e.dma_start(ap, d[key]).then_inc(LD.h, 16)
            LD.n += 96
            block.sync(_y)

            GL.n += 64
            GE.n += 1
            gl_t0 = GL.n

            def _g(e, gl_t0=gl_t0):
                for ap, key in ((bq_c[:], "l0_bq"), (bk_c[:], "l0_bk"),
                                (mur_c[:], "f0_mur"), (muk_c[:], "f0_muk")):
                    e.dma_start(ap, d[key].rearrange("(c p) -> p c", p=128),
                            allow_slow_non_contiguous=True).then_inc(GL.h, 16)
                e.wait_ge(GL.h, gl_t0)
                e.memset(epsc[:], float(EPS))
                e.drain()
                e.nop().then_inc(GE.h, 1)
            block.gpsimd(_g)
            ld_t0 = LD.n
            ge_t0 = GE.n

            def _v(e, ld_t0=ld_t0, ge_t0=ge_t0):
                e.wait_ge(LD.h, ld_t0)
                e.wait_ge(GE.h, ge_t0)
            block.vector(_v)
            bar()

        def load_brow(key):
            GL.n += 16

            def _g(e):
                e.dma_start(brow[:, :], d[key].rearrange("(a c) -> a c", a=1))\
                    .then_inc(GL.h, 16)
            block.gpsimd(_g)
            tgt = GL.n

            def _t(e, tgt=tgt):
                e.wait_ge(GL.h, tgt)
            block.tensor(_t)

        def ln_tile(t, dst, src=None):
            xin = xbuf[:, t, :] if src is None else src
            se0 = SE.n
            VE.n += 1

            def _v(e, xin=xin, se0=se0):
                e.wait_ge(SE.h, se0)
                e.bn_stats(stats[:, 0:6], xin[:, 0:512])
                e.bn_stats(stats[:, 6:12], xin[:, 512:1024])
                e.drain()
                e.bn_aggr(stats[:, 12:14], stats[:, 0:12])
                e.drain()
                e.nop().then_inc(VE.h, 1)
            block.vector(_v)
            ve1 = VE.n
            SE.n += 1

            def _s(e, ve1=ve1):
                e.wait_ge(VE.h, ve1)
                e.activation(stats[:, 14:15], stats[:, 13:14], ACT.Sqrt,
                             bias=epsc[:, 0:1])
                e.drain()
                e.nop().then_inc(SE.h, 1)
            block.scalar(_s)
            se1 = SE.n
            VE.n += 1

            def _v2(e, se1=se1):
                e.wait_ge(SE.h, se1)
                e.reciprocal(stats[:, 14:15], stats[:, 14:15])
                e.drain()
                e.scalar_tensor_tensor(stats[:, 15:16], stats[:, 12:13], -1.0,
                                       stats[:, 14:15], ALU.mult, ALU.mult)
                e.drain()
                e.nop().then_inc(VE.h, 1)
            block.vector(_v2)
            ve2 = VE.n
            SE.n += 1

            def _s2(e, xin=xin, dst=dst, ve2=ve2):
                e.wait_ge(VE.h, ve2)
                e.activation(dst, xin, ACT.Identity,
                             bias=stats[:, 15:16], scale=stats[:, 14:15])
                e.drain()
                e.nop().then_inc(SE.h, 1)
            block.scalar(_s2)

        def transpose_tile(src, dst_fn, ncc=8, ge_wait=None, te_guard=None):
            for g in range((ncc + 3) // 4):
                kk = min(4, ncc - g * 4)
                se_t = SE.n
                ve_t = VE.n
                t0 = TE.n

                TE.n += 1

                def _t(e, g=g, kk=kk, se_t=se_t, ve_t=ve_t, src=src,
                       ge_wait=ge_wait):
                    e.wait_ge(SE.h, se_t)
                    e.wait_ge(VE.h, ve_t)
                    if ge_wait is not None:
                        e.wait_ge(GE.h, ge_wait)
                    li = None
                    for k in range(kk):
                        cc = g * 4 + k
                        li = e.transpose(ps_tr[:, k * 128:(k + 1) * 128],
                                         src[:, cc * 128:(cc + 1) * 128], ident[:])
                    li.then_inc(TE.h, 1)
                block.tensor(_t)
                VE.n += 1

                def _v(e, g=g, kk=kk, t0=t0, dst_fn=dst_fn, te_guard=te_guard):
                    e.wait_ge(TE.h, t0 + 1)
                    if te_guard is not None:
                        e.wait_ge(TE.h, te_guard)
                    for k in range(kk):
                        e.tensor_copy(dst_fn(g * 4 + k),
                                      ps_tr[:, k * 128:(k + 1) * 128])
                    e.drain()
                    e.nop().then_inc(VE.h, 1)
                block.vector(_v)

        def wprep_T(src, wdst_fn):
            for r in range(8):
                ve_t = VE.n

                GL.n += 32

                def _g(e, r=r, ve_t=ve_t, src=src):
                    e.wait_ge(VE.h, ve_t)
                    e.dma_start(u2[:, 0:512],
                                src[r * 128:(r + 1) * 128, 0:512])\
                        .then_inc(GL.h, 16)
                    e.dma_start(u2[:, 512:1024],
                                src[r * 128:(r + 1) * 128, 512:1024])\
                        .then_inc(GL.h, 16)
                block.gpsimd(_g)
                gl_t = GL.n
                transpose_strip_w(gl_t, wdst_fn, r)

        def transpose_strip_w(gl_t, wdst_fn, r):
            for g in range(2):
                ve_t = VE.n
                t0 = TE.n

                TE.n += 1

                def _t(e, g=g, gl_t=gl_t, ve_t=ve_t):
                    e.wait_ge(GL.h, gl_t)
                    e.wait_ge(VE.h, ve_t)
                    li = None
                    for k in range(4):
                        cc = g * 4 + k
                        li = e.transpose(ps_tr[:, k * 128:(k + 1) * 128],
                                         u2[:, cc * 128:(cc + 1) * 128], ident[:])
                    li.then_inc(TE.h, 1)
                block.tensor(_t)
                VE.n += 1

                def _v(e, g=g, t0=t0, r=r, wdst_fn=wdst_fn):
                    e.wait_ge(TE.h, t0 + 1)
                    for k in range(4):
                        cc = g * 4 + k
                        e.tensor_copy(wdst_fn(cc)[:, r * 128:(r + 1) * 128],
                                      ps_tr[:, k * 128:(k + 1) * 128])
                    e.drain()
                    e.nop().then_inc(VE.h, 1)
                block.vector(_v)

        def wload_nat(src, wdst, nr=NCH):
            GL.n += 16

            def _g(e, src=src, wdst=wdst, nr=nr):
                e.dma_start(wdst[:, 0:nr, :],
                            src.rearrange("(r p) c -> p r c", p=128))\
                    .then_inc(GL.h, 16)
            block.gpsimd(_g)

        def _quant_gps(src, dst, w, nb, u_a, u_t, u_c, u_s, m_, scl, rb_, sb2):
            t1 = u_c[:, 128:128 + w // 2]
            t2 = u_c[:, 128 + w // 2:128 + w // 2 + w // 4]
            t3 = u_c[:, 128 + w // 2 - 64:128 + w // 2 - 64 + w // 8]

            def _g1(e):
                e.tensor_scalar(u_a.bitcast(I32), src.bitcast(I32), 0x7FFFFFFF,
                                None, ALU.bitwise_and)
                e.drain()
                a2 = u_a.rearrange("p (b k) -> p b k", k=2)
                e.tensor_tensor(t1, a2[:, :, 0], a2[:, :, 1], ALU.max)
                e.drain()
                b2 = t1.rearrange("p (b k) -> p b k", k=2)
                e.tensor_tensor(t2, b2[:, :, 0], b2[:, :, 1], ALU.max)
                e.drain()
                c2 = t2.rearrange("p (b k) -> p b k", k=2)
                e.tensor_tensor(t3, c2[:, :, 0], c2[:, :, 1], ALU.max)
                e.drain()
                d2 = t3.rearrange("p (b k) -> p b k", k=2)
                e.tensor_tensor(m_, d2[:, :, 0], d2[:, :, 1], ALU.max)
                e.drain()
                e.tensor_scalar(scl, m_, 1.0 / 6.0, 1e-12, ALU.mult, ALU.max)
                e.drain()
                e.tensor_scalar(sb2, scl, 0.5, None, ALU.mult)
                e.drain()
                e.nop().then_inc(GE.h, 1)
            GE.n += 1
            block.gpsimd(_g1)
            ge1 = GE.n
            VE.n += 1

            def _vr(e, ge1=ge1):
                e.wait_ge(GE.h, ge1)
                e.reciprocal(rb_, scl)
                e.drain()
                e.nop().then_inc(VE.h, 1)
            block.vector(_vr)
            ve1 = VE.n
            GE.n += 1

            def _g2(e, ve1=ve1):
                e.wait_ge(VE.h, ve1)
                rbb = rb_.unsqueeze(2).broadcast_to([pp, nb, 16])
                e.tensor_tensor(u_t.rearrange("p (b k) -> p b k", k=16),
                                u_a.rearrange("p (b k) -> p b k", k=16), rbb,
                                ALU.mult)
                e.drain()
                e.tensor_scalar(u_c, u_t, 5.0, 2.0, ALU.is_gt, ALU.mult)
                e.tensor_scalar(u_a, u_t, 0.25, None, ALU.is_gt)
                e.drain()
                e.scalar_tensor_tensor(u_c, u_t, 3.5, u_c, ALU.is_gt, ALU.add)
                e.scalar_tensor_tensor(u_a, u_t, 0.75, u_a, ALU.is_gt, ALU.add)
                e.drain()
                e.scalar_tensor_tensor(u_c, u_t, 2.5, u_c, ALU.is_gt, ALU.add)
                e.scalar_tensor_tensor(u_a, u_t, 1.25, u_a, ALU.is_gt, ALU.add)
                e.drain()
                e.scalar_tensor_tensor(u_a, u_t, 1.75, u_a, ALU.is_gt, ALU.add)
                e.drain()
                e.scalar_tensor_tensor(u_a, u_c, 2.0, u_a, ALU.mult, ALU.add)
                e.drain()
                sbb = sb2.unsqueeze(2).broadcast_to([pp, nb, 16])
                e.tensor_tensor(u_c.rearrange("p (b k) -> p b k", k=16),
                                u_a.rearrange("p (b k) -> p b k", k=16), sbb,
                                ALU.mult)
                e.drain()
                e.tensor_scalar(u_s, src, 0.0, 2.0, ALU.is_ge, ALU.mult)
                e.drain()
                e.scalar_tensor_tensor(dst, u_s, -1.0, u_c, ALU.add, ALU.mult)
                e.drain()
                e.nop().then_inc(GE.h, 1)
            block.gpsimd(_g2)

        def quant(src, dst, w, base, eng="dve", scr=None, pp=128,
                  sg=None, ua=None, src_evt=None, sg_guard=0):
            nb = w // 16
            S_ = (scrB if scr is None else scr)[0:pp, :]
            u_a = S_[:, base:base + w] if ua is None else ua[0:pp, 0:w]
            u_t = S_[:, base + w:base + 2 * w]
            u_c = S_[:, base + 2 * w:base + 3 * w]
            u_s = u_t
            m_ = S_[:, base + 2 * w:base + 2 * w + nb]
            scl = S_[:, base + 2 * w + nb:base + 2 * w + 2 * nb]
            rb_ = S_[:, base + 2 * w + 2 * nb:base + 2 * w + 3 * nb]
            sb2 = S_[:, base + 3 * w:base + 3 * w + nb]
            act_t = 0
            if sg is not None:
                SE.n += 1

                def _sa(e, src=src, sg=sg, src_evt=src_evt,
                        sg_guard=sg_guard, u_a=u_a):
                    if src_evt is not None:
                        e.wait_ge(src_evt[0].h, src_evt[1])
                    if sg_guard:
                        e.wait_ge(VE.h, sg_guard)
                    e.activation(u_a, src, ACT.Abs)
                    e.activation(sg, src, ACT.Sign)
                    e.drain()
                    e.nop().then_inc(SE.h, 1)
                block.scalar(_sa)
                act_t = SE.n

            def _v(e, act_t=act_t):
                if act_t:
                    e.wait_ge(SE.h, act_t)
                sv = src.rearrange("p (b k) -> p b k", k=16)
                e.tensor_reduce(m_, sv, AXL.X, ALU.max, apply_absolute_value=True)
                e.drain()
                e.tensor_scalar(scl, m_, 1.0 / 6.0, 1e-12, ALU.mult, ALU.max)
                e.drain()
                e.reciprocal(rb_, scl)
                e.tensor_scalar(sb2, scl, 0.5, None, ALU.mult)
                e.drain()
                if sg is None:
                    e.tensor_reduce(u_a, src.rearrange("p (n o) -> p n o", o=1),
                                    AXL.X, ALU.max, apply_absolute_value=True)
                    e.drain()
                rbb = rb_.unsqueeze(2).broadcast_to([pp, nb, 16])
                e.tensor_tensor(u_t.rearrange("p (b k) -> p b k", k=16),
                                u_a.rearrange("p (b k) -> p b k", k=16), rbb,
                                ALU.mult)
                e.drain()
                e.tensor_scalar(u_c, u_t, 5.0, 2.0, ALU.is_gt, ALU.mult)
                e.tensor_scalar(u_a, u_t, 0.25, None, ALU.is_gt)
                e.drain()
                e.scalar_tensor_tensor(u_c, u_t, 3.5, u_c, ALU.is_gt, ALU.add)
                e.scalar_tensor_tensor(u_a, u_t, 0.75, u_a, ALU.is_gt, ALU.add)
                e.drain()
                e.scalar_tensor_tensor(u_c, u_t, 2.5, u_c, ALU.is_gt, ALU.add)
                e.scalar_tensor_tensor(u_a, u_t, 1.25, u_a, ALU.is_gt, ALU.add)
                e.drain()
                e.scalar_tensor_tensor(u_a, u_t, 1.75, u_a, ALU.is_gt, ALU.add)
                e.drain()
                e.scalar_tensor_tensor(u_a, u_c, 2.0, u_a, ALU.mult, ALU.add)
                e.drain()
                sbb = sb2.unsqueeze(2).broadcast_to([pp, nb, 16])
                e.tensor_tensor(u_c.rearrange("p (b k) -> p b k", k=16),
                                u_a.rearrange("p (b k) -> p b k", k=16), sbb,
                                ALU.mult)
                e.drain()
                if sg is None:
                    e.tensor_scalar(u_s, src, 0.0, 2.0, ALU.is_ge, ALU.mult)
                    e.drain()
                    e.scalar_tensor_tensor(dst, u_s, -1.0, u_c, ALU.add,
                                           ALU.mult)
                else:
                    e.tensor_tensor(dst, u_c, sg, ALU.mult)
                e.drain()
                e.nop().then_inc(VE.h, 1)
            VE.n += 1
            block.vector(_v)

        def emit_out():
            ve_t = VE.n
            se_t = SE.n
            OUT.n += 64
            out_t = OUT.n

            def _y(e, ve_t=ve_t, se_t=se_t, out_t=out_t):
                e.wait_ge(VE.h, ve_t)
                e.wait_ge(SE.h, se_t)
                xv = xbuf[:]
                dv = out_ap.rearrange("(t p) c -> p t c", p=128)
                for g in range(4):
                    e.dma_start(dv[:, g * 4:(g + 1) * 4],
                                xv[:, g * 4:(g + 1) * 4]).then_inc(OUT.h, 16)
                e.wait_ge(OUT.h, out_t)
            block.sync(_y)
            block.flush()

        wcc_tiles = []
        for (srcw, cin) in ((d["f1_wg_sl"], ccg_in), (d["f1_wu_sl"], ccu_in)):
            for (r0, rr) in ((0, 128), (128, 128), (256, 96)):
                wcc_tiles.append(("gu", srcw[r0:r0 + rr, :], cin, r0, rr, C))
        for r in range(8):
            wcc_tiles.append(("d", d["f1_wd_sl"][r * 128:(r + 1) * 128, :],
                              ccd_in, r * 128, 128, SLC))
        wcc_st = {"li": 0, "qi": 0, "store": {}, "q_ve": {}, "ld_all": 0}
        cxf = ctxq[:].bitcast(F32)
        cxq_dst = ctxq[:, 2048:3072]

        def wcc_feed_load():
            if stage < 4 or wcc_st["li"] >= len(wcc_tiles):
                return
            i = wcc_st["li"]
            wcc_st["li"] += 1
            (kind, srcp, cin, r0, rr, w) = wcc_tiles[i]
            ve_t = wcc_st["q_ve"].get(i - 1, 0)
            LD.n += 16

            def _y(e, srcp=srcp, rr=rr, w=w, ve_t=ve_t):
                e.wait_ge(VE.h, ve_t)
                e.dma_start(cxf[0:rr, 0:w], srcp).then_inc(LD.h, 16)
            block.sync(_y)
            wcc_st["load_t", i] = LD.n

        def wcc_feed_quant():
            if stage < 4 or wcc_st["qi"] >= wcc_st["li"]:
                return
            i = wcc_st["qi"]
            wcc_st["qi"] += 1
            (kind, srcp, cin, r0, rr, w) = wcc_tiles[i]
            ld_t = wcc_st["load_t", i]
            st_prev = wcc_st["store"].get(i - 1, 0)
            VE.n += 1

            def _v0(e, ld_t=ld_t, st_prev=st_prev):
                e.wait_ge(LD.h, max(ld_t, st_prev))
                e.nop().then_inc(VE.h, 1)
            block.vector(_v0)
            if kind == "gu":
                for so_ in (0, 512):
                    quant(cxf[0:rr, so_:so_ + 512],
                          cxq_dst[0:rr, so_:so_ + 512], 512, 0, pp=rr)
            else:
                quant(cxf[0:rr, 0:w], cxq_dst[0:rr, 0:w], w, 0, pp=rr)
            wcc_st["q_ve"][i] = VE.n
            ve_q = VE.n
            LD.n += 16

            def _y2(e, i=i, rr=rr, w=w, ve_q=ve_q):
                e.wait_ge(VE.h, ve_q)
                e.dma_start(qstage[i, 0:rr, 0:w], cxq_dst[0:rr, 0:w])\
                    .then_inc(LD.h, 16)
            block.sync(_y2)
            wcc_st["store"][i] = LD.n

        def wcc_finish():
            if stage < 4:
                return
            while wcc_st["li"] < len(wcc_tiles):
                wcc_feed_load()
                wcc_feed_quant()
            while wcc_st["qi"] < len(wcc_tiles):
                wcc_feed_quant()
            tr_dmas = {}
            tr_evs = {}

            def tr_store(blocks):
                ve_entry = VE.n
                for g in range((len(blocks) + 3) // 4):
                    grp = blocks[g * 4:(g + 1) * 4]
                    ve_q = max([ve_entry] +
                               [tr_evs.get(k, 0) for k in range(len(grp))])
                    t0 = TE.n
                    TE.n += 1

                    def _t(e, grp=grp, ve_q=ve_q):
                        e.wait_ge(VE.h, ve_q)
                        li = None
                        for k, (sap, dap, rp, cw) in enumerate(grp):
                            li = e.transpose(ps_tr[0:cw, k * 128:k * 128 + rp],
                                             sap, ident[0:rp, 0:rp])
                        li.then_inc(TE.h, 1)
                    block.tensor(_t)
                    VE.n += 1
                    for k in range(len(grp)):
                        tr_evs[k] = VE.n
                    dprev = max([0] + [tr_dmas.get(k, 0)
                                       for k in range(len(grp))])

                    def _v(e, grp=grp, t0=t0, dprev=dprev):
                        e.wait_ge(TE.h, t0 + 1)
                        e.wait_ge(LD.h, dprev)
                        for k, (sap, dap, rp, cw) in enumerate(grp):
                            e.tensor_copy(recb[0:cw, k * 128:k * 128 + rp],
                                          ps_tr[0:cw, k * 128:k * 128 + rp])
                        e.drain()
                        e.nop().then_inc(VE.h, 1)
                    block.vector(_v)
                    ve_e = VE.n
                    for k in range(len(grp)):
                        LD.n += 16
                        tr_dmas[k] = LD.n

                    def _y(e, grp=grp, ve_e=ve_e):
                        e.wait_ge(VE.h, ve_e)
                        for k, (sap, dap, rp, cw) in enumerate(grp):
                            e.dma_start(dap, recb[0:cw, k * 128:k * 128 + rp])\
                                .then_inc(LD.h, 16)
                    block.sync(_y)

            for i, (kind, srcp, cin, r0, rr, w) in enumerate(wcc_tiles):
                ve_t = VE.n
                st_t = wcc_st["store"][i]
                LD.n += 16
                ld_t = LD.n

                def _y(e, i=i, rr=rr, w=w, ve_t=ve_t, st_t=st_t):
                    e.wait_ge(VE.h, ve_t)
                    e.wait_ge(LD.h, st_t)
                    e.dma_start(u2[0:rr, 0:w], qstage[i, 0:rr, 0:w])\
                        .then_inc(LD.h, 16)
                block.sync(_y)
                VE.n += 1

                def _v0(e, ld_t=ld_t):
                    e.wait_ge(LD.h, ld_t)
                    e.nop().then_inc(VE.h, 1)
                block.vector(_v0)
                blocks = []
                if kind == "gu":
                    for cc in range(8):
                        blocks.append((u2[0:rr, cc * 128:(cc + 1) * 128],
                                       cin[cc * 128:(cc + 1) * 128, r0:r0 + rr],
                                       rr, 128))
                else:
                    for b, bw in ((0, 128), (1, 128), (2, 96)):
                        blocks.append((u2[:, b * 128:b * 128 + bw],
                                       cin[b * 128:b * 128 + bw,
                                           r0:r0 + 128],
                                       128, bw))
                tr_store(blocks)
            ld_all = LD.n
            wcc_st["ld_all"] = ld_all
            CC.n += 3

            def _g(e, ld_all=ld_all):
                e.wait_ge(LD.h, ld_all)
                rg = [[0, 1, 2, 3, 4, 5, 6, 7]]
                e.collective_compute("AllGather", ALU.bypass, replica_groups=rg,
                                     ins=[ccg_in], outs=[ccg]).then_inc(CC.h, 1)
                e.collective_compute("AllGather", ALU.bypass, replica_groups=rg,
                                     ins=[ccu_in], outs=[ccu]).then_inc(CC.h, 1)
                e.collective_compute("AllGather", ALU.bypass, replica_groups=rg,
                                     ins=[ccd_in], outs=[ccd]).then_inc(CC.h, 1)
            block.gpsimd(_g)

        def wcc_guard():
            if stage < 4 or not wcc_st["ld_all"]:
                return
            ld_all = wcc_st["ld_all"]

            def _w(e, ld_all=ld_all):
                e.wait_ge(LD.h, ld_all)
            block.vector(_w)
            block.scalar(_w)

        def attention(layer):
            l = f"l{layer}"
            causal = (layer == 0)
            trw = (layer == 0)
            lnTb = u1[:].rearrange("p (c q) -> p c q", q=QCW)

            if layer == 1:
                GL.n += 32
                gl_b = GL.n

                def _g(e, gl_b=gl_b):
                    e.dma_start(bq_c[:], d["l1_bq"].rearrange("(c p) -> p c", p=128), allow_slow_non_contiguous=True).then_inc(GL.h, 16)
                    e.dma_start(bk_c[:], d["l1_bk"].rearrange("(c p) -> p c", p=128), allow_slow_non_contiguous=True).then_inc(GL.h, 16)
                    e.wait_ge(GL.h, gl_b)
                block.gpsimd(_g)
                bar()

            va4 = Va[:].rearrange("p (t h j) -> p t h j", h=NH, j=C // NH + 1)
            prods = ("k", "v", "q")[:ATT_SUB]
            for which in prods:
                if which == "v":
                    load_brow(f"{l}_bv")
                    VE.n += 1

                    def _vones(e):
                        e.memset(va4[:, :, :, 64:65], 1.0)
                        e.drain()
                        e.nop().then_inc(VE.h, 1)
                    block.vector(_vones)
                if trw:
                    wprep_T(d[f"{l}_w{which}"], lambda cc: wcur[:, cc, :])
                else:
                    wload_nat(d[f"{l}_w{which}"], wcur)
                bar()
                for qc in range(NQC):
                    if layer == 0:
                        wcc_feed_load()
                    for tt in range(4):
                        t = qc * 4 + tt
                        ln_tile(t, u2[:, :])
                        transpose_tile(u2[:, :],
                                       lambda cc, tt=tt: lnTb[:, cc, tt * 128:(tt + 1) * 128])
                    bar()
                    if layer == 0:
                        wcc_feed_quant()
                    if which in ("k", "q"):
                        dst = kTa[:].rearrange("p (c q) -> p c q", q=T) if which == "k" \
                            else qTa[:, :, :]
                        bcol = bk_c if which == "k" else bq_c
                        evac_tgts = []
                        for m in range(8):
                            t0 = TE.n
                            se_need = evac_tgts[m - 2] if m >= 2 else SE.n

                            def _t(e, m=m, se_need=se_need):
                                e.wait_ge(SE.h, se_need)
                                li = None
                                for cc in range(8):
                                    li = e.matmul(ps_s[m % 2][:, :],
                                                  wcur[:, cc, m * 128:(m + 1) * 128],
                                                  lnTb[:, cc, :],
                                                  start=(cc == 0), stop=(cc == 7))
                                li.then_inc(TE.h, 1)
                            block.tensor(_t)
                            TE.n += 1

                            def _s(e, m=m, qc=qc, t0=t0, dst=dst, bcol=bcol):
                                e.wait_ge(TE.h, t0 + 1)
                                e.activation(dst[:, m, qc * QCW:(qc + 1) * QCW],
                                             ps_s[m % 2][:, :], ACT.Identity,
                                             bias=bcol[:, m:m + 1])
                                e.drain()
                                e.nop().then_inc(SE.h, 1)
                            block.scalar(_s)
                            SE.n += 1
                            evac_tgts.append(SE.n)
                    else:
                        evac_tgts = []
                        for tt in range(4):
                            t = qc * 4 + tt
                            t0 = TE.n
                            ve_need = evac_tgts[tt - 1] if tt >= 1 else VE.n

                            def _t(e, tt=tt, ve_need=ve_need):
                                e.wait_ge(VE.h, ve_need)
                                li = None
                                for half in range(2):
                                    for cc in range(8):
                                        e.matmul(ps_w[:, half * 512:(half + 1) * 512],
                                                 lnTb[:, cc, tt * 128:(tt + 1) * 128],
                                                 wcur[:, cc, half * 512:(half + 1) * 512],
                                                 start=(cc == 0), stop=False)
                                    li = e.matmul(ps_w[:, half * 512:(half + 1) * 512],
                                                  tri[0:1, :],
                                                  brow[:, half * 512:(half + 1) * 512],
                                                  start=False, stop=True)
                                li.then_inc(TE.h, 1)
                            block.tensor(_t)
                            TE.n += 1

                            def _v(e, t=t, t0=t0):
                                e.wait_ge(TE.h, t0 + 1)
                                e.tensor_copy(va4[:, t, :, 0:64],
                                              ps_w[:].rearrange(
                                                  "p (h j) -> p h j", j=64))
                                e.drain()
                                e.nop().then_inc(VE.h, 1)
                            block.vector(_v)
                            VE.n += 1
                            evac_tgts.append(VE.n)
                bar()

            if ATT_SUB < 4:
                return
            if layer == 0:
                wcc_finish()
            load_brow(f"{l}_bo")
            if trw:
                wprep_T(d[f"{l}_wo"], lambda cc: wcur[:, cc, :])
            else:
                wload_nat(d[f"{l}_wo"], wcur)
            if layer == 0:
                wcc_guard()
            bar()

            kT3 = kTa[:].rearrange("p (c q) -> p c q", q=T)
            cq = ctxq[:].rearrange("p (c q) -> p c q", q=QCW)
            rb_e = scrB[64:65, 768:1024].bitcast(BF16)
            rb_o = scrB[96:97, 768:1024].bitcast(BF16)
            alibi = u1[:, 3072:3584].bitcast(F32)
            pt_slot = [u2[:, 0:512], u2[:, 512:1024],
                       u1[:, 0:512], u1[:, 512:1024]]
            if layer == 1:
                LD.n += 16
                al_t = LD.n

                def _yal(e):
                    e.dma_start(alibi, d["c_alibi"]).then_inc(LD.h, 16)
                block.sync(_yal)

                def _sal(e, al_t=al_t):
                    e.wait_ge(LD.h, al_t)
                block.scalar(_sal)
            exp_done = {}
            mask_done = {}
            av_done = {}
            ctx_evac = {}
            den_dma = {}
            seq_params = {}
            norm_done = {}
            rec_dma = {}
            pav = [[ps_x[0], ps_x[1]], [ps_cd[:, 512:1024], ps_e]]
            slot_ve_guard = {}
            LAG = 3
            recb2 = [recb[:, :], scrB[:, 512:768].bitcast(BF16)] if RECB2 \
                else [recb[:, :], recb[:, :]]

            def emit_av(seqj):
                (h2, kt2, slot2, par2, first2, last2, pcg2, qc2) = seq_params[seqj]
                se_t = exp_done[seqj]
                ve_t = mask_done.get(seqj, None)
                ve_ctx = ctx_evac.get(pcg2 - 2, 0)
                pxp = pav[pcg2 % 2][par2]

                def _t2(e, h=h2, kt=kt2, slot=slot2, pxp=pxp,
                        se_t=se_t, ve_t=ve_t, ve_ctx=ve_ctx,
                        first=first2, last=last2):
                    e.wait_ge(SE.h, se_t)
                    if ve_t is not None:
                        e.wait_ge(VE.h, ve_t)
                    if first:
                        e.wait_ge(VE.h, ve_ctx)
                    li = e.matmul(pxp[0:65, :],
                                  va4[:, kt, h, :],
                                  pt_slot[slot],
                                  start=first, stop=last)
                    li.then_inc(TE.h, 1)
                block.tensor(_t2)
                TE.n += 1
                av_done[seqj] = TE.n
                if last2 and par2 == 1:
                    emit_pair_evac(pcg2, qc2)

            def emit_pair_evac(pcg2, qc2):
                av_t = TE.n
                pc2 = pcg2 % (NH // 2)
                h_e, h_o = 2 * pc2, 2 * pc2 + 1
                dn_prev = rec_dma.get(pcg2 - 1, 0)

                pxe, pxo = pav[pcg2 % 2]

                def _v(e, pc2=pc2, av_t=av_t, dn_prev=dn_prev,
                       pxe=pxe, pxo=pxo):
                    e.wait_ge(TE.h, av_t)
                    e.wait_ge(DN.h, dn_prev)
                    e.tensor_copy(cq[0:64, pc2, :], pxe[0:64, :])
                    e.tensor_copy(cq[64:128, pc2, :], pxo[0:64, :])
                    e.tensor_copy(scrB[64:65, 0:512], pxe[64:65, :])
                    e.tensor_copy(scrB[96:97, 0:512], pxo[64:65, :])
                    e.drain()
                    e.nop().then_inc(VE.h, 1)
                    e.reciprocal(scrB[64:65, 0:512], scrB[64:65, 0:512])
                    e.reciprocal(scrB[96:97, 0:512], scrB[96:97, 0:512])
                    e.drain()
                    e.tensor_copy(rb_e, scrB[64:65, 0:512])
                    e.tensor_copy(rb_o, scrB[96:97, 0:512])
                    e.drain()
                    e.nop().then_inc(VE.h, 1)
                block.vector(_v)
                VE.n += 2
                ctx_evac[pcg2] = VE.n - 1
                ve_t = VE.n
                DN.n += 32

                def _y1(e, ve_t=ve_t, h_e=h_e, h_o=h_o):
                    e.wait_ge(VE.h, ve_t)
                    e.dma_start(rec_dram[h_e:h_e + 1, :], rb_e).then_inc(DN.h, 16)
                    e.dma_start(rec_dram[h_o:h_o + 1, :], rb_o).then_inc(DN.h, 16)
                block.sync(_y1)
                rec_dma[pcg2] = DN.n
                st_t = DN.n
                nm_prev = norm_done.get(pcg2 - 2, 0)
                sl = pc2 % 2
                DN.n += 32

                def _y2(e, sl=sl, h_e=h_e, h_o=h_o, st_t=st_t,
                        nm_prev=nm_prev):
                    e.wait_ge(DN.h, st_t)
                    e.wait_ge(VE.h, nm_prev)
                    s0 = rec_dram[h_e, :].unsqueeze(0).broadcast_to([64, 512])
                    s1 = rec_dram[h_o, :].unsqueeze(0).broadcast_to([64, 512])
                    e.dma_start(recb2[sl][0:64, :], s0).then_inc(DN.h, 16)
                    e.dma_start(recb2[sl][64:128, :], s1).then_inc(DN.h, 16)
                block.sync(_y2)
                bc_t = DN.n
                VE.n += 1

                def _vn(e, pc2=pc2, sl=sl, bc_t=bc_t):
                    e.wait_ge(DN.h, bc_t)
                    e.tensor_tensor(cq[:, pc2, :], cq[:, pc2, :],
                                    recb2[sl][:, :], ALU.mult)
                    e.drain()
                    e.nop().then_inc(VE.h, 1)
                block.vector(_vn)
                norm_done[pcg2] = VE.n
                if pc2 == NH // 2 - 1:
                    emit_qc_tail(qc2, pcg2)

            def emit_qc_tail(qc2, pcg2):
                se_guard = SE.n
                pcg_base = pcg2 - (NH // 2 - 1)
                px_free = ctx_evac.get(pcg2 - 1, 0)
                for half in range(2 if ATT_SUB >= 5 else 0):
                    ve_free = VE.n if half > 0 else 0
                    t0 = TE.n
                    for jc in range(8):
                        nm = norm_done[pcg_base + jc]
                        TE.n += 1

                        def _t3(e, half=half, jc=jc, nm=nm,
                                se_guard=se_guard, ve_free=ve_free,
                                px_free=px_free):
                            e.wait_ge(VE.h, nm)
                            if jc == 0:
                                e.wait_ge(SE.h, se_guard)
                                if half == 0:
                                    e.wait_ge(VE.h, px_free)
                                else:
                                    e.wait_ge(VE.h, ve_free)
                            li = None
                            for ti in range(2):
                                t_loc = half * 2 + ti
                                for nh2 in range(2):
                                    pp = (ps_s if ti == 0 else ps_x)[nh2]
                                    li = e.matmul(
                                        pp[:, :],
                                        cq[:, jc, t_loc * 128:(t_loc + 1) * 128],
                                        wcur[:, jc, nh2 * 512:(nh2 + 1) * 512],
                                        start=(jc == 0), stop=False)
                            li.then_inc(TE.h, 1)
                        block.tensor(_t3)
                    TE.n += 1

                    def _t4(e, half=half):
                        li = None
                        for ti in range(2):
                            for nh2 in range(2):
                                pp = (ps_s if ti == 0 else ps_x)[nh2]
                                li = e.matmul(pp[:, :], tri[0:1, :],
                                              brow[:, nh2 * 512:(nh2 + 1) * 512],
                                              start=False, stop=True)
                        li.then_inc(TE.h, 1)
                    block.tensor(_t4)
                    op_t = TE.n
                    VE.n += 1

                    def _v3(e, half=half, qc=qc2, op_t=op_t):
                        e.wait_ge(TE.h, op_t)
                        for ti in range(2):
                            t = qc * 4 + half * 2 + ti
                            for nh2 in range(2):
                                pp = (ps_s if ti == 0 else ps_x)[nh2]
                                e.tensor_tensor(xbuf[:, t, nh2 * 512:(nh2 + 1) * 512],
                                                xbuf[:, t, nh2 * 512:(nh2 + 1) * 512],
                                                pp[:, :], ALU.add)
                        e.drain()
                        e.nop().then_inc(VE.h, 1)
                    block.vector(_v3)
                ctx_evac[pcg2] = VE.n
                ctx_evac[pcg2 - 1] = VE.n
                slot_ve_guard[0] = VE.n
                slot_ve_guard[2] = VE.n

            seqg = 0
            pcg = 0
            for qc in range(NQC):
                kts = list(range(4 * qc + 4)) if causal else list(range(NKT))
                nk = len(kts)
                for pc in range(NH // 2):
                    for ki, kt in enumerate(kts):
                        for par in range(2):
                            h = 2 * pc + par
                            p0_ = par * 64
                            chk = pc
                            slot = seqg % 4
                            ssl = seqg % NSS
                            seq_params[seqg] = (h, kt, slot, par,
                                                ki == 0, ki == nk - 1, pcg, qc)
                            w_exp = exp_done.get(seqg - NSS, 0)
                            w_ve = slot_ve_guard.pop(ssl, 0)

                            def _t(e, kt=kt, ssl=ssl, p0_=p0_, chk=chk,
                                   qc=qc, w_exp=w_exp, w_ve=w_ve):
                                e.wait_ge(SE.h, w_exp)
                                if w_ve:
                                    e.wait_ge(VE.h, w_ve)
                                li = e.matmul(ps_slot[ssl][:, :],
                                              kT3[p0_:p0_ + 64, chk, kt * 128:(kt + 1) * 128],
                                              qTa[p0_:p0_ + 64, chk, qc * QCW:(qc + 1) * QCW],
                                              start=True, stop=True)
                                li.then_inc(TE.h, 1)
                            block.tensor(_t)
                            TE.n += 1
                            sc_t = TE.n
                            w_av2 = av_done.get(seqg - 4, 0)

                            def _s(e, h=h, kt=kt, slot=slot, ssl=ssl,
                                   sc_t=sc_t, layer=layer, w_av2=w_av2):
                                e.wait_ge(TE.h, max(sc_t, w_av2))
                                if layer == 1:
                                    b = alibi[:, h * NKT + kt:h * NKT + kt + 1]
                                    e.activation(pt_slot[slot], ps_slot[ssl][:, :],
                                                 ACT.Exp, bias=b, scale=0.125)
                                else:
                                    e.activation(pt_slot[slot], ps_slot[ssl][:, :],
                                                 ACT.Exp, scale=0.125)
                                e.drain()
                                e.nop().then_inc(SE.h, 1)
                            block.scalar(_s)
                            SE.n += 1
                            exp_done[seqg] = SE.n

                            need_mask = causal and (4 * qc <= kt <= 4 * qc + 3)
                            if need_mask:
                                v = kt - 4 * qc
                                se_t = SE.n

                                def _v(e, slot=slot, v=v, se_t=se_t):
                                    e.wait_ge(SE.h, se_t)
                                    pt = pt_slot[slot]
                                    if v > 0:
                                        e.memset(pt[:, 0:128 * v], 0.0)
                                    e.tensor_tensor(pt[:, 128 * v:128 * (v + 1)],
                                                    pt[:, 128 * v:128 * (v + 1)],
                                                    tri[:], ALU.mult)
                                    e.drain()
                                    e.nop().then_inc(VE.h, 1)
                                block.vector(_v)
                                VE.n += 1
                                mask_done[seqg] = VE.n

                            if seqg - LAG >= 0:
                                emit_av(seqg - LAG)
                            seqg += 1
                    pcg += 1
            for seqj in range(max(0, seqg - LAG), seqg):
                emit_av(seqj)
            bar()


        def ffn0():
            kf = kTa[:]
            xr = kf[:, 0:4096].rearrange("p (c q) -> p c q", q=512)
            xk = kf[:, 4096:8192].rearrange("p (c q) -> p c q", q=512)
            kact = kf[:, 8192:12288].rearrange("p (c q) -> p c q", q=512)
            Vf = Va[:]
            wkf = Vf[:, 0:8192].rearrange("p (c k) -> p c k", k=C)
            wvf = Vf[:, 8192:16384].rearrange("p (c k) -> p c k", k=C)
            ln2T = qTa

            wprep_T(d["f0_wr"], lambda cc: wcur[:, cc, :])
            wprep_T(d["f0_wk"], lambda cc: wkf[:, cc, :])
            wprep_T(d["f0_wv"], lambda cc: wvf[:, cc, :])
            bar()
            for t in range(NT):
                ln_tile(t, u2[:, :])
                transpose_tile(u2[:, :],
                               lambda cc, t=t: ln2T[:, cc, t * 128:(t + 1) * 128])
            bar()

            dbuf = scrB[:, 0:512]
            rfull = scrB[:, 512:1536]
            for pc in range(NQC):
                s0 = pc * 512
                for cc in range(8):
                    def _v(e, cc=cc, s0=s0, pc=pc):
                        if pc == 0:
                            cur = ln2T[:, cc, 1:512]
                            prv = ln2T[:, cc, 0:511]
                            e.tensor_sub(dbuf[:, 1:512], cur, prv)
                            e.drain()
                            e.scalar_tensor_tensor(xr[:, cc, 1:512], dbuf[:, 1:512],
                                                   mur_c[:, cc:cc + 1], prv,
                                                   ALU.mult, ALU.add)
                            e.scalar_tensor_tensor(xk[:, cc, 1:512], dbuf[:, 1:512],
                                                   muk_c[:, cc:cc + 1], prv,
                                                   ALU.mult, ALU.add)
                            e.drain()
                            e.tensor_scalar(xr[:, cc, 0:1], ln2T[:, cc, 0:1],
                                            mur_c[:, cc:cc + 1], None, ALU.mult)
                            e.tensor_scalar(xk[:, cc, 0:1], ln2T[:, cc, 0:1],
                                            muk_c[:, cc:cc + 1], None, ALU.mult)
                            e.drain()
                        else:
                            cur = ln2T[:, cc, s0:s0 + 512]
                            prv = ln2T[:, cc, s0 - 1:s0 + 511]
                            e.tensor_sub(dbuf[:, :], cur, prv)
                            e.drain()
                            e.scalar_tensor_tensor(xr[:, cc, :], dbuf[:, :],
                                                   mur_c[:, cc:cc + 1], prv,
                                                   ALU.mult, ALU.add)
                            e.scalar_tensor_tensor(xk[:, cc, :], dbuf[:, :],
                                                   muk_c[:, cc:cc + 1], prv,
                                                   ALU.mult, ALU.add)
                            e.drain()
                        e.nop().then_inc(VE.h, 1)
                    block.vector(_v)
                    VE.n += 1
                xrk_done = VE.n

                se_evt = []
                for m in range(8):
                    t0 = TE.n
                    se_prev = se_evt[m - 2] if m >= 2 else SE.n

                    def _t(e, m=m, se_prev=se_prev, xrk=xrk_done):
                        e.wait_ge(VE.h, xrk)
                        e.wait_ge(SE.h, se_prev)
                        li = None
                        for cc in range(8):
                            li = e.matmul(ps_s[m % 2][:, :],
                                          wkf[:, cc, m * 128:(m + 1) * 128],
                                          xk[:, cc, :], start=(cc == 0), stop=(cc == 7))
                        li.then_inc(TE.h, 1)
                    block.tensor(_t)
                    TE.n += 1

                    def _s(e, m=m, t0=t0):
                        e.wait_ge(TE.h, t0 + 1)
                        e.activation(u2[:, (m % 2) * 512:(m % 2) * 512 + 512],
                                     ps_s[m % 2][:, :], ACT.Relu)
                        e.drain()
                        e.nop().then_inc(SE.h, 1)
                    block.scalar(_s)
                    SE.n += 1
                    se_evt.append(SE.n)
                    se_now = SE.n

                    def _v(e, m=m, se_now=se_now):
                        e.wait_ge(SE.h, se_now)
                        rk = u2[:, (m % 2) * 512:(m % 2) * 512 + 512]
                        e.tensor_tensor(kact[:, m, :], rk, rk, ALU.mult)
                        e.drain()
                        e.nop().then_inc(VE.h, 1)
                    block.vector(_v)
                    VE.n += 1
                kact_done = VE.n

                for tt in range(4):
                    t = pc * 4 + tt
                    t0 = TE.n
                    ve_now = VE.n

                    def _t(e, tt=tt, kd=kact_done, ve_now=ve_now):
                        e.wait_ge(VE.h, max(kd, ve_now))
                        li = None
                        for half in range(2):
                            for cc in range(8):
                                li = e.matmul(ps_w[:, half * 512:(half + 1) * 512],
                                              xr[:, cc, tt * 128:(tt + 1) * 128],
                                              wcur[:, cc, half * 512:(half + 1) * 512],
                                              start=(cc == 0), stop=(cc == 7))
                        li.then_inc(TE.h, 1)
                    block.tensor(_t)
                    TE.n += 1
                    r_t = TE.n

                    def _s(e, r_t=r_t):
                        e.wait_ge(TE.h, r_t)
                        e.activation(rfull[:, :], ps_w[:, :], ACT.Sigmoid)
                        e.drain()
                        e.nop().then_inc(SE.h, 1)
                    block.scalar(_s)
                    SE.n += 1
                    sg_t = SE.n

                    def _t2(e, tt=tt, sg_t=sg_t):
                        e.wait_ge(SE.h, sg_t)
                        li = None
                        for half in range(2):
                            for m in range(8):
                                li = e.matmul(ps_w[:, half * 512:(half + 1) * 512],
                                              kact[:, m, tt * 128:(tt + 1) * 128],
                                              wvf[:, m, half * 512:(half + 1) * 512],
                                              start=(m == 0), stop=(m == 7))
                        li.then_inc(TE.h, 1)
                    block.tensor(_t2)
                    TE.n += 1
                    ov_t = TE.n

                    def _v(e, t=t, ov_t=ov_t):
                        e.wait_ge(TE.h, ov_t)
                        e.tensor_tensor(rfull[:, :], rfull[:, :], ps_w[:, :],
                                        ALU.mult)
                        e.drain()
                        e.tensor_tensor(xbuf[:, t, :], xbuf[:, t, :], rfull[:, :],
                                        ALU.add)
                        e.drain()
                        e.nop().then_inc(VE.h, 1)
                    block.vector(_v)
                    VE.n += 1
            bar()

        def ffn1():
            xqT = qTa
            kf = kTa[:]
            wg_sl = [kf[:, s * 4096:(s + 1) * 4096]
                     .rearrange("p (c k) -> p c k", k=512) for s in range(2)]
            wu_sl = [kf[:, (2 + s) * 4096:(3 + s) * 4096]
                     .rearrange("p (c k) -> p c k", k=512) for s in range(2)]
            hqw = [Va[:, 0:512], Va[:, 512:1024]]
            wbf = wcur[:].rearrange("p r k -> p (r k)").bitcast(F32)
            hbuf = [wbf[:, 1024:1536], wbf[:, 1536:2048]]
            sgs = [wbf[:, 2048:2560], wbf[:, 2560:3072]]
            uas = [wbf[:, 3072:3584], wbf[:, 3584:4096]]
            u2s = [u2[:, 0:512], u2[:, 512:1024]]
            psg = [ps_ab[:, 0:512], ps_ab[:, 512:1024]]
            psu = [ps_cd[:, 0:512], ps_cd[:, 512:1024]]
            cc_t = CC.n

            wd_chunk = []
            for kk in range(NIT):
                if kk < 15:
                    wd_chunk.append(Va[:, (1 + kk) * 1024:(2 + kk) * 1024])
                elif kk < 19:
                    wd_chunk.append(ctxq[:, (kk - 15) * 1024:(kk - 14) * 1024])
                elif kk < 21:
                    wd_chunk.append(u1[:, (kk - 19) * 1024:(kk - 18) * 1024])
                else:
                    wd_chunk.append(qTa[:, 0, 0:1024])

            def wd_load(kk, guard_te):
                DN.n += 16

                def _y(e, kk=kk, guard_te=guard_te, cc_t=cc_t):
                    e.wait_ge(CC.h, cc_t)
                    if guard_te is not None:
                        e.wait_ge(TE.h, guard_te)
                    e.dma_start(wd_chunk[kk], ccd[kk * 128:(kk + 1) * 128, :])\
                        .then_inc(DN.h, 16)
                block.sync(_y)
            for kk in range(21):
                wd_load(kk, None)

            ICW = [512] * 5 + [256]
            NIC2 = 6
            wld_t = {}

            def wld(ic2, guard_te):
                sl = ic2 % 2
                w = ICW[ic2]
                i0 = ic2 * 512
                pieces = []
                i = i0
                while i < i0 + w:
                    b = i // SLC
                    take = min(i0 + w, (b + 1) * SLC) - i
                    pieces.append((b, i - b * SLC, i - i0, take))
                    i += take
                LD.n += 16 * 2 * len(pieces)

                def _y(e, sl=sl, pieces=pieces, guard_te=guard_te, cc_t=cc_t):
                    e.wait_ge(CC.h, cc_t)
                    if guard_te is not None:
                        e.wait_ge(TE.h, guard_te)
                    for (csrc, dsl) in ((ccg, wg_sl[sl]), (ccu, wu_sl[sl])):
                        for (b, so, do, take) in pieces:
                            e.dma_start(dsl[:, :, do:do + take],
                                        csrc[b * C:(b + 1) * C, so:so + take]
                                        .rearrange("(c p) k -> p c k", p=128))\
                                .then_inc(LD.h, 16)
                block.sync(_y)
                wld_t[ic2] = LD.n
            wld(0, None)
            wld(1, None)

            qa_prev = {}
            for t in range(NT):
                ln_tile(t, wbf[:, 0:1024])
                se_t = SE.n

                def _v0(e, se_t=se_t):
                    e.wait_ge(SE.h, se_t)
                block.vector(_v0)
                for sub in range(2):
                    quant(wbf[:, sub * 512:(sub + 1) * 512],
                          u2[:, sub * 512:(sub + 1) * 512], 512, 0,
                          sg=sgs[sub], ua=uas[sub],
                          sg_guard=qa_prev.get(sub, 0))
                    qa_prev[sub] = VE.n
                transpose_tile(u2[:, :],
                               lambda cc, t=t: xqT[:, cc, t * 128:(t + 1) * 128])
            bar()

            mm_d = {}
            silu_d = {}
            tt_d = {}
            q_d = {}
            trmm_d = {}
            trev_d = {}
            hq_dma = {}

            def emit_trhq(g):
                ic2, t = divmod(g, NT)
                w = ICW[ic2]
                nb = w // 128
                s = g % 2
                hq_prev = hq_dma.get(g - 2, 0)
                q_t = q_d[g]
                prev_ev = trev_d.get(g - 1, 0)
                t0 = TE.n
                TE.n += 1

                def _t(e, s=s, nb=nb, q_t=q_t, prev_ev=prev_ev):
                    e.wait_ge(VE.h, max(q_t, prev_ev))
                    li = None
                    for k in range(nb):
                        li = e.transpose(ps_tr[:, k * 128:(k + 1) * 128],
                                         u2s[s][:, k * 128:(k + 1) * 128],
                                         ident[:])
                    li.then_inc(TE.h, 1)
                block.tensor(_t)
                trmm_d[g] = TE.n
                VE.n += 1

                def _v(e, s=s, nb=nb, t0=t0, hq_prev=hq_prev):
                    e.wait_ge(TE.h, t0 + 1)
                    e.wait_ge(HQ.h, hq_prev)
                    for k in range(nb):
                        e.tensor_copy(hqw[s][:, k * 128:(k + 1) * 128],
                                      ps_tr[:, k * 128:(k + 1) * 128])
                    e.drain()
                    e.nop().then_inc(VE.h, 1)
                block.vector(_v)
                trev_d[g] = VE.n
                ve_t2 = VE.n
                HQ.n += 16

                def _y(e, ic2=ic2, t=t, s=s, nb=nb, ve_t2=ve_t2):
                    e.wait_ge(VE.h, ve_t2)
                    e.dma_start(hq_dram[t, ic2 * 4:ic2 * 4 + nb]
                                .rearrange("b p q -> p b q"),
                                hqw[s][:, 0:nb * 128]
                                .rearrange("p (b q) -> p b q", q=128))\
                        .then_inc(HQ.h, 16)
                block.sync(_y)
                hq_dma[g] = HQ.n

            G = NIC2 * NT
            for ic2 in range(NIC2):
                w = ICW[ic2]
                sl = ic2 % 2
                for t in range(NT):
                    g = ic2 * NT + t
                    p = t % 2
                    se_w = silu_d.get(g - 2, 0)
                    ve_w = tt_d.get(g - 2, 0)
                    ld_w = wld_t[ic2]
                    TE.n += 1

                    def _t(e, t=t, w=w, p=p, sl=sl, se_w=se_w, ve_w=ve_w,
                           ld_w=ld_w):
                        e.wait_ge(SE.h, se_w)
                        e.wait_ge(VE.h, ve_w)
                        e.wait_ge(LD.h, ld_w)
                        for cc in range(8):
                            e.matmul(psg[p][:, 0:w],
                                     xqT[:, cc, t * 128:(t + 1) * 128],
                                     wg_sl[sl][:, cc, 0:w],
                                     start=(cc == 0), stop=(cc == 7))
                        li = None
                        for cc in range(8):
                            li = e.matmul(psu[p][:, 0:w],
                                          xqT[:, cc, t * 128:(t + 1) * 128],
                                          wu_sl[sl][:, cc, 0:w],
                                          start=(cc == 0), stop=(cc == 7))
                        li.then_inc(TE.h, 1)
                    block.tensor(_t)
                    mm_d[g] = TE.n
                    q_w = q_d.get(g - 2, 0)
                    mt = mm_d[g]
                    SE.n += 1

                    def _s(e, w=w, p=p, mt=mt, q_w=q_w):
                        e.wait_ge(TE.h, mt)
                        e.wait_ge(VE.h, q_w)
                        e.activation(hbuf[p][:, 0:w], psg[p][:, 0:w], ACT.Silu)
                        e.drain()
                        e.nop().then_inc(SE.h, 1)
                    block.scalar(_s)
                    silu_d[g] = SE.n
                    st = silu_d[g]
                    trg = trmm_d.get(g - 2, 0)
                    VE.n += 1

                    def _v(e, w=w, p=p, st=st, trg=trg):
                        e.wait_ge(SE.h, st)
                        e.wait_ge(TE.h, trg)
                        e.tensor_tensor(hbuf[p][:, 0:w], hbuf[p][:, 0:w],
                                        psu[p][:, 0:w], ALU.mult)
                        e.drain()
                        e.nop().then_inc(VE.h, 1)
                    block.vector(_v)
                    tt_d[g] = VE.n
                    quant(hbuf[p][:, 0:w], u2s[p][:, 0:w], w, 0,
                          sg=sgs[p][:, 0:w], ua=uas[p],
                          src_evt=(VE, tt_d[g]),
                          sg_guard=q_d.get(g - 2, 0))
                    q_d[g] = VE.n
                    if g >= 2:
                        emit_trhq(g - 2)
                if ic2 + 2 < NIC2:
                    wld(ic2 + 2, TE.n)
            emit_trhq(G - 2)
            emit_trhq(G - 1)
            wd_load(21, TE.n)
            bar()

            hq_sl = [kf[:, 0:NIT * 128], kf[:, NIT * 128:2 * NIT * 128]]
            wd_all = DN.n
            use_done = {}
            for t in range(NT):
                sl = t % 2
                prev_use = use_done.get(t - 2, 0)
                HQ.n += 32

                def _y(e, t=t, sl=sl, prev_use=prev_use):
                    e.wait_ge(TE.h, prev_use)
                    hv = hq_sl[sl].rearrange("p (b q) -> p b q", q=128)
                    dv = hq_dram[t].rearrange("b p q -> p b q")
                    e.dma_start(hv[:, 0:11], dv[:, 0:11]).then_inc(HQ.h, 16)
                    e.dma_start(hv[:, 11:22], dv[:, 11:22]).then_inc(HQ.h, 16)
                block.sync(_y)
                hq_t = HQ.n
                ve_t = VE.n

                def _t(e, sl=sl, hq_t=hq_t, ve_t=ve_t, wd_all=wd_all):
                    e.wait_ge(HQ.h, hq_t)
                    e.wait_ge(VE.h, ve_t)
                    e.wait_ge(DN.h, wd_all)
                    li = None
                    for half in range(2):
                        for kk in range(NIT):
                            li = e.matmul(ps_w[:, half * 512:(half + 1) * 512],
                                          hq_sl[sl][:, kk * 128:(kk + 1) * 128],
                                          wd_chunk[kk][:, half * 512:(half + 1) * 512],
                                          start=(kk == 0), stop=(kk == NIT - 1))
                    li.then_inc(TE.h, 1)
                block.tensor(_t)
                TE.n += 1
                use_done[t] = TE.n
                mm_t = TE.n

                def _v(e, t=t, mm_t=mm_t):
                    e.wait_ge(TE.h, mm_t)
                    e.tensor_tensor(xbuf[:, t, :], xbuf[:, t, :], ps_w[:, :],
                                    ALU.add)
                    e.drain()
                    e.nop().then_inc(VE.h, 1)
                block.vector(_v)
                VE.n += 1
            bar()

        p0()
        if stage >= 1:
            attention(0)
        if stage >= 2:
            ffn0()
        if stage >= 3:
            attention(1)
        if stage >= 4:
            ffn1()
        emit_out()

    return nc


def _consts():
    ident = np.eye(128, dtype=np.float32).astype(ml_dtypes.bfloat16)
    kp = np.arange(128)[:, None]
    qp = np.arange(128)[None, :]
    tri = (qp >= kp).astype(np.float32).astype(ml_dtypes.bfloat16)
    alibi = np.zeros((128, NH * NKT), dtype=np.float32)
    xs = (2.0 ** 8) ** (1.0 / NH)
    slopes = np.array([1.0 / xs ** (i + 1) for i in range(NH)], dtype=np.float64)
    p = np.arange(128)
    for h in range(NH):
        for kt in range(NKT):
            alibi[:, h * NKT + kt] = (slopes[h] * (kt * 128 + p - (T - 1))).astype(np.float32)
    return ident, tri, alibi


def kernel(**inputs):
    B = 8
    ident, tri, alibi = _consts()
    nc = _build(STAGE)
    base = {
        "l0_wq": inputs["L0_attn_Wq"], "l0_wk": inputs["L0_attn_Wk"],
        "l0_wv": inputs["L0_attn_Wv"], "l0_wo": inputs["L0_attn_Wo"],
        "l0_bq": inputs["L0_attn_bq"], "l0_bk": inputs["L0_attn_bk"],
        "l0_bv": inputs["L0_attn_bv"], "l0_bo": inputs["L0_attn_bo"],
        "f0_wr": inputs["L0_ffn_W_r"], "f0_wk": inputs["L0_ffn_W_k"],
        "f0_wv": inputs["L0_ffn_W_v"],
        "f0_mur": inputs["L0_ffn_mu_r"], "f0_muk": inputs["L0_ffn_mu_k"],
        "l1_wq": inputs["L1_attn_Wq"], "l1_wk": inputs["L1_attn_Wk"],
        "l1_wv": inputs["L1_attn_Wv"], "l1_wo": inputs["L1_attn_Wo"],
        "l1_bq": inputs["L1_attn_bq"], "l1_bk": inputs["L1_attn_bk"],
        "l1_bv": inputs["L1_attn_bv"], "l1_bo": inputs["L1_attn_bo"],
        "c_ident": ident, "c_tri": tri, "c_alibi": alibi,
    }
    base = {k: np.ascontiguousarray(np.asarray(v)) for k, v in base.items()}
    wg = np.asarray(inputs["L1_ffn_w_gate"])
    wu = np.asarray(inputs["L1_ffn_w_up"])
    wd = np.asarray(inputs["L1_ffn_w_down"])
    in_maps = []
    for b in range(B):
        m = dict(base)
        m["x"] = np.ascontiguousarray(np.asarray(inputs["L0_attn_x"][b]))
        m["f1_wg_sl"] = np.ascontiguousarray(wg[b * (IFF // 8):(b + 1) * (IFF // 8)])
        m["f1_wu_sl"] = np.ascontiguousarray(wu[b * (IFF // 8):(b + 1) * (IFF // 8)])
        m["f1_wd_sl"] = np.ascontiguousarray(wd[:, b * (IFF // 8):(b + 1) * (IFF // 8)])
        in_maps.append(m)
    res = run_bass_kernel_spmd(nc, in_maps, core_ids=list(range(B)))
    out = np.stack([np.asarray(res.results[b]["out"]) for b in range(B)], axis=0)
    return out.astype(np.float32)


# version 18
# speedup vs baseline: 1.0096x; 1.0034x over previous
import contextlib
import numpy as np
import ml_dtypes
import concourse.bass as bass
import concourse.mybir as mybir
from concourse.bass_utils import run_bass_kernel_spmd

F32 = mybir.dt.float32
I32 = mybir.dt.int32
BF16 = mybir.dt.bfloat16
ALU = mybir.AluOpType
ACT = mybir.ActivationFunctionType
AXL = mybir.AxisListType

T, C = 2048, 1024
NT, NCH, NH, HD = 16, 8, 16, 64
QCW, NQC, NKT = 512, 4, 16
IFF, NIT = 2816, 22
EPS = 1e-5

STAGE = 4
NORM_GPS = False
RECB2 = True
ATT_SUB = 5


class _Sem:
    __slots__ = ("h", "n")

    def __init__(self, handle):
        self.h = handle
        self.n = 0


class _Plan:

    def __init__(self, real_block):
        self.rb = real_block
        self.q = {k: [] for k in ("sync", "tensor", "vector", "scalar", "gpsimd")}

    def sync(self, f):
        self.q["sync"].append(f)

    def tensor(self, f):
        self.q["tensor"].append(f)

    def vector(self, f):
        self.q["vector"].append(f)

    def scalar(self, f):
        self.q["scalar"].append(f)

    def gpsimd(self, f):
        self.q["gpsimd"].append(f)

    def flush(self):
        for name in ("sync", "tensor", "vector", "scalar", "gpsimd"):
            fns = self.q[name]
            if not fns:
                continue
            self.q[name] = []

            def _run(e, fns=fns):
                for f in fns:
                    f(e)
            getattr(self.rb, name)(_run)


def _build(stage):
    nc = bass.Bass("TRN2", target_bir_lowering=False, debug=False)

    def din(name, shape, dt=F32):
        return nc.dram_tensor(name, list(shape), dt, kind="ExternalInput").ap()

    d = {"x": din("x", [T, C])}
    for L in ("l0", "l1"):
        for w in ("wq", "wk", "wv", "wo"):
            d[f"{L}_{w}"] = din(f"{L}_{w}", [C, C])
        for b in ("bq", "bk", "bv", "bo"):
            d[f"{L}_{b}"] = din(f"{L}_{b}", [C])
    for w in ("wr", "wk", "wv"):
        d[f"f0_{w}"] = din(f"f0_{w}", [C, C])
    d["f0_mur"] = din("f0_mur", [C])
    d["f0_muk"] = din("f0_muk", [C])
    d["f1_wg_sl"] = din("f1_wg_sl", [IFF // 8, C])
    d["f1_wu_sl"] = din("f1_wu_sl", [IFF // 8, C])
    d["f1_wd_sl"] = din("f1_wd_sl", [C, IFF // 8])
    d["c_ident"] = din("c_ident", [128, 128], BF16)
    d["c_tri"] = din("c_tri", [128, 128], BF16)
    d["c_alibi"] = din("c_alibi", [128, NH * NKT], F32)
    out_ap = nc.dram_tensor("out", [T, C], F32, kind="ExternalOutput").ap()
    hq_dram = nc.dram_tensor("hq_scr", [NT, NIT, 128, 128], BF16,
                             kind="Internal").ap()
    rec_dram = nc.dram_tensor("rec_scr", [16, 512], BF16, kind="Internal").ap()
    qstage = nc.dram_tensor("qstage", [14, 128, 1024], BF16, kind="Internal").ap()
    SLC = IFF // 8
    ccg_in = nc.dram_tensor("ccg_in", [C, SLC], BF16, kind="Internal").ap()
    ccu_in = nc.dram_tensor("ccu_in", [C, SLC], BF16, kind="Internal").ap()
    ccd_in = nc.dram_tensor("ccd_in", [SLC, C], BF16, kind="Internal").ap()
    ccg = nc.dram_tensor("ccg", [8 * C, SLC], BF16, kind="Internal").ap()
    ccu = nc.dram_tensor("ccu", [8 * C, SLC], BF16, kind="Internal").ap()
    ccd = nc.dram_tensor("ccd", [IFF, C], BF16, kind="Internal").ap()

    with contextlib.ExitStack() as est:
        def sbuf(name, shape, dt):
            return est.enter_context(nc.sbuf_tensor(name, list(shape), dt))

        def psum(name, shape, dt):
            return est.enter_context(nc.psum_tensor(name, list(shape), dt))

        xbuf = sbuf("xbuf", [128, NT, C], F32)
        qTa = sbuf("qTa", [128, NCH, T], BF16)
        kTa = sbuf("kTa", [128, NCH * T], BF16)
        Va = sbuf("Va", [128, NT * (C + NH)], BF16)
        wcur = sbuf("wcur", [128, NCH, C], BF16)
        u1 = sbuf("u1", [128, NCH * QCW], BF16)
        u2 = sbuf("u2", [128, C], BF16)
        ctxq = sbuf("ctxq", [128, NCH * QCW], BF16)
        scrB = sbuf("scrB", [128, 1600], F32)
        recb = sbuf("recb", [128, 512], BF16)
        epsc = sbuf("epsc", [128, 1], F32)
        den16 = scrB[:, 0:512]
        stats = sbuf("stats", [128, 16], F32)
        bq_c = sbuf("bq_c", [128, 8], F32)
        bk_c = sbuf("bk_c", [128, 8], F32)
        mur_c = sbuf("mur_c", [128, 8], F32)
        muk_c = sbuf("muk_c", [128, 8], F32)
        brow = sbuf("brow", [1, C], BF16)
        ident = sbuf("ident", [128, 128], BF16)
        tri = sbuf("tri", [128, 128], BF16)

        ps_ab = psum("ps_ab", [128, 1024], F32)
        ps_cd = psum("ps_cd", [128, 1024], F32)
        ps_x = [psum("ps_x0", [128, 512], F32), psum("ps_x1", [128, 512], F32)]
        ps_e = psum("ps_e", [128, 512], F32)
        ps_tr = psum("ps_tr", [128, 1024], BF16)
        ps_s = [ps_ab[:, 0:512], ps_ab[:, 512:1024]]
        ps_slot = [ps_ab[:, 0:512], ps_cd[:, 0:512], ps_ab[:, 512:1024]]
        NSS = 3
        ps_w = ps_ab

        LD = _Sem(est.enter_context(nc.semaphore("s_ld")))
        GL = _Sem(est.enter_context(nc.semaphore("s_gl")))
        DN = _Sem(est.enter_context(nc.semaphore("s_dn")))
        HQ = _Sem(est.enter_context(nc.semaphore("s_hq")))
        OUT = _Sem(est.enter_context(nc.semaphore("s_out")))
        TE = _Sem(est.enter_context(nc.semaphore("s_te")))
        VE = _Sem(est.enter_context(nc.semaphore("s_ve")))
        SE = _Sem(est.enter_context(nc.semaphore("s_se")))
        GE = _Sem(est.enter_context(nc.semaphore("s_ge")))
        CC = _Sem(est.enter_context(nc.semaphore("s_cc")))
        BAR = _Sem(est.enter_context(nc.semaphore("s_bar")))

        rblock = est.enter_context(nc.Block())
        block = _Plan(rblock)

        def bar():
            block.flush()
            BAR.n += 5
            tgt = BAR.n

            def _f(e, tgt=tgt):
                e.drain()
                e.nop().then_inc(BAR.h, 1)
                e.wait_ge(BAR.h, tgt)
            for em in (rblock.sync, rblock.tensor, rblock.vector, rblock.scalar,
                       rblock.gpsimd):
                em(_f)

        def p0():
            def _y(e):
                xv = xbuf[:]
                dv = d["x"].rearrange("(t p) c -> p t c", p=128)
                for g in range(4):
                    e.dma_start(xv[:, g * 4:(g + 1) * 4],
                                dv[:, g * 4:(g + 1) * 4]).then_inc(LD.h, 16)
                for ap, key in ((ident[:], "c_ident"), (tri[:], "c_tri")):
                    e.dma_start(ap, d[key]).then_inc(LD.h, 16)
            LD.n += 96
            block.sync(_y)

            GL.n += 64
            GE.n += 1
            gl_t0 = GL.n

            def _g(e, gl_t0=gl_t0):
                for ap, key in ((bq_c[:], "l0_bq"), (bk_c[:], "l0_bk"),
                                (mur_c[:], "f0_mur"), (muk_c[:], "f0_muk")):
                    e.dma_start(ap, d[key].rearrange("(c p) -> p c", p=128),
                            allow_slow_non_contiguous=True).then_inc(GL.h, 16)
                e.wait_ge(GL.h, gl_t0)
                e.memset(epsc[:], float(EPS))
                e.drain()
                e.nop().then_inc(GE.h, 1)
            block.gpsimd(_g)
            ld_t0 = LD.n
            ge_t0 = GE.n

            def _v(e, ld_t0=ld_t0, ge_t0=ge_t0):
                e.wait_ge(LD.h, ld_t0)
                e.wait_ge(GE.h, ge_t0)
            block.vector(_v)
            bar()

        def load_brow(key):
            GL.n += 16

            def _g(e):
                e.dma_start(brow[:, :], d[key].rearrange("(a c) -> a c", a=1))\
                    .then_inc(GL.h, 16)
            block.gpsimd(_g)
            tgt = GL.n

            def _t(e, tgt=tgt):
                e.wait_ge(GL.h, tgt)
            block.tensor(_t)

        def ln_tile(t, dst, src=None):
            xin = xbuf[:, t, :] if src is None else src
            se0 = SE.n
            VE.n += 1

            def _v(e, xin=xin, se0=se0):
                e.wait_ge(SE.h, se0)
                e.bn_stats(stats[:, 0:6], xin[:, 0:512])
                e.bn_stats(stats[:, 6:12], xin[:, 512:1024])
                e.drain()
                e.bn_aggr(stats[:, 12:14], stats[:, 0:12])
                e.drain()
                e.nop().then_inc(VE.h, 1)
            block.vector(_v)
            ve1 = VE.n
            SE.n += 1

            def _s(e, ve1=ve1):
                e.wait_ge(VE.h, ve1)
                e.activation(stats[:, 14:15], stats[:, 13:14], ACT.Sqrt,
                             bias=epsc[:, 0:1])
                e.drain()
                e.nop().then_inc(SE.h, 1)
            block.scalar(_s)
            se1 = SE.n
            VE.n += 1

            def _v2(e, se1=se1):
                e.wait_ge(SE.h, se1)
                e.reciprocal(stats[:, 14:15], stats[:, 14:15])
                e.drain()
                e.scalar_tensor_tensor(stats[:, 15:16], stats[:, 12:13], -1.0,
                                       stats[:, 14:15], ALU.mult, ALU.mult)
                e.drain()
                e.nop().then_inc(VE.h, 1)
            block.vector(_v2)
            ve2 = VE.n
            SE.n += 1

            def _s2(e, xin=xin, dst=dst, ve2=ve2):
                e.wait_ge(VE.h, ve2)
                e.activation(dst, xin, ACT.Identity,
                             bias=stats[:, 15:16], scale=stats[:, 14:15])
                e.drain()
                e.nop().then_inc(SE.h, 1)
            block.scalar(_s2)

        def transpose_tile(src, dst_fn, ncc=8, ge_wait=None, te_guard=None):
            for g in range((ncc + 3) // 4):
                kk = min(4, ncc - g * 4)
                se_t = SE.n
                ve_t = VE.n
                t0 = TE.n

                TE.n += 1

                def _t(e, g=g, kk=kk, se_t=se_t, ve_t=ve_t, src=src,
                       ge_wait=ge_wait):
                    e.wait_ge(SE.h, se_t)
                    e.wait_ge(VE.h, ve_t)
                    if ge_wait is not None:
                        e.wait_ge(GE.h, ge_wait)
                    li = None
                    for k in range(kk):
                        cc = g * 4 + k
                        li = e.transpose(ps_tr[:, k * 128:(k + 1) * 128],
                                         src[:, cc * 128:(cc + 1) * 128], ident[:])
                    li.then_inc(TE.h, 1)
                block.tensor(_t)
                VE.n += 1

                def _v(e, g=g, kk=kk, t0=t0, dst_fn=dst_fn, te_guard=te_guard):
                    e.wait_ge(TE.h, t0 + 1)
                    if te_guard is not None:
                        e.wait_ge(TE.h, te_guard)
                    for k in range(kk):
                        e.tensor_copy(dst_fn(g * 4 + k),
                                      ps_tr[:, k * 128:(k + 1) * 128])
                    e.drain()
                    e.nop().then_inc(VE.h, 1)
                block.vector(_v)

        def wprep_T(src, wdst_fn):
            for r in range(8):
                ve_t = VE.n

                GL.n += 32

                def _g(e, r=r, ve_t=ve_t, src=src):
                    e.wait_ge(VE.h, ve_t)
                    e.dma_start(u2[:, 0:512],
                                src[r * 128:(r + 1) * 128, 0:512])\
                        .then_inc(GL.h, 16)
                    e.dma_start(u2[:, 512:1024],
                                src[r * 128:(r + 1) * 128, 512:1024])\
                        .then_inc(GL.h, 16)
                block.gpsimd(_g)
                gl_t = GL.n
                transpose_strip_w(gl_t, wdst_fn, r)

        def transpose_strip_w(gl_t, wdst_fn, r):
            for g in range(2):
                ve_t = VE.n
                t0 = TE.n

                TE.n += 1

                def _t(e, g=g, gl_t=gl_t, ve_t=ve_t):
                    e.wait_ge(GL.h, gl_t)
                    e.wait_ge(VE.h, ve_t)
                    li = None
                    for k in range(4):
                        cc = g * 4 + k
                        li = e.transpose(ps_tr[:, k * 128:(k + 1) * 128],
                                         u2[:, cc * 128:(cc + 1) * 128], ident[:])
                    li.then_inc(TE.h, 1)
                block.tensor(_t)
                VE.n += 1

                def _v(e, g=g, t0=t0, r=r, wdst_fn=wdst_fn):
                    e.wait_ge(TE.h, t0 + 1)
                    for k in range(4):
                        cc = g * 4 + k
                        e.tensor_copy(wdst_fn(cc)[:, r * 128:(r + 1) * 128],
                                      ps_tr[:, k * 128:(k + 1) * 128])
                    e.drain()
                    e.nop().then_inc(VE.h, 1)
                block.vector(_v)

        def wload_nat(src, wdst, nr=NCH):
            GL.n += 16

            def _g(e, src=src, wdst=wdst, nr=nr):
                e.dma_start(wdst[:, 0:nr, :],
                            src.rearrange("(r p) c -> p r c", p=128))\
                    .then_inc(GL.h, 16)
            block.gpsimd(_g)

        def _quant_gps(src, dst, w, nb, u_a, u_t, u_c, u_s, m_, scl, rb_, sb2):
            t1 = u_c[:, 128:128 + w // 2]
            t2 = u_c[:, 128 + w // 2:128 + w // 2 + w // 4]
            t3 = u_c[:, 128 + w // 2 - 64:128 + w // 2 - 64 + w // 8]

            def _g1(e):
                e.tensor_scalar(u_a.bitcast(I32), src.bitcast(I32), 0x7FFFFFFF,
                                None, ALU.bitwise_and)
                e.drain()
                a2 = u_a.rearrange("p (b k) -> p b k", k=2)
                e.tensor_tensor(t1, a2[:, :, 0], a2[:, :, 1], ALU.max)
                e.drain()
                b2 = t1.rearrange("p (b k) -> p b k", k=2)
                e.tensor_tensor(t2, b2[:, :, 0], b2[:, :, 1], ALU.max)
                e.drain()
                c2 = t2.rearrange("p (b k) -> p b k", k=2)
                e.tensor_tensor(t3, c2[:, :, 0], c2[:, :, 1], ALU.max)
                e.drain()
                d2 = t3.rearrange("p (b k) -> p b k", k=2)
                e.tensor_tensor(m_, d2[:, :, 0], d2[:, :, 1], ALU.max)
                e.drain()
                e.tensor_scalar(scl, m_, 1.0 / 6.0, 1e-12, ALU.mult, ALU.max)
                e.drain()
                e.tensor_scalar(sb2, scl, 0.5, None, ALU.mult)
                e.drain()
                e.nop().then_inc(GE.h, 1)
            GE.n += 1
            block.gpsimd(_g1)
            ge1 = GE.n
            VE.n += 1

            def _vr(e, ge1=ge1):
                e.wait_ge(GE.h, ge1)
                e.reciprocal(rb_, scl)
                e.drain()
                e.nop().then_inc(VE.h, 1)
            block.vector(_vr)
            ve1 = VE.n
            GE.n += 1

            def _g2(e, ve1=ve1):
                e.wait_ge(VE.h, ve1)
                rbb = rb_.unsqueeze(2).broadcast_to([pp, nb, 16])
                e.tensor_tensor(u_t.rearrange("p (b k) -> p b k", k=16),
                                u_a.rearrange("p (b k) -> p b k", k=16), rbb,
                                ALU.mult)
                e.drain()
                e.tensor_scalar(u_c, u_t, 5.0, 2.0, ALU.is_gt, ALU.mult)
                e.tensor_scalar(u_a, u_t, 0.25, None, ALU.is_gt)
                e.drain()
                e.scalar_tensor_tensor(u_c, u_t, 3.5, u_c, ALU.is_gt, ALU.add)
                e.scalar_tensor_tensor(u_a, u_t, 0.75, u_a, ALU.is_gt, ALU.add)
                e.drain()
                e.scalar_tensor_tensor(u_c, u_t, 2.5, u_c, ALU.is_gt, ALU.add)
                e.scalar_tensor_tensor(u_a, u_t, 1.25, u_a, ALU.is_gt, ALU.add)
                e.drain()
                e.scalar_tensor_tensor(u_a, u_t, 1.75, u_a, ALU.is_gt, ALU.add)
                e.drain()
                e.scalar_tensor_tensor(u_a, u_c, 2.0, u_a, ALU.mult, ALU.add)
                e.drain()
                sbb = sb2.unsqueeze(2).broadcast_to([pp, nb, 16])
                e.tensor_tensor(u_c.rearrange("p (b k) -> p b k", k=16),
                                u_a.rearrange("p (b k) -> p b k", k=16), sbb,
                                ALU.mult)
                e.drain()
                e.tensor_scalar(u_s, src, 0.0, 2.0, ALU.is_ge, ALU.mult)
                e.drain()
                e.scalar_tensor_tensor(dst, u_s, -1.0, u_c, ALU.add, ALU.mult)
                e.drain()
                e.nop().then_inc(GE.h, 1)
            block.gpsimd(_g2)

        def quant(src, dst, w, base, eng="dve", scr=None, pp=128,
                  sg=None, ua=None, src_evt=None, sg_guard=0):
            nb = w // 16
            S_ = (scrB if scr is None else scr)[0:pp, :]
            u_a = S_[:, base:base + w] if ua is None else ua[0:pp, 0:w]
            u_t = S_[:, base + w:base + 2 * w]
            u_c = S_[:, base + 2 * w:base + 3 * w]
            u_s = u_t
            m_ = S_[:, base + 2 * w:base + 2 * w + nb]
            scl = S_[:, base + 2 * w + nb:base + 2 * w + 2 * nb]
            rb_ = S_[:, base + 2 * w + 2 * nb:base + 2 * w + 3 * nb]
            sb2 = S_[:, base + 3 * w:base + 3 * w + nb]
            act_t = 0
            if sg is not None:
                SE.n += 1

                def _sa(e, src=src, sg=sg, src_evt=src_evt,
                        sg_guard=sg_guard, u_a=u_a):
                    if src_evt is not None:
                        e.wait_ge(src_evt[0].h, src_evt[1])
                    if sg_guard:
                        e.wait_ge(VE.h, sg_guard)
                    e.activation(u_a, src, ACT.Abs)
                    e.activation(sg, src, ACT.Sign)
                    e.drain()
                    e.nop().then_inc(SE.h, 1)
                block.scalar(_sa)
                act_t = SE.n

            def _v(e, act_t=act_t):
                if act_t:
                    e.wait_ge(SE.h, act_t)
                sv = src.rearrange("p (b k) -> p b k", k=16)
                e.tensor_reduce(m_, sv, AXL.X, ALU.max, apply_absolute_value=True)
                e.drain()
                e.tensor_scalar(scl, m_, 1.0 / 6.0, 1e-12, ALU.mult, ALU.max)
                e.drain()
                e.reciprocal(rb_, scl)
                e.tensor_scalar(sb2, scl, 0.5, None, ALU.mult)
                e.drain()
                if sg is None:
                    e.tensor_reduce(u_a, src.rearrange("p (n o) -> p n o", o=1),
                                    AXL.X, ALU.max, apply_absolute_value=True)
                    e.drain()
                rbb = rb_.unsqueeze(2).broadcast_to([pp, nb, 16])
                e.tensor_tensor(u_t.rearrange("p (b k) -> p b k", k=16),
                                u_a.rearrange("p (b k) -> p b k", k=16), rbb,
                                ALU.mult)
                e.drain()
                e.tensor_scalar(u_c, u_t, 5.0, 2.0, ALU.is_gt, ALU.mult)
                e.tensor_scalar(u_a, u_t, 0.25, None, ALU.is_gt)
                e.drain()
                e.scalar_tensor_tensor(u_c, u_t, 3.5, u_c, ALU.is_gt, ALU.add)
                e.scalar_tensor_tensor(u_a, u_t, 0.75, u_a, ALU.is_gt, ALU.add)
                e.drain()
                e.scalar_tensor_tensor(u_c, u_t, 2.5, u_c, ALU.is_gt, ALU.add)
                e.scalar_tensor_tensor(u_a, u_t, 1.25, u_a, ALU.is_gt, ALU.add)
                e.drain()
                e.scalar_tensor_tensor(u_a, u_t, 1.75, u_a, ALU.is_gt, ALU.add)
                e.drain()
                e.scalar_tensor_tensor(u_a, u_c, 2.0, u_a, ALU.mult, ALU.add)
                e.drain()
                sbb = sb2.unsqueeze(2).broadcast_to([pp, nb, 16])
                e.tensor_tensor(u_c.rearrange("p (b k) -> p b k", k=16),
                                u_a.rearrange("p (b k) -> p b k", k=16), sbb,
                                ALU.mult)
                e.drain()
                if sg is None:
                    e.tensor_scalar(u_s, src, 0.0, 2.0, ALU.is_ge, ALU.mult)
                    e.drain()
                    e.scalar_tensor_tensor(dst, u_s, -1.0, u_c, ALU.add,
                                           ALU.mult)
                else:
                    e.tensor_tensor(dst, u_c, sg, ALU.mult)
                e.drain()
                e.nop().then_inc(VE.h, 1)
            VE.n += 1
            block.vector(_v)

        def emit_out():
            ve_t = VE.n
            se_t = SE.n
            OUT.n += 64
            out_t = OUT.n

            def _y(e, ve_t=ve_t, se_t=se_t, out_t=out_t):
                e.wait_ge(VE.h, ve_t)
                e.wait_ge(SE.h, se_t)
                xv = xbuf[:]
                dv = out_ap.rearrange("(t p) c -> p t c", p=128)
                for g in range(4):
                    e.dma_start(dv[:, g * 4:(g + 1) * 4],
                                xv[:, g * 4:(g + 1) * 4]).then_inc(OUT.h, 16)
                e.wait_ge(OUT.h, out_t)
            block.sync(_y)
            block.flush()

        wcc_tiles = []
        for (srcw, cin) in ((d["f1_wg_sl"], ccg_in), (d["f1_wu_sl"], ccu_in)):
            for (r0, rr) in ((0, 128), (128, 128), (256, 96)):
                wcc_tiles.append(("gu", srcw[r0:r0 + rr, :], cin, r0, rr, C))
        for r in range(8):
            wcc_tiles.append(("d", d["f1_wd_sl"][r * 128:(r + 1) * 128, :],
                              ccd_in, r * 128, 128, SLC))
        wcc_st = {"li": 0, "qi": 0, "store": {}, "q_ve": {}, "ld_all": 0}
        cxf = ctxq[:].bitcast(F32)
        cxq_dst = ctxq[:, 2048:3072]

        def wcc_feed_load():
            if stage < 4 or wcc_st["li"] >= len(wcc_tiles):
                return
            i = wcc_st["li"]
            wcc_st["li"] += 1
            (kind, srcp, cin, r0, rr, w) = wcc_tiles[i]
            ve_t = wcc_st["q_ve"].get(i - 1, 0)
            LD.n += 16

            def _y(e, srcp=srcp, rr=rr, w=w, ve_t=ve_t):
                e.wait_ge(VE.h, ve_t)
                e.dma_start(cxf[0:rr, 0:w], srcp).then_inc(LD.h, 16)
            block.sync(_y)
            wcc_st["load_t", i] = LD.n

        def wcc_feed_quant():
            if stage < 4 or wcc_st["qi"] >= wcc_st["li"]:
                return
            i = wcc_st["qi"]
            wcc_st["qi"] += 1
            (kind, srcp, cin, r0, rr, w) = wcc_tiles[i]
            ld_t = wcc_st["load_t", i]
            st_prev = wcc_st["store"].get(i - 1, 0)
            VE.n += 1

            def _v0(e, ld_t=ld_t, st_prev=st_prev):
                e.wait_ge(LD.h, max(ld_t, st_prev))
                e.nop().then_inc(VE.h, 1)
            block.vector(_v0)
            if kind == "gu":
                for so_ in (0, 512):
                    quant(cxf[0:rr, so_:so_ + 512],
                          cxq_dst[0:rr, so_:so_ + 512], 512, 0, pp=rr)
            else:
                quant(cxf[0:rr, 0:w], cxq_dst[0:rr, 0:w], w, 0, pp=rr)
            wcc_st["q_ve"][i] = VE.n
            ve_q = VE.n
            LD.n += 16

            def _y2(e, i=i, rr=rr, w=w, ve_q=ve_q):
                e.wait_ge(VE.h, ve_q)
                e.dma_start(qstage[i, 0:rr, 0:w], cxq_dst[0:rr, 0:w])\
                    .then_inc(LD.h, 16)
            block.sync(_y2)
            wcc_st["store"][i] = LD.n

        def wcc_finish():
            if stage < 4:
                return
            while wcc_st["li"] < len(wcc_tiles):
                wcc_feed_load()
                wcc_feed_quant()
            while wcc_st["qi"] < len(wcc_tiles):
                wcc_feed_quant()
            tr_dmas = {}
            tr_evs = {}

            def tr_store(blocks):
                ve_entry = VE.n
                for g in range((len(blocks) + 3) // 4):
                    grp = blocks[g * 4:(g + 1) * 4]
                    ve_q = max([ve_entry] +
                               [tr_evs.get(k, 0) for k in range(len(grp))])
                    t0 = TE.n
                    TE.n += 1

                    def _t(e, grp=grp, ve_q=ve_q):
                        e.wait_ge(VE.h, ve_q)
                        li = None
                        for k, (sap, dap, rp, cw) in enumerate(grp):
                            li = e.transpose(ps_tr[0:cw, k * 128:k * 128 + rp],
                                             sap, ident[0:rp, 0:rp])
                        li.then_inc(TE.h, 1)
                    block.tensor(_t)
                    VE.n += 1
                    for k in range(len(grp)):
                        tr_evs[k] = VE.n
                    dprev = max([0] + [tr_dmas.get(k, 0)
                                       for k in range(len(grp))])

                    def _v(e, grp=grp, t0=t0, dprev=dprev):
                        e.wait_ge(TE.h, t0 + 1)
                        e.wait_ge(LD.h, dprev)
                        for k, (sap, dap, rp, cw) in enumerate(grp):
                            e.tensor_copy(recb[0:cw, k * 128:k * 128 + rp],
                                          ps_tr[0:cw, k * 128:k * 128 + rp])
                        e.drain()
                        e.nop().then_inc(VE.h, 1)
                    block.vector(_v)
                    ve_e = VE.n
                    for k in range(len(grp)):
                        LD.n += 16
                        tr_dmas[k] = LD.n

                    def _y(e, grp=grp, ve_e=ve_e):
                        e.wait_ge(VE.h, ve_e)
                        for k, (sap, dap, rp, cw) in enumerate(grp):
                            e.dma_start(dap, recb[0:cw, k * 128:k * 128 + rp])\
                                .then_inc(LD.h, 16)
                    block.sync(_y)

            for i, (kind, srcp, cin, r0, rr, w) in enumerate(wcc_tiles):
                ve_t = VE.n
                st_t = wcc_st["store"][i]
                LD.n += 16
                ld_t = LD.n

                def _y(e, i=i, rr=rr, w=w, ve_t=ve_t, st_t=st_t):
                    e.wait_ge(VE.h, ve_t)
                    e.wait_ge(LD.h, st_t)
                    e.dma_start(u2[0:rr, 0:w], qstage[i, 0:rr, 0:w])\
                        .then_inc(LD.h, 16)
                block.sync(_y)
                VE.n += 1

                def _v0(e, ld_t=ld_t):
                    e.wait_ge(LD.h, ld_t)
                    e.nop().then_inc(VE.h, 1)
                block.vector(_v0)
                blocks = []
                if kind == "gu":
                    for cc in range(8):
                        blocks.append((u2[0:rr, cc * 128:(cc + 1) * 128],
                                       cin[cc * 128:(cc + 1) * 128, r0:r0 + rr],
                                       rr, 128))
                else:
                    for b, bw in ((0, 128), (1, 128), (2, 96)):
                        blocks.append((u2[:, b * 128:b * 128 + bw],
                                       cin[b * 128:b * 128 + bw,
                                           r0:r0 + 128],
                                       128, bw))
                tr_store(blocks)
            ld_all = LD.n
            wcc_st["ld_all"] = ld_all
            CC.n += 3

            def _g(e, ld_all=ld_all):
                e.wait_ge(LD.h, ld_all)
                rg = [[0, 1, 2, 3, 4, 5, 6, 7]]
                e.collective_compute("AllGather", ALU.bypass, replica_groups=rg,
                                     ins=[ccg_in], outs=[ccg]).then_inc(CC.h, 1)
                e.collective_compute("AllGather", ALU.bypass, replica_groups=rg,
                                     ins=[ccu_in], outs=[ccu]).then_inc(CC.h, 1)
                e.collective_compute("AllGather", ALU.bypass, replica_groups=rg,
                                     ins=[ccd_in], outs=[ccd]).then_inc(CC.h, 1)
            block.gpsimd(_g)

        def wcc_guard():
            if stage < 4 or not wcc_st["ld_all"]:
                return
            ld_all = wcc_st["ld_all"]

            def _w(e, ld_all=ld_all):
                e.wait_ge(LD.h, ld_all)
            block.vector(_w)
            block.scalar(_w)

        def attention(layer):
            l = f"l{layer}"
            causal = (layer == 0)
            trw = (layer == 0)
            lnTb = u1[:].rearrange("p (c q) -> p c q", q=QCW)

            if layer == 1:
                GL.n += 32
                gl_b = GL.n

                def _g(e, gl_b=gl_b):
                    e.dma_start(bq_c[:], d["l1_bq"].rearrange("(c p) -> p c", p=128), allow_slow_non_contiguous=True).then_inc(GL.h, 16)
                    e.dma_start(bk_c[:], d["l1_bk"].rearrange("(c p) -> p c", p=128), allow_slow_non_contiguous=True).then_inc(GL.h, 16)
                    e.wait_ge(GL.h, gl_b)
                block.gpsimd(_g)
                bar()

            va4 = Va[:].rearrange("p (t h j) -> p t h j", h=NH, j=C // NH + 1)
            prods = ("k", "v", "q")[:ATT_SUB]
            for which in prods:
                if which == "v":
                    load_brow(f"{l}_bv")
                    VE.n += 1

                    def _vones(e):
                        e.memset(va4[:, :, :, 64:65], 1.0)
                        e.drain()
                        e.nop().then_inc(VE.h, 1)
                    block.vector(_vones)
                if trw:
                    wprep_T(d[f"{l}_w{which}"], lambda cc: wcur[:, cc, :])
                else:
                    wload_nat(d[f"{l}_w{which}"], wcur)
                bar()
                for qc in range(NQC):
                    if layer == 0:
                        wcc_feed_load()
                    for tt in range(4):
                        t = qc * 4 + tt
                        ln_tile(t, u2[:, :])
                        transpose_tile(u2[:, :],
                                       lambda cc, tt=tt: lnTb[:, cc, tt * 128:(tt + 1) * 128])
                    bar()
                    if layer == 0:
                        wcc_feed_quant()
                    if which in ("k", "q"):
                        dst = kTa[:].rearrange("p (c q) -> p c q", q=T) if which == "k" \
                            else qTa[:, :, :]
                        bcol = bk_c if which == "k" else bq_c
                        evac_tgts = []
                        for m in range(8):
                            t0 = TE.n
                            se_need = evac_tgts[m - 2] if m >= 2 else SE.n

                            def _t(e, m=m, se_need=se_need):
                                e.wait_ge(SE.h, se_need)
                                li = None
                                for cc in range(8):
                                    li = e.matmul(ps_s[m % 2][:, :],
                                                  wcur[:, cc, m * 128:(m + 1) * 128],
                                                  lnTb[:, cc, :],
                                                  start=(cc == 0), stop=(cc == 7))
                                li.then_inc(TE.h, 1)
                            block.tensor(_t)
                            TE.n += 1

                            def _s(e, m=m, qc=qc, t0=t0, dst=dst, bcol=bcol):
                                e.wait_ge(TE.h, t0 + 1)
                                e.activation(dst[:, m, qc * QCW:(qc + 1) * QCW],
                                             ps_s[m % 2][:, :], ACT.Identity,
                                             bias=bcol[:, m:m + 1])
                                e.drain()
                                e.nop().then_inc(SE.h, 1)
                            block.scalar(_s)
                            SE.n += 1
                            evac_tgts.append(SE.n)
                    else:
                        evac_tgts = []
                        for tt in range(4):
                            t = qc * 4 + tt
                            t0 = TE.n
                            ve_need = evac_tgts[tt - 1] if tt >= 1 else VE.n

                            def _t(e, tt=tt, ve_need=ve_need):
                                e.wait_ge(VE.h, ve_need)
                                li = None
                                for half in range(2):
                                    for cc in range(8):
                                        e.matmul(ps_w[:, half * 512:(half + 1) * 512],
                                                 lnTb[:, cc, tt * 128:(tt + 1) * 128],
                                                 wcur[:, cc, half * 512:(half + 1) * 512],
                                                 start=(cc == 0), stop=False)
                                    li = e.matmul(ps_w[:, half * 512:(half + 1) * 512],
                                                  tri[0:1, :],
                                                  brow[:, half * 512:(half + 1) * 512],
                                                  start=False, stop=True)
                                li.then_inc(TE.h, 1)
                            block.tensor(_t)
                            TE.n += 1

                            def _v(e, t=t, t0=t0):
                                e.wait_ge(TE.h, t0 + 1)
                                e.tensor_copy(va4[:, t, :, 0:64],
                                              ps_w[:].rearrange(
                                                  "p (h j) -> p h j", j=64))
                                e.drain()
                                e.nop().then_inc(VE.h, 1)
                            block.vector(_v)
                            VE.n += 1
                            evac_tgts.append(VE.n)
                bar()

            if ATT_SUB < 4:
                return
            if layer == 0:
                wcc_finish()
            load_brow(f"{l}_bo")
            if trw:
                wprep_T(d[f"{l}_wo"], lambda cc: wcur[:, cc, :])
            else:
                wload_nat(d[f"{l}_wo"], wcur)
            if layer == 0:
                wcc_guard()
            bar()

            kT3 = kTa[:].rearrange("p (c q) -> p c q", q=T)
            cq = ctxq[:].rearrange("p (c q) -> p c q", q=QCW)
            rb_e = scrB[64:65, 768:1024].bitcast(BF16)
            rb_o = scrB[96:97, 768:1024].bitcast(BF16)
            alibi = u1[:, 3072:3584].bitcast(F32)
            pt_slot = [u2[:, 0:512], u2[:, 512:1024],
                       u1[:, 0:512], u1[:, 512:1024]]
            if layer == 1:
                LD.n += 16
                al_t = LD.n

                def _yal(e):
                    e.dma_start(alibi, d["c_alibi"]).then_inc(LD.h, 16)
                block.sync(_yal)

                def _sal(e, al_t=al_t):
                    e.wait_ge(LD.h, al_t)
                block.scalar(_sal)
            exp_done = {}
            mask_done = {}
            av_done = {}
            ctx_evac = {}
            den_dma = {}
            seq_params = {}
            norm_done = {}
            rec_dma = {}
            pav = [[ps_x[0], ps_x[1]], [ps_cd[:, 512:1024], ps_e]]
            slot_ve_guard = {}
            LAG = 3
            recb2 = [recb[:, :], scrB[:, 512:768].bitcast(BF16)] if RECB2 \
                else [recb[:, :], recb[:, :]]

            def emit_av(seqj):
                (h2, kt2, slot2, par2, first2, last2, pcg2, qc2) = seq_params[seqj]
                se_t = exp_done[seqj]
                ve_t = mask_done.get(seqj, None)
                ve_ctx = ctx_evac.get(pcg2 - 2, 0)
                pxp = pav[pcg2 % 2][par2]

                def _t2(e, h=h2, kt=kt2, slot=slot2, pxp=pxp,
                        se_t=se_t, ve_t=ve_t, ve_ctx=ve_ctx,
                        first=first2, last=last2):
                    e.wait_ge(SE.h, se_t)
                    if ve_t is not None:
                        e.wait_ge(VE.h, ve_t)
                    if first:
                        e.wait_ge(VE.h, ve_ctx)
                    li = e.matmul(pxp[0:65, :],
                                  va4[:, kt, h, :],
                                  pt_slot[slot],
                                  start=first, stop=last)
                    li.then_inc(TE.h, 1)
                block.tensor(_t2)
                TE.n += 1
                av_done[seqj] = TE.n
                if last2 and par2 == 1:
                    emit_pair_evac(pcg2, qc2)

            def emit_pair_evac(pcg2, qc2):
                av_t = TE.n
                pc2 = pcg2 % (NH // 2)
                h_e, h_o = 2 * pc2, 2 * pc2 + 1
                dn_prev = rec_dma.get(pcg2 - 1, 0)

                pxe, pxo = pav[pcg2 % 2]

                def _v(e, pc2=pc2, av_t=av_t, dn_prev=dn_prev,
                       pxe=pxe, pxo=pxo):
                    e.wait_ge(TE.h, av_t)
                    e.wait_ge(DN.h, dn_prev)
                    e.tensor_copy(cq[0:64, pc2, :], pxe[0:64, :])
                    e.tensor_copy(cq[64:128, pc2, :], pxo[0:64, :])
                    e.tensor_copy(scrB[64:65, 0:512], pxe[64:65, :])
                    e.tensor_copy(scrB[96:97, 0:512], pxo[64:65, :])
                    e.drain()
                    e.nop().then_inc(VE.h, 1)
                    e.reciprocal(scrB[64:65, 0:512], scrB[64:65, 0:512])
                    e.reciprocal(scrB[96:97, 0:512], scrB[96:97, 0:512])
                    e.drain()
                    e.tensor_copy(rb_e, scrB[64:65, 0:512])
                    e.tensor_copy(rb_o, scrB[96:97, 0:512])
                    e.drain()
                    e.nop().then_inc(VE.h, 1)
                block.vector(_v)
                VE.n += 2
                ctx_evac[pcg2] = VE.n - 1
                ve_t = VE.n
                DN.n += 32

                def _y1(e, ve_t=ve_t, h_e=h_e, h_o=h_o):
                    e.wait_ge(VE.h, ve_t)
                    e.dma_start(rec_dram[h_e:h_e + 1, :], rb_e).then_inc(DN.h, 16)
                    e.dma_start(rec_dram[h_o:h_o + 1, :], rb_o).then_inc(DN.h, 16)
                block.sync(_y1)
                rec_dma[pcg2] = DN.n
                st_t = DN.n
                nm_prev = norm_done.get(pcg2 - 2, 0)
                sl = pc2 % 2
                DN.n += 32

                def _y2(e, sl=sl, h_e=h_e, h_o=h_o, st_t=st_t,
                        nm_prev=nm_prev):
                    e.wait_ge(DN.h, st_t)
                    e.wait_ge(VE.h, nm_prev)
                    s0 = rec_dram[h_e, :].unsqueeze(0).broadcast_to([64, 512])
                    s1 = rec_dram[h_o, :].unsqueeze(0).broadcast_to([64, 512])
                    e.dma_start(recb2[sl][0:64, :], s0).then_inc(DN.h, 16)
                    e.dma_start(recb2[sl][64:128, :], s1).then_inc(DN.h, 16)
                block.sync(_y2)
                bc_t = DN.n
                VE.n += 1

                def _vn(e, pc2=pc2, sl=sl, bc_t=bc_t):
                    e.wait_ge(DN.h, bc_t)
                    e.tensor_tensor(cq[:, pc2, :], cq[:, pc2, :],
                                    recb2[sl][:, :], ALU.mult)
                    e.drain()
                    e.nop().then_inc(VE.h, 1)
                block.vector(_vn)
                norm_done[pcg2] = VE.n
                if pc2 == NH // 2 - 1:
                    emit_qc_tail(qc2, pcg2)

            def emit_qc_tail(qc2, pcg2):
                se_guard = SE.n
                pcg_base = pcg2 - (NH // 2 - 1)
                px_free = ctx_evac.get(pcg2 - 1, 0)
                for half in range(2 if ATT_SUB >= 5 else 0):
                    ve_free = VE.n if half > 0 else 0
                    t0 = TE.n
                    for jc in range(8):
                        nm = norm_done[pcg_base + jc]
                        TE.n += 1

                        def _t3(e, half=half, jc=jc, nm=nm,
                                se_guard=se_guard, ve_free=ve_free,
                                px_free=px_free):
                            e.wait_ge(VE.h, nm)
                            if jc == 0:
                                e.wait_ge(SE.h, se_guard)
                                if half == 0:
                                    e.wait_ge(VE.h, px_free)
                                else:
                                    e.wait_ge(VE.h, ve_free)
                            li = None
                            for ti in range(2):
                                t_loc = half * 2 + ti
                                for nh2 in range(2):
                                    pp = (ps_s if ti == 0 else ps_x)[nh2]
                                    li = e.matmul(
                                        pp[:, :],
                                        cq[:, jc, t_loc * 128:(t_loc + 1) * 128],
                                        wcur[:, jc, nh2 * 512:(nh2 + 1) * 512],
                                        start=(jc == 0), stop=False)
                            li.then_inc(TE.h, 1)
                        block.tensor(_t3)
                    TE.n += 1

                    def _t4(e, half=half):
                        li = None
                        for ti in range(2):
                            for nh2 in range(2):
                                pp = (ps_s if ti == 0 else ps_x)[nh2]
                                li = e.matmul(pp[:, :], tri[0:1, :],
                                              brow[:, nh2 * 512:(nh2 + 1) * 512],
                                              start=False, stop=True)
                        li.then_inc(TE.h, 1)
                    block.tensor(_t4)
                    op_t = TE.n
                    VE.n += 1

                    def _v3(e, half=half, qc=qc2, op_t=op_t):
                        e.wait_ge(TE.h, op_t)
                        for ti in range(2):
                            t = qc * 4 + half * 2 + ti
                            for nh2 in range(2):
                                pp = (ps_s if ti == 0 else ps_x)[nh2]
                                e.tensor_tensor(xbuf[:, t, nh2 * 512:(nh2 + 1) * 512],
                                                xbuf[:, t, nh2 * 512:(nh2 + 1) * 512],
                                                pp[:, :], ALU.add)
                        e.drain()
                        e.nop().then_inc(VE.h, 1)
                    block.vector(_v3)
                ctx_evac[pcg2] = VE.n
                ctx_evac[pcg2 - 1] = VE.n
                slot_ve_guard[0] = VE.n
                slot_ve_guard[2] = VE.n

            seqg = 0
            pcg = 0
            for qc in range(NQC):
                kts = list(range(4 * qc + 4)) if causal else list(range(NKT))
                nk = len(kts)
                for pc in range(NH // 2):
                    for ki, kt in enumerate(kts):
                        for par in range(2):
                            h = 2 * pc + par
                            p0_ = par * 64
                            chk = pc
                            slot = seqg % 4
                            ssl = seqg % NSS
                            seq_params[seqg] = (h, kt, slot, par,
                                                ki == 0, ki == nk - 1, pcg, qc)
                            w_exp = exp_done.get(seqg - NSS, 0)
                            w_ve = slot_ve_guard.pop(ssl, 0)

                            def _t(e, kt=kt, ssl=ssl, p0_=p0_, chk=chk,
                                   qc=qc, w_exp=w_exp, w_ve=w_ve):
                                e.wait_ge(SE.h, w_exp)
                                if w_ve:
                                    e.wait_ge(VE.h, w_ve)
                                li = e.matmul(ps_slot[ssl][:, :],
                                              kT3[p0_:p0_ + 64, chk, kt * 128:(kt + 1) * 128],
                                              qTa[p0_:p0_ + 64, chk, qc * QCW:(qc + 1) * QCW],
                                              start=True, stop=True)
                                li.then_inc(TE.h, 1)
                            block.tensor(_t)
                            TE.n += 1
                            sc_t = TE.n
                            w_av2 = av_done.get(seqg - 4, 0)

                            def _s(e, h=h, kt=kt, slot=slot, ssl=ssl,
                                   sc_t=sc_t, layer=layer, w_av2=w_av2):
                                e.wait_ge(TE.h, max(sc_t, w_av2))
                                if layer == 1:
                                    b = alibi[:, h * NKT + kt:h * NKT + kt + 1]
                                    e.activation(pt_slot[slot], ps_slot[ssl][:, :],
                                                 ACT.Exp, bias=b, scale=0.125)
                                else:
                                    e.activation(pt_slot[slot], ps_slot[ssl][:, :],
                                                 ACT.Exp, scale=0.125)
                                e.drain()
                                e.nop().then_inc(SE.h, 1)
                            block.scalar(_s)
                            SE.n += 1
                            exp_done[seqg] = SE.n

                            need_mask = causal and (4 * qc <= kt <= 4 * qc + 3)
                            if need_mask:
                                v = kt - 4 * qc
                                se_t = SE.n

                                def _v(e, slot=slot, v=v, se_t=se_t):
                                    e.wait_ge(SE.h, se_t)
                                    pt = pt_slot[slot]
                                    if v > 0:
                                        e.memset(pt[:, 0:128 * v], 0.0)
                                    e.tensor_tensor(pt[:, 128 * v:128 * (v + 1)],
                                                    pt[:, 128 * v:128 * (v + 1)],
                                                    tri[:], ALU.mult)
                                    e.drain()
                                    e.nop().then_inc(VE.h, 1)
                                block.vector(_v)
                                VE.n += 1
                                mask_done[seqg] = VE.n

                            if seqg - LAG >= 0:
                                emit_av(seqg - LAG)
                            seqg += 1
                    pcg += 1
            for seqj in range(max(0, seqg - LAG), seqg):
                emit_av(seqj)
            bar()


        def ffn0():
            kf = kTa[:]
            xr = kf[:, 0:4096].rearrange("p (c q) -> p c q", q=512)
            xk = kf[:, 4096:8192].rearrange("p (c q) -> p c q", q=512)
            kact = kf[:, 8192:12288].rearrange("p (c q) -> p c q", q=512)
            Vf = Va[:]
            wkf = Vf[:, 0:8192].rearrange("p (c k) -> p c k", k=C)
            wvf = Vf[:, 8192:16384].rearrange("p (c k) -> p c k", k=C)
            ln2T = qTa

            wprep_T(d["f0_wr"], lambda cc: wcur[:, cc, :])
            wprep_T(d["f0_wk"], lambda cc: wkf[:, cc, :])
            wprep_T(d["f0_wv"], lambda cc: wvf[:, cc, :])
            bar()
            for t in range(NT):
                ln_tile(t, u2[:, :])
                transpose_tile(u2[:, :],
                               lambda cc, t=t: ln2T[:, cc, t * 128:(t + 1) * 128])
            bar()

            dbuf = scrB[:, 0:512]
            rfull = scrB[:, 512:1536]
            for pc in range(NQC):
                s0 = pc * 512
                for cc in range(8):
                    def _v(e, cc=cc, s0=s0, pc=pc):
                        if pc == 0:
                            cur = ln2T[:, cc, 1:512]
                            prv = ln2T[:, cc, 0:511]
                            e.tensor_sub(dbuf[:, 1:512], cur, prv)
                            e.drain()
                            e.scalar_tensor_tensor(xr[:, cc, 1:512], dbuf[:, 1:512],
                                                   mur_c[:, cc:cc + 1], prv,
                                                   ALU.mult, ALU.add)
                            e.scalar_tensor_tensor(xk[:, cc, 1:512], dbuf[:, 1:512],
                                                   muk_c[:, cc:cc + 1], prv,
                                                   ALU.mult, ALU.add)
                            e.drain()
                            e.tensor_scalar(xr[:, cc, 0:1], ln2T[:, cc, 0:1],
                                            mur_c[:, cc:cc + 1], None, ALU.mult)
                            e.tensor_scalar(xk[:, cc, 0:1], ln2T[:, cc, 0:1],
                                            muk_c[:, cc:cc + 1], None, ALU.mult)
                            e.drain()
                        else:
                            cur = ln2T[:, cc, s0:s0 + 512]
                            prv = ln2T[:, cc, s0 - 1:s0 + 511]
                            e.tensor_sub(dbuf[:, :], cur, prv)
                            e.drain()
                            e.scalar_tensor_tensor(xr[:, cc, :], dbuf[:, :],
                                                   mur_c[:, cc:cc + 1], prv,
                                                   ALU.mult, ALU.add)
                            e.scalar_tensor_tensor(xk[:, cc, :], dbuf[:, :],
                                                   muk_c[:, cc:cc + 1], prv,
                                                   ALU.mult, ALU.add)
                            e.drain()
                        e.nop().then_inc(VE.h, 1)
                    block.vector(_v)
                    VE.n += 1
                xrk_done = VE.n

                se_evt = []
                for m in range(8):
                    t0 = TE.n
                    se_prev = se_evt[m - 2] if m >= 2 else SE.n

                    def _t(e, m=m, se_prev=se_prev, xrk=xrk_done):
                        e.wait_ge(VE.h, xrk)
                        e.wait_ge(SE.h, se_prev)
                        li = None
                        for cc in range(8):
                            li = e.matmul(ps_s[m % 2][:, :],
                                          wkf[:, cc, m * 128:(m + 1) * 128],
                                          xk[:, cc, :], start=(cc == 0), stop=(cc == 7))
                        li.then_inc(TE.h, 1)
                    block.tensor(_t)
                    TE.n += 1

                    def _s(e, m=m, t0=t0):
                        e.wait_ge(TE.h, t0 + 1)
                        e.activation(u2[:, (m % 2) * 512:(m % 2) * 512 + 512],
                                     ps_s[m % 2][:, :], ACT.Relu)
                        e.drain()
                        e.nop().then_inc(SE.h, 1)
                    block.scalar(_s)
                    SE.n += 1
                    se_evt.append(SE.n)
                    se_now = SE.n

                    def _v(e, m=m, se_now=se_now):
                        e.wait_ge(SE.h, se_now)
                        rk = u2[:, (m % 2) * 512:(m % 2) * 512 + 512]
                        e.tensor_tensor(kact[:, m, :], rk, rk, ALU.mult)
                        e.drain()
                        e.nop().then_inc(VE.h, 1)
                    block.vector(_v)
                    VE.n += 1
                kact_done = VE.n

                for tt in range(4):
                    t = pc * 4 + tt
                    t0 = TE.n
                    ve_now = VE.n

                    def _t(e, tt=tt, kd=kact_done, ve_now=ve_now):
                        e.wait_ge(VE.h, max(kd, ve_now))
                        li = None
                        for half in range(2):
                            for cc in range(8):
                                li = e.matmul(ps_w[:, half * 512:(half + 1) * 512],
                                              xr[:, cc, tt * 128:(tt + 1) * 128],
                                              wcur[:, cc, half * 512:(half + 1) * 512],
                                              start=(cc == 0), stop=(cc == 7))
                        li.then_inc(TE.h, 1)
                    block.tensor(_t)
                    TE.n += 1
                    r_t = TE.n

                    def _s(e, r_t=r_t):
                        e.wait_ge(TE.h, r_t)
                        e.activation(rfull[:, :], ps_w[:, :], ACT.Sigmoid)
                        e.drain()
                        e.nop().then_inc(SE.h, 1)
                    block.scalar(_s)
                    SE.n += 1
                    sg_t = SE.n

                    def _t2(e, tt=tt, sg_t=sg_t):
                        e.wait_ge(SE.h, sg_t)
                        li = None
                        for half in range(2):
                            for m in range(8):
                                li = e.matmul(ps_w[:, half * 512:(half + 1) * 512],
                                              kact[:, m, tt * 128:(tt + 1) * 128],
                                              wvf[:, m, half * 512:(half + 1) * 512],
                                              start=(m == 0), stop=(m == 7))
                        li.then_inc(TE.h, 1)
                    block.tensor(_t2)
                    TE.n += 1
                    ov_t = TE.n

                    def _v(e, t=t, ov_t=ov_t):
                        e.wait_ge(TE.h, ov_t)
                        e.tensor_tensor(rfull[:, :], rfull[:, :], ps_w[:, :],
                                        ALU.mult)
                        e.drain()
                        e.tensor_tensor(xbuf[:, t, :], xbuf[:, t, :], rfull[:, :],
                                        ALU.add)
                        e.drain()
                        e.nop().then_inc(VE.h, 1)
                    block.vector(_v)
                    VE.n += 1
            bar()

        def ffn1():
            xqT = qTa
            kf = kTa[:]
            wg_sl = [kf[:, s * 4096:(s + 1) * 4096]
                     .rearrange("p (c k) -> p c k", k=512) for s in range(2)]
            wu_sl = [kf[:, (2 + s) * 4096:(3 + s) * 4096]
                     .rearrange("p (c k) -> p c k", k=512) for s in range(2)]
            hqw = [Va[:, 0:512], Va[:, 512:1024]]
            wbf = wcur[:].rearrange("p r k -> p (r k)").bitcast(F32)
            hbuf = [wbf[:, 1024:1536], wbf[:, 1536:2048]]
            sgs = [wbf[:, 2048:2560], wbf[:, 2560:3072]]
            uas = [wbf[:, 3072:3584], wbf[:, 3584:4096]]
            u2s = [u2[:, 0:512], u2[:, 512:1024]]
            psg = [ps_ab[:, 0:512], ps_ab[:, 512:1024]]
            psu = [ps_cd[:, 0:512], ps_cd[:, 512:1024]]
            cc_t = CC.n

            wd_chunk = []
            for kk in range(NIT):
                if kk < 15:
                    wd_chunk.append(Va[:, (1 + kk) * 1024:(2 + kk) * 1024])
                elif kk < 19:
                    wd_chunk.append(ctxq[:, (kk - 15) * 1024:(kk - 14) * 1024])
                elif kk < 21:
                    wd_chunk.append(u1[:, (kk - 19) * 1024:(kk - 18) * 1024])
                else:
                    wd_chunk.append(qTa[:, 0, 0:1024])

            def wd_load(kk, guard_te):
                DN.n += 16

                def _y(e, kk=kk, guard_te=guard_te, cc_t=cc_t):
                    e.wait_ge(CC.h, cc_t)
                    if guard_te is not None:
                        e.wait_ge(TE.h, guard_te)
                    e.dma_start(wd_chunk[kk], ccd[kk * 128:(kk + 1) * 128, :])\
                        .then_inc(DN.h, 16)
                block.sync(_y)
            for kk in range(21):
                wd_load(kk, None)

            ICW = [512] * 5 + [256]
            NIC2 = 6
            wld_t = {}

            def wld(ic2, guard_te):
                sl = ic2 % 2
                w = ICW[ic2]
                i0 = ic2 * 512
                pieces = []
                i = i0
                while i < i0 + w:
                    b = i // SLC
                    take = min(i0 + w, (b + 1) * SLC) - i
                    pieces.append((b, i - b * SLC, i - i0, take))
                    i += take
                LD.n += 16 * 2 * len(pieces)

                def _y(e, sl=sl, pieces=pieces, guard_te=guard_te, cc_t=cc_t):
                    e.wait_ge(CC.h, cc_t)
                    if guard_te is not None:
                        e.wait_ge(TE.h, guard_te)
                    for (csrc, dsl) in ((ccg, wg_sl[sl]), (ccu, wu_sl[sl])):
                        for (b, so, do, take) in pieces:
                            e.dma_start(dsl[:, :, do:do + take],
                                        csrc[b * C:(b + 1) * C, so:so + take]
                                        .rearrange("(c p) k -> p c k", p=128))\
                                .then_inc(LD.h, 16)
                block.sync(_y)
                wld_t[ic2] = LD.n
            wld(0, None)
            wld(1, None)

            qa_prev = {}
            for t in range(NT):
                ln_tile(t, wbf[:, 0:1024])
                se_t = SE.n

                def _v0(e, se_t=se_t):
                    e.wait_ge(SE.h, se_t)
                block.vector(_v0)
                for sub in range(2):
                    quant(wbf[:, sub * 512:(sub + 1) * 512],
                          u2[:, sub * 512:(sub + 1) * 512], 512, 0,
                          sg=sgs[sub], ua=uas[sub],
                          sg_guard=qa_prev.get(sub, 0))
                    qa_prev[sub] = VE.n
                transpose_tile(u2[:, :],
                               lambda cc, t=t: xqT[:, cc, t * 128:(t + 1) * 128])
            bar()

            mm_d = {}
            silu_d = {}
            tt_d = {}
            q_d = {}
            trmm_d = {}
            trev_d = {}
            hq_dma = {}

            def emit_trhq(g):
                ic2, t = divmod(g, NT)
                w = ICW[ic2]
                nb = w // 128
                s = g % 2
                hq_prev = hq_dma.get(g - 2, 0)
                q_t = q_d[g]
                prev_ev = trev_d.get(g - 1, 0)
                t0 = TE.n
                TE.n += 1

                def _t(e, s=s, nb=nb, q_t=q_t, prev_ev=prev_ev):
                    e.wait_ge(VE.h, max(q_t, prev_ev))
                    li = None
                    for k in range(nb):
                        li = e.transpose(ps_tr[:, k * 128:(k + 1) * 128],
                                         u2s[s][:, k * 128:(k + 1) * 128],
                                         ident[:])
                    li.then_inc(TE.h, 1)
                block.tensor(_t)
                trmm_d[g] = TE.n
                VE.n += 1

                def _v(e, s=s, nb=nb, t0=t0, hq_prev=hq_prev):
                    e.wait_ge(TE.h, t0 + 1)
                    e.wait_ge(HQ.h, hq_prev)
                    for k in range(nb):
                        e.tensor_copy(hqw[s][:, k * 128:(k + 1) * 128],
                                      ps_tr[:, k * 128:(k + 1) * 128])
                    e.drain()
                    e.nop().then_inc(VE.h, 1)
                block.vector(_v)
                trev_d[g] = VE.n
                ve_t2 = VE.n
                HQ.n += 16

                def _y(e, ic2=ic2, t=t, s=s, nb=nb, ve_t2=ve_t2):
                    e.wait_ge(VE.h, ve_t2)
                    e.dma_start(hq_dram[t, ic2 * 4:ic2 * 4 + nb]
                                .rearrange("b p q -> p b q"),
                                hqw[s][:, 0:nb * 128]
                                .rearrange("p (b q) -> p b q", q=128))\
                        .then_inc(HQ.h, 16)
                block.sync(_y)
                hq_dma[g] = HQ.n

            G = NIC2 * NT
            for ic2 in range(NIC2):
                w = ICW[ic2]
                sl = ic2 % 2
                for t in range(NT):
                    g = ic2 * NT + t
                    p = t % 2
                    se_w = silu_d.get(g - 2, 0)
                    ve_w = tt_d.get(g - 2, 0)
                    ld_w = wld_t[ic2]
                    TE.n += 1

                    def _t(e, t=t, w=w, p=p, sl=sl, se_w=se_w, ve_w=ve_w,
                           ld_w=ld_w):
                        e.wait_ge(SE.h, se_w)
                        e.wait_ge(VE.h, ve_w)
                        e.wait_ge(LD.h, ld_w)
                        for cc in range(8):
                            e.matmul(psg[p][:, 0:w],
                                     xqT[:, cc, t * 128:(t + 1) * 128],
                                     wg_sl[sl][:, cc, 0:w],
                                     start=(cc == 0), stop=(cc == 7))
                        li = None
                        for cc in range(8):
                            li = e.matmul(psu[p][:, 0:w],
                                          xqT[:, cc, t * 128:(t + 1) * 128],
                                          wu_sl[sl][:, cc, 0:w],
                                          start=(cc == 0), stop=(cc == 7))
                        li.then_inc(TE.h, 1)
                    block.tensor(_t)
                    mm_d[g] = TE.n
                    q_w = q_d.get(g - 2, 0)
                    mt = mm_d[g]
                    SE.n += 1

                    def _s(e, w=w, p=p, mt=mt, q_w=q_w):
                        e.wait_ge(TE.h, mt)
                        e.wait_ge(VE.h, q_w)
                        e.activation(hbuf[p][:, 0:w], psg[p][:, 0:w], ACT.Silu)
                        e.drain()
                        e.nop().then_inc(SE.h, 1)
                    block.scalar(_s)
                    silu_d[g] = SE.n
                    st = silu_d[g]
                    trg = trmm_d.get(g - 2, 0)
                    VE.n += 1

                    def _v(e, w=w, p=p, st=st, trg=trg):
                        e.wait_ge(SE.h, st)
                        e.wait_ge(TE.h, trg)
                        e.tensor_tensor(hbuf[p][:, 0:w], hbuf[p][:, 0:w],
                                        psu[p][:, 0:w], ALU.mult)
                        e.drain()
                        e.nop().then_inc(VE.h, 1)
                    block.vector(_v)
                    tt_d[g] = VE.n
                    quant(hbuf[p][:, 0:w], u2s[p][:, 0:w], w, 0,
                          sg=sgs[p][:, 0:w], ua=uas[p],
                          src_evt=(VE, tt_d[g]),
                          sg_guard=q_d.get(g - 2, 0))
                    q_d[g] = VE.n
                    if g >= 2:
                        emit_trhq(g - 2)
                if ic2 + 2 < NIC2:
                    wld(ic2 + 2, TE.n)
            emit_trhq(G - 2)
            emit_trhq(G - 1)
            wd_load(21, TE.n)
            bar()

            hq_sl = [kf[:, 0:NIT * 128], kf[:, NIT * 128:2 * NIT * 128]]
            psC = [ps_ab, ps_cd]
            wd_all = DN.n
            use_done = {}
            ev_d = {}
            for t in range(NT):
                sl = t % 2
                prev_use = use_done.get(t - 2, 0)
                HQ.n += 64

                def _y(e, t=t, sl=sl, prev_use=prev_use):
                    e.wait_ge(TE.h, prev_use)
                    hv = hq_sl[sl].rearrange("p (b q) -> p b q", q=128)
                    dv = hq_dram[t].rearrange("b p q -> p b q")
                    for b0, b1 in ((0, 6), (6, 11), (11, 17), (17, 22)):
                        e.dma_start(hv[:, b0:b1], dv[:, b0:b1])\
                            .then_inc(HQ.h, 16)
                block.sync(_y)
                hq_t = HQ.n
                ve_t = ev_d.get(t - 2, 0)
                pw = psC[t % 2]

                def _t(e, sl=sl, hq_t=hq_t, ve_t=ve_t, wd_all=wd_all, pw=pw):
                    e.wait_ge(HQ.h, hq_t)
                    e.wait_ge(VE.h, ve_t)
                    e.wait_ge(DN.h, wd_all)
                    li = None
                    for half in range(2):
                        for kk in range(NIT):
                            li = e.matmul(pw[:, half * 512:(half + 1) * 512],
                                          hq_sl[sl][:, kk * 128:(kk + 1) * 128],
                                          wd_chunk[kk][:, half * 512:(half + 1) * 512],
                                          start=(kk == 0), stop=(kk == NIT - 1))
                    li.then_inc(TE.h, 1)
                block.tensor(_t)
                TE.n += 1
                use_done[t] = TE.n
                mm_t = TE.n

                def _v(e, t=t, mm_t=mm_t, pw=pw):
                    e.wait_ge(TE.h, mm_t)
                    e.tensor_tensor(xbuf[:, t, :], xbuf[:, t, :], pw[:, :],
                                    ALU.add)
                    e.drain()
                    e.nop().then_inc(VE.h, 1)
                block.vector(_v)
                VE.n += 1
                ev_d[t] = VE.n
            bar()

        p0()
        if stage >= 1:
            attention(0)
        if stage >= 2:
            ffn0()
        if stage >= 3:
            attention(1)
        if stage >= 4:
            ffn1()
        emit_out()

    return nc


def _consts():
    ident = np.eye(128, dtype=np.float32).astype(ml_dtypes.bfloat16)
    kp = np.arange(128)[:, None]
    qp = np.arange(128)[None, :]
    tri = (qp >= kp).astype(np.float32).astype(ml_dtypes.bfloat16)
    alibi = np.zeros((128, NH * NKT), dtype=np.float32)
    xs = (2.0 ** 8) ** (1.0 / NH)
    slopes = np.array([1.0 / xs ** (i + 1) for i in range(NH)], dtype=np.float64)
    p = np.arange(128)
    for h in range(NH):
        for kt in range(NKT):
            alibi[:, h * NKT + kt] = (slopes[h] * (kt * 128 + p - (T - 1))).astype(np.float32)
    return ident, tri, alibi


def kernel(**inputs):
    B = 8
    ident, tri, alibi = _consts()
    nc = _build(STAGE)
    base = {
        "l0_wq": inputs["L0_attn_Wq"], "l0_wk": inputs["L0_attn_Wk"],
        "l0_wv": inputs["L0_attn_Wv"], "l0_wo": inputs["L0_attn_Wo"],
        "l0_bq": inputs["L0_attn_bq"], "l0_bk": inputs["L0_attn_bk"],
        "l0_bv": inputs["L0_attn_bv"], "l0_bo": inputs["L0_attn_bo"],
        "f0_wr": inputs["L0_ffn_W_r"], "f0_wk": inputs["L0_ffn_W_k"],
        "f0_wv": inputs["L0_ffn_W_v"],
        "f0_mur": inputs["L0_ffn_mu_r"], "f0_muk": inputs["L0_ffn_mu_k"],
        "l1_wq": inputs["L1_attn_Wq"], "l1_wk": inputs["L1_attn_Wk"],
        "l1_wv": inputs["L1_attn_Wv"], "l1_wo": inputs["L1_attn_Wo"],
        "l1_bq": inputs["L1_attn_bq"], "l1_bk": inputs["L1_attn_bk"],
        "l1_bv": inputs["L1_attn_bv"], "l1_bo": inputs["L1_attn_bo"],
        "c_ident": ident, "c_tri": tri, "c_alibi": alibi,
    }
    base = {k: np.ascontiguousarray(np.asarray(v)) for k, v in base.items()}
    wg = np.asarray(inputs["L1_ffn_w_gate"])
    wu = np.asarray(inputs["L1_ffn_w_up"])
    wd = np.asarray(inputs["L1_ffn_w_down"])
    in_maps = []
    for b in range(B):
        m = dict(base)
        m["x"] = np.ascontiguousarray(np.asarray(inputs["L0_attn_x"][b]))
        m["f1_wg_sl"] = np.ascontiguousarray(wg[b * (IFF // 8):(b + 1) * (IFF // 8)])
        m["f1_wu_sl"] = np.ascontiguousarray(wu[b * (IFF // 8):(b + 1) * (IFF // 8)])
        m["f1_wd_sl"] = np.ascontiguousarray(wd[:, b * (IFF // 8):(b + 1) * (IFF // 8)])
        in_maps.append(m)
    res = run_bass_kernel_spmd(nc, in_maps, core_ids=list(range(B)))
    out = np.stack([np.asarray(res.results[b]["out"]) for b in range(B)], axis=0)
    return out.astype(np.float32)
